# version 07
# baseline (speedup 1.0000x reference)
import numpy as np
import ml_dtypes
import concourse.bass as bass
import concourse.mybir as mybir
from concourse.bass_utils import run_bass_kernel_spmd
from concourse.alu_op_type import AluOpType as OP
from contextlib import ExitStack

F32 = mybir.dt.float32
BF16 = mybir.dt.bfloat16
AF = mybir.ActivationFunctionType
bf16 = ml_dtypes.bfloat16

DM = 1024
SEQ = 2048
NT = 8
HEADS = 16
DH = 64
NPAIR = 8
DEPTH = 6
FI = 2730
FIP = 2816
KH = 11
DIN = 2048
NH = 32
MTB = 256
MCH = 64
NTI = 16
CHW = 512
NCHUNK = SEQ // CHW
VS = 66
NBAND = 4


def _host_consts():
    c = {}
    f64 = np.float64
    half = np.arange(0, DH, 2, dtype=f64)
    inv_freq = 1.0 / 10000.0 ** (half / DH)
    t = np.arange(SEQ, dtype=f64)
    freqs = np.concatenate([t[:, None] * inv_freq[None, :]] * 2, -1)
    base_scale = (half + 0.4 * DH) / (1.4 * DH)
    power = (t - SEQ // 2) / 128.0
    scale = np.concatenate([base_scale[None, :] ** power[:, None]] * 2, -1)
    cos, sin = np.cos(freqs), np.sin(freqs)
    tile2 = lambda a: np.ascontiguousarray(np.concatenate([a.T, a.T], 0)).astype(np.float32)
    c["cs_q"], c["ss_q"] = tile2(cos * scale), tile2(sin * scale)
    c["cs_k"], c["ss_k"] = tile2(cos / scale), tile2(sin / scale)
    R = np.zeros((DH, DH), f64)
    for i in range(32):
        R[i, i + 32] = -1.0
        R[i + 32, i] = 1.0
    rt2 = np.zeros((128, 128), f64)
    rt2[:64, :64] = R.T
    rt2[64:, 64:] = R.T
    c["rot"] = rt2.astype(np.float32)
    jj, ii = np.meshgrid(np.arange(128), np.arange(128), indexing="ij")
    c["UT"] = (jj >= ii).astype(np.float32)
    c["LTm"] = (jj <= ii).astype(np.float32)
    p = np.arange(128)[:, None] % 64
    q = np.arange(MTB)[None, :] % 64
    c["LT4"] = (q >= p + NBAND).astype(np.float32)
    bm = np.ones((128, MTB), np.float32)
    bm[:, ::MCH] = 0.0
    c["bmask"] = bm
    dd = np.zeros((128, 128), np.float64)
    dd[:64, :64] = 1.0
    dd[64:, 64:] = 1.0
    c["dup64blk"] = dd.astype(np.float32)
    c["ident"] = np.eye(128).astype(np.float32)
    am = np.zeros((128, 9 * 128), f64)
    jj2, ii2 = np.meshgrid(np.arange(128), np.arange(128), indexing="ij")
    am[:, 3 * 128:4 * 128] = (jj2 <= ii2)
    am[:, 4 * 128:5 * 128] = 1.0
    am[:, 5 * 128:6 * 128] = (jj2 >= ii2)
    c["amask"] = am.astype(np.float32)
    c["onescol"] = np.ones((128, 1), np.float32)
    c["onescol_b"] = np.ones((128, 1)).astype(np.float32)
    sel2 = np.zeros((128, 2), f64)
    sel2[:64, 0] = 1.0
    sel2[64:, 1] = 1.0
    c["sel2"] = sel2.astype(np.float32)
    selt22 = np.zeros((4, 128), f64)
    selt22[0, :64] = 1.0; selt22[1, 64:] = 1.0
    selt22[2, :64] = 1.0; selt22[3, 64:] = 1.0
    c["selt22"] = selt22.astype(np.float32)
    c["ones2T"] = np.ones((2, 128)).astype(np.float32)
    c["onesrow"] = np.ones((1, 128)).astype(np.float32)
    selA = np.zeros((1, 128)); selA[0, :64] = 1.0
    selB = np.zeros((1, 128)); selB[0, 64:] = 1.0
    c["selA"] = selA.astype(np.float32)
    c["selB"] = selB.astype(np.float32)
    selscan = np.zeros((32, NTI * 128), f64)
    for ti in range(NTI):
        for pp in range(128):
            selscan[2 * ti + pp // 64, ti * 128 + pp] = 1.0
    c["selscan"] = selscan.astype(np.float32)
    m = np.ones((32, MTB), np.float32)
    m[:, ::MCH] = 0.0
    c["scanmask"] = m
    c["c_neg30"] = np.full((128, 1), -30.0, np.float32)
    c["c_eps5"] = np.full((128, 1), 1e-5, np.float32)
    c["c_eps24"] = np.full((128, 1), 1e-24, np.float32)
    c["c_one"] = np.full((128, 1), 1.0, np.float32)
    return c


def _host_weights(inp):
    w = {}
    w["wqkv"] = np.ascontiguousarray(inp["L0_ffn_Wqkv"], np.float32)
    w["wo"] = np.ascontiguousarray(inp["L0_ffn_Wo"], np.float32)
    W1 = inp["L0_ffn_W1"]
    w1a = np.zeros((DEPTH, DM, FIP), np.float32); w1a[:, :, :FI] = W1[:, :, :FI]
    w1g = np.zeros((DEPTH, DM, FIP), np.float32); w1g[:, :, :FI] = W1[:, :, FI:]
    w["w1a"], w["w1g"] = w1a, w1g
    w2p = np.zeros((DEPTH, FIP, DM), np.float32); w2p[:, :FI, :] = inp["L0_ffn_W2"]
    w["w2"] = w2p
    qs, ks = inp["L0_ffn_q_scale"], inp["L0_ffn_k_scale"]
    w["qs_cols"] = np.ascontiguousarray(np.concatenate([qs, qs], 1).T, np.float32)
    w["ks_cols"] = np.ascontiguousarray(np.concatenate([ks, ks], 1).T, np.float32)
    g, bb = inp["L0_ffn_ln_g"], inp["L0_ffn_ln_b"]
    tcol = lambda a, k: np.ascontiguousarray(
        a.reshape(DEPTH, k, 128).transpose(2, 0, 1).reshape(128, DEPTH * k), np.float32)
    w["lng_cols"], w["lnb_cols"] = tcol(g, 8), tcol(bb, 8)
    b1 = inp["L0_ffn_b1"]
    b1a = np.zeros((DEPTH, FIP), np.float32); b1a[:, :FI] = b1[:, :FI]
    b1g = np.zeros((DEPTH, FIP), np.float32); b1g[:, :FI] = b1[:, FI:]
    w["b1a_cols"], w["b1g_cols"] = tcol(b1a, 22), tcol(b1g, 22)
    w["b2_cols"] = tcol(inp["L0_ffn_b2"], 8)
    Win = inp["L0_attn_W_in"]
    base = 4096 + 129 * np.arange(NH)
    brows = np.concatenate([base[n] + np.arange(64) for n in range(NH)])
    crows = np.concatenate([base[n] + 64 + np.arange(64) for n in range(NH)])
    perm = np.zeros(8224, np.int64)
    for ti in range(16):
        o = ti * 512
        perm[o:o + 128] = np.arange(ti * 128, ti * 128 + 128)
        perm[o + 128:o + 256] = brows[ti * 128:(ti + 1) * 128]
        perm[o + 256:o + 384] = crows[ti * 128:(ti + 1) * 128]
        perm[o + 384:o + 512] = np.arange(2048 + ti * 128, 2048 + (ti + 1) * 128)
    perm[8192:8224] = base + 128
    w["winT"] = np.ascontiguousarray(Win[perm].T, np.float32)
    w["woutT"] = np.ascontiguousarray(inp["L0_attn_W_out"].T, np.float32)
    cw = inp["L0_attn_conv_w"]
    w["convw_cols"] = np.ascontiguousarray(cw.reshape(16, 128, 4).transpose(1, 0, 2).reshape(128, 64), np.float32)
    w["convb_cols"] = np.ascontiguousarray(inp["L0_attn_conv_b"].reshape(16, 128).T, np.float32)
    w["alog_cols"] = np.ascontiguousarray(inp["L0_attn_A_log"].reshape(2048).reshape(16, 128).T, np.float32)
    w["d_cols"] = np.ascontiguousarray(np.repeat(inp["L0_attn_D"], 64).reshape(16, 128).T, np.float32)
    w["dtb_col"] = np.ascontiguousarray(inp["L0_attn_dt_bias"].reshape(32, 1), np.float32)
    return w


class Prog:
    def __init__(self, nc, ctx):
        self.nc = nc
        self.ops = {e: [] for e in "tvsgy"}
        self.semh = {}
        self.cnt = {}
        self.memo = {}
        names = ["pe", "ve", "se", "ge"]
        for q in ["dq0", "dq1", "dq2", "dq3"]:
            for i in range(3):
                names.append(f"{q}_{i}")
        for name in names:
            self.semh[name] = ctx.enter_context(nc.semaphore(name))
            self.cnt[name] = 0
        self.dqi = {q: 0 for q in ["dq0", "dq1", "dq2", "dq3"]}
        self.eng_sem = {"t": "pe", "v": "ve", "s": "se", "g": "ge"}

    def raw(self, eng, fn):
        if eng == "t":
            self.op(eng, fn)
            return
        self.ops[eng].append(fn)

    def op(self, eng, fn):
        sname = self.eng_sem[eng]
        if eng in "vsg":
            self.wait(eng, (sname, self.cnt[sname]))
        self.cnt[sname] += 1
        val = self.cnt[sname]
        sem = self.semh[sname]
        def run(e, fn=fn, sem=sem):
            fn(e).then_inc(sem, 1)
        self.ops[eng].append(run)
        return (sname, val)

    def tok(self, eng):
        s = self.eng_sem[eng]
        return (s, self.cnt[s])

    def wait(self, eng, token):
        if token is None:
            return
        sname, val = token
        key = (eng, sname)
        if self.memo.get(key, 0) >= val:
            return
        self.memo[key] = val
        sem = self.semh[sname]
        self.ops[eng].append(lambda e, sem=sem, val=val: e.wait_ge(sem, val))

    def waits(self, eng, tokens):
        for tk in tokens:
            self.wait(eng, tk)

    def dma(self, eng, q, dst, src):
        i = self.dqi[q]
        self.dqi[q] = (i + 1) % 3
        sname = f"{q}_{i}"
        self.wait(eng, (sname, self.cnt[sname]))
        self.cnt[sname] += 16
        val = self.cnt[sname]
        sem = self.semh[sname]
        def run(e, dst=dst, src=src, sem=sem):
            e.dma_start(dst, src).then_inc(sem, 16)
        self.ops[eng].append(run)
        return (sname, val)

    def run(self, block):
        secs = {"t": block.tensor, "v": block.vector, "s": block.scalar,
                "g": block.gpsimd, "y": block.sync}
        for e in "tvsgy":
            ops = self.ops[e]
            @secs[e]
            def _(eng, ops=ops):
                for f in ops:
                    f(eng)


def tmax(*toks):
    best = {}
    for t in toks:
        if t is None:
            continue
        s, v = t
        if best.get(s, -1) < v:
            best[s] = v
    return [(s, v) for s, v in best.items()]


class Build:
    def __init__(self, nc, ctx, stages, debug):
        self.nc = nc
        self.p = Prog(nc, ctx)
        self.ctx = ctx
        self.stages = stages
        self.debug = debug
        self.sb = {}
        self.ps = {}
        self.dram = {}
        self.tk = {}
        self.final_toks = []
        self._mk_dram()
        self._mk_sbuf()
        self._mk_psum()

    def _mk_dram(self):
        nc = self.nc
        d = self.dram
        def di(name, shape, dt=F32):
            d[name] = nc.dram_tensor(name, list(shape), dt, kind="ExternalInput").ap()
        di("x0", (DM, SEQ))
        di("wqkv", (DEPTH, DM, 3 * DM)); di("wo", (DEPTH, DM, DM))
        di("w1a", (DEPTH, DM, FIP)); di("w1g", (DEPTH, DM, FIP)); di("w2", (DEPTH, FIP, DM))
        di("winT", (DM, 8224)); di("woutT", (DIN, DM))
        di("qs_cols", (128, 6)); di("ks_cols", (128, 6))
        di("lng_cols", (128, 48)); di("lnb_cols", (128, 48))
        di("b1a_cols", (128, 132)); di("b1g_cols", (128, 132)); di("b2_cols", (128, 48))
        di("convw_cols", (128, 64)); di("convb_cols", (128, 16))
        di("alog_cols", (128, 16)); di("d_cols", (128, 16)); di("dtb_col", (32, 1))
        di("amask", (128, 9 * 128))
        for n, shp in [("cs_q", (128, SEQ)), ("ss_q", (128, SEQ)),
                       ("cs_k", (128, SEQ)), ("ss_k", (128, SEQ)),
                       ("rot", (128, 128)), ("UT", (128, 128)), ("LTm", (128, 128)),
                       ("LT4", (128, MTB)), ("bmask", (128, MTB)),
                       ("dup64blk", (128, 128)), ("ident", (128, 128)),
                       ("onescol_b", (128, 1)), ("sel2", (128, 2)),
                       ("selt22", (4, 128)), ("ones2T", (2, 128)),
                       ("onesrow", (1, 128)), ("selA", (1, 128)), ("selB", (1, 128)),
                       ("selscan", (32, NTI * 128))]:
            di(n, shp)
        for n, shp in [("onescol", (128, 1)), ("scanmask", (32, MTB)),
                       ("c_neg30", (128, 1)), ("c_eps5", (128, 1)), ("c_eps24", (128, 1)),
                       ("c_one", (128, 1))]:
            di(n, shp)
        kd = "ExternalOutput" if self.debug else "Internal"
        d["xres"] = nc.dram_tensor("xres", [DM, SEQ], F32, kind=kd).ap()
        d["xstack"] = nc.dram_tensor("xstack", [DM, SEQ], F32, kind=kd).ap()
        d["y"] = nc.dram_tensor("y", [DM, SEQ], F32, kind="ExternalOutput").ap()
        if self.debug:
            d["dbgrows"] = nc.dram_tensor("dbgrows", [4, 5 * CHW], F32, kind="ExternalOutput").ap()
            d["dbga"] = nc.dram_tensor("dbga", [128, 16384], F32, kind="ExternalOutput").ap()

    def _mk_sbuf(self):
        S = lambda n, s, dt: self.sb.__setitem__(
            n, self.ctx.enter_context(self.nc.sbuf_tensor("sb_" + n, list(s), dt)))
        for n in ["rot", "UT", "LTm", "ident"]:
            S(n, (128, 128), BF16)
        S("LT4", (128, MTB), BF16)
        S("bmask", (128, MTB), BF16); S("dup64blk", (128, 128), BF16)
        S("onescol_b", (128, 1), BF16); S("sel2", (128, 2), BF16)
        S("selt22", (4, 128), BF16); S("ones2T", (2, 128), BF16)
        S("onesrow", (1, 128), BF16); S("selA", (1, 128), BF16); S("selB", (1, 128), BF16)
        S("selscan", (32, NTI * 128), BF16); S("scanmask", (32, MTB), F32)
        for n in ["onescol", "c_neg30", "c_eps5", "c_eps24", "c_one"]:
            S(n, (128, 1), F32)
        S("qs_cols", (128, 6), F32); S("ks_cols", (128, 6), F32)
        S("lng_cols", (128, 48), F32); S("lnb_cols", (128, 48), F32)
        S("b1a_cols", (128, 132), F32); S("b1g_cols", (128, 132), F32)
        S("b2_cols", (128, 48), F32)
        S("convw_cols", (128, 64), F32); S("convb_cols", (128, 16), F32)
        S("alog_cols", (128, 16), F32); S("d_cols", (128, 16), F32)
        S("dtb_col", (32, 1), F32)
        S("A_cols", (128, 16), F32); S("negA_cols", (128, 16), F32)
        S("clamp_cols", (128, 16), F32)
        S("xs", (128, 2 * NT * (CHW + 2)), BF16)
        S("stage", (128, 2 * CHW), F32)
        S("outacc", (128, NT * CHW), F32)
        S("scr", (128, 4 * CHW), BF16)
        S("scrf", (128, 2 * CHW), F32)
        S("rows", (1, 5 * CHW), F32)
        S("rowsb", (1, 4 * CHW), BF16)
        S("attnw", (128, 4 * CHW), BF16)
        S("srecip", (1, 2 * CHW), BF16)
        S("amask", (128, 9 * 128), BF16)
        S("mh", (128, NTI * 64), F32)
        S("mhb", (128, NTI * 64), BF16)
        S("xtails", (128, NTI * 3), BF16)
        S("mdt", (32, MTB), F32); S("mdtb", (32, MTB), BF16)
        S("mDl", (32, MTB), F32); S("mDrev", (32, MTB), F32)
        S("mDlhl", (32, 2 * MTB), BF16)
        S("mDrevhl", (32, 2 * MTB), BF16)
        S("mDtot", (32, 4), F32)
        S("mDtotb", (32, 8), BF16)
        S("mPfull", (128, NTI * 4), F32)
        S("mscrf", (128, MTB), F32)
        S("arena", (128, 59648), BF16)

    def ar(self, off, n):
        return self.sb["arena"][:, off:off + n]

    AT_WQKV = 0
    AT_KR = 24576
    AT_Q = 32768
    AT_VR = 36864
    AT_AO = 45312
    AT_WO = 49408
    AT_TAB = 57600
    FF_W1A = 0
    FF_W1G = 11264
    FF_W2 = 22528
    FF_XN = 33792
    FF_H2 = 50176
    MM_XN = 0
    MM_WIN = 16384
    MM_WOUT = 24576
    MM_XHTM = 28672
    MM_YG = 36864
    MM_XIN = 40960
    MM_SCR = 45104
    MM_SCR2 = 46640
    MM_G = 47408
    MM_WDT = 47920
    MM_E1 = 48176
    MM_E2 = 48432
    MM_E3 = 48688
    MM_P0 = 48944

    def _mk_psum(self):
        for i in range(7):
            self.ps[i] = self.ctx.enter_context(
                self.nc.psum_tensor(f"bank{i}", [128, 512], F32))
        self.ps[7] = self.ctx.enter_context(
            self.nc.psum_tensor("bank7", [128, 1024], BF16))

    def xs_slice(self, parity, tile, shifted, width=CHW):
        base = parity * NT * (CHW + 2) + tile * (CHW + 2)
        off = 1 if shifted else 2
        return self.sb["xs"][:, base + off: base + off + width]

    def bank_guard(self, bank, *toks):
        old = self.tk.get(("bank", bank)) or []
        self.tk[("bank", bank)] = tmax(*(old + list(toks)))

    def bank_wait(self, bank):
        toks = self.tk.get(("bank", bank)) or []
        self.p.waits("t", toks)
        self.tk[("bank", bank)] = []

    def emit_consts(self):
        p = self.p
        toks = []
        for n in ["rot", "UT", "LTm", "LT4", "bmask", "dup64blk",
                  "ident", "onescol_b", "sel2", "selt22", "ones2T",
                  "onesrow", "selA", "selB", "selscan",
                  "scanmask", "onescol", "c_neg30", "c_eps5", "c_eps24", "c_one",
                  "qs_cols", "ks_cols", "lng_cols", "lnb_cols",
                  "b1a_cols", "b1g_cols", "b2_cols", "convw_cols",
                  "convb_cols", "alog_cols", "d_cols", "dtb_col", "amask"]:
            toks.append(p.dma("g", "dq0", self.sb[n][:], self.dram[n]))
        consts_tok = tmax(*toks)
        p.waits("s", consts_tok)
        p.waits("v", consts_tok)
        p.waits("t", consts_tok)
        p.waits("g", consts_tok)
        t1 = p.op("s", lambda e: e.activation(self.sb["negA_cols"][:], self.sb["alog_cols"][:], AF.Exp))
        p.wait("v", t1)
        p.op("v", lambda e: e.tensor_scalar(self.sb["A_cols"][:], self.sb["negA_cols"][:], -1.0, None, op0=OP.mult))
        p.op("v", lambda e: e.reciprocal(self.sb["clamp_cols"][:], self.sb["negA_cols"][:]))
        p.op("v", lambda e: e.tensor_scalar(self.sb["clamp_cols"][:], self.sb["clamp_cols"][:], 78.0, None, op0=OP.mult))
        for nm in ["arena", "scr", "scrf", "rows", "rowsb", "attnw", "srecip",
                   "stage", "outacc", "xs", "mscrf", "mdt", "mdtb",
                   "mDl", "mDrev", "mDlhl", "mDrevhl", "mDtot", "mDtotb", "mPfull", "mhb"]:
            p.op("v", lambda e, nm=nm: e.memset(self.sb[nm][:], 0.0))
        p.op("v", lambda e: e.memset(self.sb["mh"][:], 0.0))
        p.op("v", lambda e: e.memset(self.sb["mhb"][:], 0.0))
        p.op("v", lambda e: e.memset(self.sb["xtails"][:], 0.0))
        self.tk["init"] = p.tok("v")

    def load_xs_chunk(self, src_dram, c, parity, q="dq1"):
        if isinstance(src_dram, str):
            src_dram = self.dram[src_dram]
        p = self.p
        xs = self.sb["xs"]
        base = parity * NT * (CHW + 2)
        t0 = c * CHW
        prev = self.tk.get(("xsfree", parity))
        if prev is not None:
            p.waits("g", tmax(*prev) if isinstance(prev, list) else tmax(prev))
        p.waits("g", tmax(self.tk.get(("xstail", parity))))
        if c == 0:
            zt = p.op("v", lambda e: e.memset(
                xs[:, base:base + NT * (CHW + 2)].rearrange("p (a c) -> p a c", c=CHW + 2)[:, :, 0:2], 0.0))
            p.wait("g", zt)
            src = src_dram[:, 0:CHW].rearrange("(a p) t -> p a t", p=128)
            dst = xs[:, base:base + NT * (CHW + 2)].rearrange("p (a c) -> p a c", c=CHW + 2)[:, :, 2:CHW + 2]
            tmain = p.dma("g", q, dst, src)
            self.tk[("xsload", parity)] = tmain
            return tmain
        grid = xs[:, base:base + NT * (CHW + 2)].rearrange("p (a c) -> p a c", c=CHW + 2)
        src = src_dram[:, t0: t0 + CHW].rearrange("(a p) t -> p a t", p=128)
        tmain = p.dma("g", q, grid[:, :, 2:CHW + 2], src)
        pbase = (1 - parity) * NT * (CHW + 2)
        pgrid = xs[:, pbase:pbase + NT * (CHW + 2)].rearrange("p (a c) -> p a c", c=CHW + 2)
        p.waits("v", tmax(self.tk.get(("xsload", 1 - parity)), tmain))
        tcp = p.op("v", lambda e: e.tensor_copy(grid[:, :, 0:2], pgrid[:, :, CHW:CHW + 2]))
        self.tk[("xsload", parity)] = tmain
        self.tk[("xstail", 1 - parity)] = tcp
        return tcp

    def emit_ln(self, tiles, writer, in_toks=None):
        p = self.p
        sq = self.sb["scr"]
        rows = self.sb["rows"]
        rowsb = self.sb["rowsb"]
        one = self.sb["onescol_b"]
        bs, bb, bq = self.ps[4], self.ps[5], self.ps[6]
        if in_toks is None:
            in_toks = [None] * NT
        R = lambda i: rows[0:1, i * CHW:(i + 1) * CHW]
        Rb = lambda i: rowsb[0:1, i * CHW:(i + 1) * CHW]
        self.bank_wait(4)
        self.bank_wait(6)
        sq_toks = []
        for a in range(NT):
            dst = sq[:, (a % 4) * CHW:(a % 4 + 1) * CHW]
            prev = self.tk.get(("sqslot", a % 4))
            p.waits("s", tmax(prev, in_toks[a]))
            sqt = p.op("s", lambda e, dst=dst, srcap=tiles[a]: e.activation(dst, srcap, AF.Square))
            sq_toks.append(sqt)
            p.waits("t", tmax(in_toks[a]))
            p.raw("t", lambda e, a=a, srcap=tiles[a]: e.matmul(
                bs[0:1, 0:CHW], one[:], srcap, start=(a == 0), stop=(a == NT - 1)))
            p.waits("t", tmax(sqt))
            last = a == NT - 1
            fn = lambda e, dst=dst, a=a, last=last: e.matmul(
                bq[0:1, 0:CHW], one[:], dst, start=(a == 0), stop=last)
            if last:
                tsum = p.op("t", fn)
            else:
                p.raw("t", fn)
            self.tk[("sqslot", a % 4)] = ("pe", p.cnt["pe"])
        p.wait("v", tsum)
        p.waits("v", tmax(self.tk.get("lnrows_s"), self.tk.get("rows_dump")))
        p.op("v", lambda e: e.tensor_scalar(R(0), bs[0:1, 0:CHW], 1.0 / DM, None, op0=OP.mult))
        t2 = p.op("v", lambda e: e.tensor_scalar(R(1), bq[0:1, 0:CHW], 1.0 / DM, None, op0=OP.mult))
        p.wait("v", t2)
        p.op("v", lambda e: e.tensor_tensor(R(4), R(0), R(0), op=OP.mult))
        t3 = p.op("v", lambda e: e.tensor_tensor(R(1), R(4), R(1), op=OP.subtract))
        self.bank_guard(4, t3)
        self.bank_guard(6, t3)
        p.wait("s", t3)
        t4 = p.op("s", lambda e: e.activation(R(2), R(1), AF.Sqrt,
                                              bias=self.sb["c_eps5"][0:1, :], scale=-1.0))
        p.wait("v", t4)
        t5 = p.op("v", lambda e: e.reciprocal(R(2), R(2)))
        t6 = p.op("v", lambda e: e.scalar_tensor_tensor(R(3), R(0), -1.0, R(2),
                                                        op0=OP.mult, op1=OP.mult))
        p.wait("v", t5)
        p.waits("v", tmax(self.tk.get("lnrowsb_t")))
        h1 = p.op("v", lambda e: e.tensor_copy(Rb(0), R(2)))
        p.wait("v", h1)
        h2 = p.op("v", lambda e: e.tensor_tensor(Rb(1), R(2), Rb(0), op=OP.subtract))
        p.wait("v", t6)
        h3 = p.op("v", lambda e: e.tensor_copy(Rb(2), R(3)))
        p.wait("v", h3)
        h4 = p.op("v", lambda e: e.tensor_tensor(Rb(3), R(3), Rb(2), op=OP.subtract))
        orow = self.sb["onesrow"]
        self.bank_wait(5)
        p.waits("t", tmax(h2))
        p.raw("t", lambda e: e.matmul(bb[:, 0:CHW], orow[:], Rb(0), start=True, stop=False))
        p.raw("t", lambda e: e.matmul(bb[:, 0:CHW], orow[:], Rb(1), start=False, stop=True))
        self.bank_wait(4)
        p.waits("t", tmax(h4))
        p.raw("t", lambda e: e.matmul(bs[:, 0:CHW], orow[:], Rb(2), start=True, stop=False))
        tmm = p.op("t", lambda e: e.matmul(bs[:, 0:CHW], orow[:], Rb(3), start=False, stop=True))
        self.tk["lnrows_s"] = t4
        self.tk["lnrowsb_t"] = tmm
        lastc = None
        for a in range(NT):
            lastc = writer(a, tiles[a], bb[:, 0:CHW], bs[:, 0:CHW], tmm)
        self.bank_guard(4, lastc)
        self.bank_guard(5, lastc)
        return {"v": lastc, "s": sq_toks[-1], "t": tsum}

    def emit_outer_ln(self, src, dst):
        p = self.p
        stage = self.sb["stage"]
        for c in range(NCHUNK):
            par = c % 2
            lt = self.load_xs_chunk(src, c, par)
            tiles = [self.xs_slice(par, a, False) for a in range(NT)]

            def writer(a, tile, aB, bB, tmm, c=c, par=par):
                u = self.sb["scrf"][:, (a % 2) * CHW:(a % 2 + 1) * CHW]
                p.waits("v", tmax(tmm, self.tk.get(("scrfslot", a % 2)),
                                  self.tk.get(("stageslot", a % 2)), self.tk.get(("stageslot_s", a % 2))))
                p.op("v", lambda e, u=u, tile=tile, aB=aB: e.tensor_tensor(u, tile, aB, op=OP.mult))
                dstap = stage[:, (a % 2) * CHW:(a % 2 + 1) * CHW]
                tv = p.op("v", lambda e, dstap=dstap, u=u, bB=bB: e.tensor_tensor(dstap, u, bB, op=OP.add))
                self.tk[("scrfslot", a % 2)] = tv
                p.wait("y", tv)
                td = p.dma("y", "dq2",
                           self.dram[dst][a * 128:(a + 1) * 128, c * CHW:(c + 1) * CHW], dstap)
                self.final_toks.append(td)
                self.tk[("stageslot", a % 2)] = td
                return tv
            res = self.emit_ln(tiles, writer, in_toks=[lt] * NT)
            self.tk[("xsfree", par)] = [res["v"], res["s"], res["t"]]
            if self.debug:
                p.waits("y", tmax(res["v"], res["s"], res["t"]))
                tdump = p.dma("y", "dq3", self.dram["dbgrows"][c:c + 1, :], self.sb["rows"][0:1, :])
                self.final_toks.append(tdump)
                self.tk["rows_dump"] = tdump

    def emit_residual_combine(self, srcA, srcB, scaleB, dst, mamba_xn=False):
        p = self.p
        stage = self.sb["stage"]
        if mamba_xn:
            self.phase_barrier()
        for c in range(NCHUNK):
            par = c % 2
            bf_toks = []
            last_td = None
            for a in range(NT):
                ap_a = stage[:, (a % 2) * CHW:(a % 2 + 1) * CHW]
                ap_b = self.sb["outacc"][:, a * CHW:(a + 1) * CHW]
                p.waits("y", tmax(self.tk.get(("stageslot", a % 2)), self.tk.get(("stageslot_s", a % 2))))
                tA = p.dma("y", "dq2", ap_a,
                           self.dram[srcA][a * 128:(a + 1) * 128, c * CHW:(c + 1) * CHW])
                p.waits("y", tmax(self.tk.get(("oaslot", a))))
                tB = p.dma("y", "dq3", ap_b,
                           self.dram[srcB][a * 128:(a + 1) * 128, c * CHW:(c + 1) * CHW])
                p.waits("v", tmax(tA, tB))
                tv = p.op("v", lambda e, ap_a=ap_a, ap_b=ap_b: e.scalar_tensor_tensor(
                    ap_a, ap_b, scaleB, ap_a, op0=OP.mult, op1=OP.add))
                p.wait("y", tv)
                td = p.dma("y", "dq2",
                           self.dram[dst][a * 128:(a + 1) * 128, c * CHW:(c + 1) * CHW], ap_a)
                self.final_toks.append(td)
                last_td = td
                self.tk[("stageslot", a % 2)] = td
                self.tk[("oaslot", a)] = tv
                if mamba_xn:
                    xt = self.xs_slice(par, a, False)
                    p.waits("s", tmax(tv, *(self.tk.get(("xsfree", par)) or []),
                                      self.tk.get(("xstail", par))))
                    bft = p.op("s", lambda e, xt=xt, ap_a=ap_a: e.activation(xt, ap_a, AF.Copy))
                    bf_toks.append(bft)
                    self.tk[("stageslot_s", a % 2)] = bft
            if mamba_xn:
                tiles = [self.xs_slice(par, a, False) for a in range(NT)]

                def writer(a, tile, aB, bB, tmm, c=c):
                    u = self.sb["scrf"][:, (a % 2) * CHW:(a % 2 + 1) * CHW]
                    p.waits("v", tmax(tmm, self.tk.get(("scrfslot", a % 2))))
                    p.op("v", lambda e, u=u, tile=tile, aB=aB: e.tensor_tensor(u, tile, aB, op=OP.mult))
                    dstap = self.ar(Build.MM_XN + a * SEQ + c * CHW, CHW)
                    tv = p.op("v", lambda e, dstap=dstap, u=u, bB=bB: e.tensor_tensor(dstap, u, bB, op=OP.add))
                    self.tk[("scrfslot", a % 2)] = tv
                    return tv
                res = self.emit_ln(tiles, writer, in_toks=bf_toks)
                self.tk[("xsfree", par)] = [res["v"], res["s"], res["t"]]

    def emit_final_combine(self):
        self.emit_residual_combine("xres", "xstack", 0.5, "y", mamba_xn=False)

    def phase_barrier(self):
        p = self.p
        toks = [("pe", p.cnt["pe"]), ("ve", p.cnt["ve"]), ("se", p.cnt["se"]),
                ("ge", p.cnt["ge"])] + [(f"{q}_{i}", p.cnt[f"{q}_{i}"])
                                        for q in ["dq0", "dq1", "dq2", "dq3"] for i in range(3)]
        toks = [t for t in toks if t[1] > 0]
        for e in "tvsgy":
            p.waits(e, toks)

    def emit_attn_sub(self, l):
        p = self.p
        ar = self.ar
        B = Build
        sel2 = self.sb["sel2"]
        scr = self.sb["scr"]
        attnw = self.sb["attnw"]
        rnr = self.sb["rows"]
        rnrb = self.sb["rowsb"]
        self.phase_barrier()
        vones_ap = ar(B.AT_VR, 8 * 1056).rearrange("p (s h c) -> p s h c", h=16, c=66)[:, :, :, 64:65]
        tvones = p.op("v", lambda e: e.memset(vones_ap, 1.0))
        wq_src = self.dram["wqkv"][l].rearrange("(kt p) o -> p kt o", p=128)
        twq = p.dma("g", "dq1", ar(B.AT_WQKV, 24576).rearrange("p (kt o) -> p kt o", o=3072), wq_src)
        wo_toks = []
        for kt in range(NT):
            wo_toks.append(p.dma("g", "dq1", ar(B.AT_WO + kt * 1024, 1024),
                                 self.dram["wo"][l][kt * 128:(kt + 1) * 128, :]))
        qs_col = self.sb["qs_cols"][:, l:l + 1]
        ks_col = self.sb["ks_cols"][:, l:l + 1]
        bstep = [0]

        def prep_A(c, par, pair, is_q, xs_tok):
            bk = bstep[0] % 2
            bank = self.ps[bk]
            bstep[0] += 1
            ot = (0 if is_q else 8) + pair
            self.bank_wait(bk)
            p.waits("t", tmax(twq, xs_tok))
            if is_q:
                dst = ar(B.AT_Q + pair * CHW, CHW)
                p.waits("v", tmax(self.tk.get(("qslot", pair))))
            else:
                dst = ar(B.AT_KR + (c % 2) * 4096 + pair * CHW, CHW)
                p.waits("v", tmax(self.tk.get(("kslot", c % 2, pair))))
            for kt in range(NT):
                rhs = self.xs_slice(par, kt, kt >= 4)
                lhsT = ar(B.AT_WQKV + kt * 3072 + ot * 128, 128)
                fn = lambda e, lhsT=lhsT, rhs=rhs, kt=kt: e.matmul(
                    bank[:, 0:CHW], lhsT, rhs, start=(kt == 0), stop=(kt == NT - 1))
                if kt == NT - 1:
                    tmm = p.op("t", fn)
                else:
                    p.raw("t", fn)
            sA = scr[:, (bk * 2) * CHW:(bk * 2 + 1) * CHW]
            sB = scr[:, (bk * 2 + 1) * CHW:(bk * 2 + 2) * CHW]
            p.waits("s", tmax(tmm, self.tk.get(("scrA", bk)), self.tk.get(("scrB", bk))))
            t_raw = p.op("s", lambda e: e.activation(sA, bank[:, 0:CHW], AF.Copy))
            t_sq = p.op("s", lambda e: e.activation(sB, bank[:, 0:CHW], AF.Square))
            p.waits("t", tmax(t_raw, t_sq))
            p.raw("t", lambda e: e.matmul(bank[0:1, 0:CHW], sel2[:, 0:1], sB, start=True, stop=True))
            tnsq = p.op("t", lambda e: e.matmul(bank[32:33, 0:CHW], sel2[:, 1:2], sB,
                                                start=True, stop=True, tile_position=(0, 32)))
            return (c, bk, bank, sA, sB, dst, tnsq, pair, is_q)

        def prep_B(st, tab_tok):
            c, bk, bank, sA, sB, dst, tnsq, pair, is_q = st
            RA = rnr[0:1, 0:CHW]
            RB = rnr[0:1, CHW:2 * CHW]
            p.waits("s", tmax(tnsq, self.tk.get("rnfree")))
            s1 = p.op("s", lambda e: e.activation(RA, bank[0:1, 0:CHW], AF.Sqrt,
                                                  bias=self.sb["c_eps24"][0:1, :]))
            s2 = p.op("s", lambda e: e.activation(RB, bank[32:33, 0:CHW], AF.Sqrt,
                                                  bias=self.sb["c_eps24"][0:1, :]))
            p.waits("v", tmax(s1, s2, self.tk.get("rnbfree")))
            v1 = p.op("v", lambda e: e.reciprocal(RA, RA))
            v2 = p.op("v", lambda e: e.reciprocal(RB, RB))
            Hb = lambda i: rnrb[0:1, i * CHW:(i + 1) * CHW]
            p.wait("v", v1)
            h1 = p.op("v", lambda e: e.tensor_copy(Hb(0), RA))
            p.wait("v", h1)
            h2 = p.op("v", lambda e: e.tensor_tensor(Hb(1), RA, Hb(0), op=OP.subtract))
            p.wait("v", v2)
            h3 = p.op("v", lambda e: e.tensor_copy(Hb(2), RB))
            p.wait("v", h3)
            h4 = p.op("v", lambda e: e.tensor_tensor(Hb(3), RB, Hb(2), op=OP.subtract))
            self.tk["rnfree"] = h4
            p.waits("t", tmax(h2, h4))
            selA, selB = self.sb["selA"], self.sb["selB"]
            p.raw("t", lambda e: e.matmul(bank[:, 0:CHW], selA[:], Hb(0), start=True, stop=False))
            p.raw("t", lambda e: e.matmul(bank[:, 0:CHW], selA[:], Hb(1), start=False, stop=False))
            p.raw("t", lambda e: e.matmul(bank[:, 0:CHW], selB[:], Hb(2), start=False, stop=False))
            trn = p.op("t", lambda e: e.matmul(bank[:, 0:CHW], selB[:], Hb(3), start=False, stop=True))
            self.tk["rnbfree"] = trn
            scol = qs_col if is_q else ks_col
            p.waits("v", tmax(trn))
            vq1 = p.op("v", lambda e: e.scalar_tensor_tensor(sB, sA, scol, bank[:, 0:CHW],
                                                             op0=OP.mult, op1=OP.mult))
            p.waits("t", tmax(vq1))
            tq1r = p.op("t", lambda e: e.matmul(bank[:, 0:CHW], self.sb["rot"][:], sB,
                                                start=True, stop=True))
            tb = B.AT_TAB
            cs = ar(tb + (0 if is_q else 2) * CHW, CHW)
            ss = ar(tb + (1 if is_q else 3) * CHW, CHW)
            p.waits("v", tmax(tq1r, *tab_tok) if isinstance(tab_tok, list) else tmax(tq1r, tab_tok))
            vt2 = p.op("v", lambda e: e.tensor_tensor(sA, bank[:, 0:CHW], ss, op=OP.mult))
            vt1 = p.op("v", lambda e: e.tensor_tensor(sB, sB, cs, op=OP.mult))
            p.waits("v", tmax(vt1, vt2))
            vd = p.op("v", lambda e: e.tensor_tensor(dst, sB, sA, op=OP.add))
            self.tk[("scrA", bk)] = vd
            self.tk[("scrB", bk)] = vd
            self.bank_guard(bk, vt2)
            if is_q:
                self.tk[("qprep", pair)] = vd
            else:
                self.tk[("kprep", c % 2, pair)] = vd
            return vd

        def prep_tiles(c, par, xs_tok, tab_tok):
            plist = [(pair, True) for pair in range(NPAIR)] + \
                    [(pair, False) for pair in range(NPAIR)]
            pend = None
            for pair, is_q in plist:
                stA = prep_A(c, par, pair, is_q, xs_tok)
                if pend is not None:
                    prep_B(pend, tab_tok)
                pend = stA
            prep_B(pend, tab_tok)

        def v_tiles(c, par, xs_tok):
            for tt in range(4):
                for vc in range(2):
                    v_one(c, par, xs_tok, tt, vc)

        def v_one(c, par, xs_tok, tt, vc):
                    jt = 4 * c + tt
                    slot = jt % 8
                    bk = bstep[0] % 2
                    bank = self.ps[bk]
                    bstep[0] += 1
                    self.bank_wait(bk)
                    p.waits("t", tmax(twq, xs_tok))
                    for kt in range(NT):
                        lhsT = self.xs_slice(par, kt, kt >= 4)[:, tt * 128:(tt + 1) * 128]
                        rhs = ar(B.AT_WQKV + kt * 3072 + 2048 + vc * CHW, CHW)
                        fn = lambda e, bank=bank, lhsT=lhsT, rhs=rhs, kt=kt: e.matmul(
                            bank[:, 0:CHW], lhsT, rhs, start=(kt == 0), stop=(kt == NT - 1))
                        if kt == NT - 1:
                            tmm = p.op("t", fn)
                        else:
                            p.raw("t", fn)
                    dst = ar(B.AT_VR + slot * 1056 + vc * 8 * 66, 8 * 66).rearrange(
                        "p (h c) -> p h c", c=66)[:, :, 0:64]
                    p.waits("s", tmax(tmm, tvones, self.tk.get(("vslot", slot, vc))))
                    vev = p.op("s", lambda e, dst=dst, bank=bank: e.activation(
                        dst, bank[:, 0:CHW].rearrange("p (h c) -> p h c", c=64), AF.Copy))
                    self.bank_guard(bk, vev)
                    self.tk[("vdata", jt, vc)] = vev

        def attention(c):
            for pair in range(NPAIR):
                attn_pair(c, pair)

        def attn_pair(c, pair):
            Jlist = [J for J in range(4 * c - 2, 4 * c + 4) if J >= 0]
            nJ = len(Jlist)
            state = {}
            tavs = {}

            def emit_sim(pr, idx):
                J = Jlist[idx]
                b = 2 + pr
                simb = self.ps[b]
                rpar = (J // 4) % 2
                r = J - 4 * c + 2
                b0 = max(0, r - 2) * 128
                self.bank_wait(b)
                p.waits("t", tmax(self.tk.get(("qprep", pair)), self.tk.get(("kprep", rpar, pair))))
                lhsT = ar(B.AT_KR + rpar * 4096 + pair * CHW + (J % 4) * 128, 128)[pr * 64: pr * 64 + 64, :]
                rhs = ar(B.AT_Q + pair * CHW, CHW)[pr * 64: pr * 64 + 64, b0:CHW]
                tsim = p.op("t", lambda e, simb=simb, lhsT=lhsT, rhs=rhs, pr=pr, b0=b0: e.matmul(
                    simb[:, b0:CHW], lhsT, rhs, start=True, stop=True, tile_position=(pr * 64, 0)))
                self.tk[("kslot", rpar, pair)] = tsim
                self.tk[("qslot", pair)] = tsim
                slot = (idx % 2) * 2 + pr
                aw = attnw[:, slot * CHW:(slot + 1) * CHW]
                p.waits("s", tmax(tsim, self.tk.get(("awfree", slot))))
                texp = p.op("s", lambda e, aw=aw, simb=simb, b0=b0: e.activation(
                    aw[:, b0:CHW], simb[:, b0:CHW], AF.Exp, bias=self.sb["c_neg30"][:, 0:1], scale=8.0))
                self.bank_guard(b, texp)
                msk = self.sb["amask"][:, (5 - r) * 128 + b0:(5 - r) * 128 + CHW]
                p.waits("g", tmax(texp))
                tmsk = p.op("g", lambda e, aw=aw, msk=msk, b0=b0: e.tensor_tensor(
                    aw[:, b0:CHW], aw[:, b0:CHW], msk, op=OP.mult))
                state[(pr, idx)] = (aw, slot, tmsk, J, b0)

            def emit_av(pr, idx):
                aw, slot, tmsk, J, b0 = state[(pr, idx)]
                first, last = idx == 0, idx == nJ - 1
                if first:
                    self.bank_wait(4 + pr)
                p.waits("t", tmax(tmsk, *[self.tk.get(("vdata", J, vc)) for vc in range(2)]))
                h = 2 * pair + pr
                outb = self.ps[4 + pr]
                lhsT = ar(B.AT_VR + (J % 8) * 1056 + h * 66, 65)
                tav = p.op("t", lambda e, outb=outb, lhsT=lhsT, aw=aw, first=first, last=last, b0=b0: e.matmul(
                    outb[0:65, b0:CHW], lhsT, aw[:, b0:CHW], start=first, stop=last))
                self.tk[("awfree", slot)] = tav
                for vc in range(2):
                    self.tk[("vslot", J % 8, vc)] = tav
                tavs[pr] = tav

            for idx in range(nJ + 1):
                for pr in range(2):
                    if idx < nJ:
                        emit_sim(pr, idx)
                for pr in range(2):
                    if idx >= 1:
                        emit_av(pr, idx - 1)
            for pr in range(2):
                normalize(c, pair, pr, tavs[pr])

        def normalize(c, pair, pr, tav):
            outb = self.ps[4 + pr]
            srecip = self.sb["srecip"]
            rf = self.sb["rows"][0:1, (2 + pr) * CHW:(3 + pr) * CHW]
            p.waits("v", tmax(tav, self.tk.get("srfree")))
            vr = p.op("v", lambda e, rf=rf, outb=outb: e.reciprocal(rf, outb[64:65, 0:CHW]))
            p.wait("v", vr)
            vh = p.op("v", lambda e, rf=rf: e.tensor_copy(srecip[0:1, 0:CHW], rf))
            p.wait("v", vh)
            vlo = p.op("v", lambda e, rf=rf: e.tensor_tensor(srecip[0:1, CHW:2 * CHW], rf,
                                                             srecip[0:1, 0:CHW], op=OP.subtract))
            self.bank_wait(6)
            p.waits("t", tmax(vh, vlo))
            p.raw("t", lambda e: e.matmul(self.ps[6][0:64, 0:CHW], self.sb["onesrow"][0:1, 0:64],
                                          srecip[0:1, 0:CHW], start=True, stop=False))
            trb = p.op("t", lambda e: e.matmul(self.ps[6][0:64, 0:CHW], self.sb["onesrow"][0:1, 0:64],
                                               srecip[0:1, CHW:2 * CHW], start=False, stop=True))
            self.tk["srfree"] = trb
            recS = self.sb["scrf"][0:64, pr * CHW:(pr + 1) * CHW]
            p.waits("s", tmax(trb, self.tk.get(("recSfree", pr))))
            vcp = p.op("s", lambda e, recS=recS: e.activation(recS, self.ps[6][0:64, 0:CHW], AF.Copy))
            self.bank_guard(6, vcp)
            dst = ar(B.AT_AO + pair * CHW, CHW)[pr * 64: pr * 64 + 64, :]
            p.waits("v", tmax(vcp, self.tk.get("aoutfree")))
            vout = p.op("v", lambda e, dst=dst, outb=outb, recS=recS: e.tensor_tensor(
                dst, outb[0:64, 0:CHW], recS, op=OP.mult))
            self.bank_guard(4 + pr, vout)
            self.tk[("recSfree", pr)] = vout

        def wo_chunk(c):
            for oc in range(NT):
                wo_one(c, oc)

        def wo_one(c, oc):
                stage = self.sb["stage"]
                bk = bstep[0] % 2
                bank = self.ps[bk]
                bstep[0] += 1
                self.bank_wait(bk)
                p.waits("t", tmax(wo_toks[0], wo_toks[-1]))
                p.waits("t", tmax(self.tk.get(("aout_done", c))))
                for kt in range(NT):
                    lhsT = ar(B.AT_WO + kt * 1024 + oc * 128, 128)
                    rhs = ar(B.AT_AO + kt * CHW, CHW)
                    fn = lambda e, lhsT=lhsT, rhs=rhs, kt=kt: e.matmul(
                        bank[:, 0:CHW], lhsT, rhs, start=(kt == 0), stop=(kt == NT - 1))
                    if kt == NT - 1:
                        two = p.op("t", fn)
                    else:
                        p.raw("t", fn)
                self.tk["aoutfree"] = two
                old = self.sb["outacc"][:, oc * CHW:(oc + 1) * CHW]
                p.waits("y", tmax(self.tk.get(("oaslot", oc))))
                tld = p.dma("y", "dq2", old,
                            self.dram["xstack"][oc * 128:(oc + 1) * 128, c * CHW:(c + 1) * CHW])
                st = stage[:, (oc % 2) * CHW:(oc % 2 + 1) * CHW]
                p.waits("v", tmax(two, tld, self.tk.get(("stageslot", oc % 2))))
                tv = p.op("v", lambda e: e.tensor_tensor(st, bank[:, 0:CHW], old, op=OP.add))
                self.bank_guard(bk, tv)
                self.tk[("oaslot", oc)] = tv
                p.wait("y", tv)
                td = p.dma("y", "dq2",
                           self.dram["xstack"][oc * 128:(oc + 1) * 128, c * CHW:(c + 1) * CHW], st)
                self.tk[("stageslot", oc % 2)] = td
                self.tk[("xstack_w", c)] = td
                self.final_toks.append(td)

        pend_wo = None
        for c in range(NCHUNK):
            par = c % 2
            xs_tok = self.load_xs_chunk("xstack", c, par)
            tabs = []
            tb = B.AT_TAB
            p.waits("g", tmax(self.tk.get("tabfree")))
            for i, nm in enumerate(["cs_q", "ss_q", "cs_k", "ss_k"]):
                tabs.append(p.dma("g", "dq1", ar(tb + i * CHW, CHW),
                                  self.dram[nm][:, c * CHW:(c + 1) * CHW]))
            tab_tok = tabs
            prep_tiles(c, par, xs_tok, tab_tok)
            self.tk["tabfree"] = ("pe", p.cnt["pe"])
            self.tk["tabfree"] = ("ve", p.cnt["ve"])
            v_tiles(c, par, xs_tok)
            self.tk[("xsfree", par)] = [("pe", p.cnt["pe"]), ("se", p.cnt["se"])]
            if pend_wo is not None:
                wo_chunk(pend_wo)
            attention(c)
            self.tk[("aout_done", c)] = ("ve", p.cnt["ve"])
            pend_wo = c
        wo_chunk(pend_wo)

    def emit_ffn_sub(self, l, ln_only=False, upto="all"):
        p = self.p
        ar = self.ar
        B = Build
        scr = self.sb["scr"]
        stage = self.sb["stage"]
        if upto == "nothing":
            return
        if upto == "barrier2":
            self.p.waits("t", [("ve", self.p.cnt["ve"])])
            return
        if upto == "barrier3":
            self.p.waits("v", [("pe", self.p.cnt["pe"])])
            return
        if upto == "barrier4":
            self.p.waits("y", [("pe", self.p.cnt["pe"]), ("ve", self.p.cnt["ve"]), ("se", self.p.cnt["se"])])
            return
        if upto == "barrier5":
            for q in ["dq0", "dq1", "dq2", "dq3"]:
                self.p.waits("t", [(q, self.p.cnt[q])])
            return
        self.phase_barrier()
        if upto == "barrier":
            return
        xn_toks = {}
        for c in range(NCHUNK):
            par = c % 2
            lt = self.load_xs_chunk("xstack", c, par)
            if upto == "loads":
                xn_toks[c] = lt
                continue
            tiles = [self.xs_slice(par, a, a >= 4) for a in range(NT)]
            ctoks = []

            def writer(a, tile, aB, bB, tmm, c=c, ctoks=ctoks):
                u = self.sb["scrf"][:, (a % 2) * CHW:(a % 2 + 1) * CHW]
                p.waits("v", tmax(tmm, self.tk.get(("scrfslot", a % 2))))
                p.op("v", lambda e, u=u, tile=tile, aB=aB: e.tensor_tensor(u, tile, aB, op=OP.mult))
                p.op("v", lambda e, u=u, bB=bB: e.tensor_tensor(u, u, bB, op=OP.add))
                g_col = self.sb["lng_cols"][:, l * 8 + a: l * 8 + a + 1]
                b_col = self.sb["lnb_cols"][:, l * 8 + a: l * 8 + a + 1]
                dstap = ar(B.FF_XN + a * SEQ + c * CHW, CHW)
                tv = p.op("v", lambda e, dstap=dstap, u=u, g_col=g_col, b_col=b_col: e.tensor_scalar(
                    dstap, u, g_col, b_col, op0=OP.mult, op1=OP.add))
                self.tk[("scrfslot", a % 2)] = tv
                ctoks.append(tv)
                return tv
            res = self.emit_ln(tiles, writer, in_toks=[lt] * NT)
            self.tk[("xsfree", par)] = [res["v"], res["s"], res["t"]]
            xn_toks[c] = tmax(*ctoks)[0]
        if self.debug:
            p.waits("g", tmax(*[xn_toks[c] for c in range(NCHUNK)]))
            self.final_toks.append(p.dma("g", "dq1", self.dram["dbga"][:, 0:8192], ar(B.FF_XN, 8192)))
            self.final_toks.append(p.dma("g", "dq1", self.dram["dbga"][:, 8192:8704],
                                         self.xs_slice(0, 0, False)))
            self.final_toks.append(p.dma("g", "dq1", self.dram["dbga"][0:1, 8704:8704 + 2048],
                                         self.sb["rowsb"][0:1, 0:4 * CHW]))
            self.final_toks.append(p.dma("g", "dq1", self.dram["dbga"][0:1, 11000:11000 + 2048],
                                         self.sb["rows"][0:1, 0:4 * CHW]))
        if ln_only:
            return
        bstep = [0]
        for hf in range(2):
            if hf == 1:
                p.waits("g", tmax(("pe", p.cnt["pe"]), ("se", p.cnt["se"]), ("ve", p.cnt["ve"])))
            tw1a = p.dma("g", "dq1",
                         ar(B.FF_W1A, 11264).rearrange("p (ki o) -> p ki o", o=1408),
                         self.dram["w1a"][l].rearrange("(ki p) o -> p ki o", p=128)[:, :, hf * 1408:(hf + 1) * 1408])
            tw1g = p.dma("g", "dq1",
                         ar(B.FF_W1G, 11264).rearrange("p (ki o) -> p ki o", o=1408),
                         self.dram["w1g"][l].rearrange("(ki p) o -> p ki o", p=128)[:, :, hf * 1408:(hf + 1) * 1408])
            tw2 = p.dma("g", "dq1",
                        ar(B.FF_W2, 11264).rearrange("p (kk o) -> p kk o", o=1024),
                        self.dram["w2"][l][hf * 1408:(hf + 1) * 1408, :].rearrange("(kk p) o -> p kk o", p=128))
            for c in range(NCHUNK):
                h2_toks = []
                for k in range(KH):
                    h2_toks.append(self._ffn_h2(l, hf, c, k, tw1a, tw1g, xn_toks[c]))
                for oc in range(NT):
                    self._ffn_w2(l, hf, c, oc, tw2, h2_toks, bstep)

    def _ffn_h2(self, l, hf, c, k, tw1a, tw1g, xn_tok):
        p = self.p
        ar = self.ar
        B = Build
        scr = self.sb["scr"]
        ba, bg = self.ps[(k % 2) * 2], self.ps[(k % 2) * 2 + 1]
        self.bank_wait((k % 2) * 2)
        p.waits("t", tmax(tw1a, xn_tok))
        for kt in range(NT):
            lhsT = ar(B.FF_W1A + kt * 1408 + k * 128, 128)
            rhs = ar(B.FF_XN + kt * SEQ + c * CHW, CHW)
            fn = lambda e, lhsT=lhsT, rhs=rhs, kt=kt: e.matmul(
                ba[:, 0:CHW], lhsT, rhs, start=(kt == 0), stop=(kt == NT - 1))
            if kt == NT - 1:
                tma = p.op("t", fn)
            else:
                p.raw("t", fn)
        self.bank_wait((k % 2) * 2 + 1)
        p.waits("t", tmax(tw1g))
        for kt in range(NT):
            lhsT = ar(B.FF_W1G + kt * 1408 + k * 128, 128)
            rhs = ar(B.FF_XN + kt * SEQ + c * CHW, CHW)
            fn = lambda e, lhsT=lhsT, rhs=rhs, kt=kt: e.matmul(
                bg[:, 0:CHW], lhsT, rhs, start=(kt == 0), stop=(kt == NT - 1))
            if kt == NT - 1:
                tmg = p.op("t", fn)
            else:
                p.raw("t", fn)
        sl = scr[:, (k % 2) * CHW:(k % 2 + 1) * CHW]
        b1g_col = self.sb["b1g_cols"][:, l * 22 + hf * 11 + k: l * 22 + hf * 11 + k + 1]
        p.waits("s", tmax(tmg, self.tk.get(("ffnscr", k % 2)), self.tk.get(("sqslot", k % 2))))
        tgel = p.op("s", lambda e, sl=sl, b1g_col=b1g_col: e.activation(
            sl, bg[:, 0:CHW], AF.Gelu, bias=b1g_col))
        self.bank_guard((k % 2) * 2 + 1, tgel)
        b1a_col = self.sb["b1a_cols"][:, l * 22 + hf * 11 + k: l * 22 + hf * 11 + k + 1]
        dst = ar(B.FF_H2 + k * CHW, CHW)
        p.waits("v", tmax(tma, tgel, self.tk.get(("h2slot", k))))
        th2 = p.op("v", lambda e, dst=dst, sl=sl, b1a_col=b1a_col: e.scalar_tensor_tensor(
            dst, ba[:, 0:CHW], b1a_col, sl, op0=OP.add, op1=OP.mult))
        self.bank_guard((k % 2) * 2, th2)
        self.tk[("ffnscr", k % 2)] = th2
        self.tk[("sqslot", k % 2)] = th2
        return th2

    def _ffn_w2(self, l, hf, c, oc, tw2, h2_toks, bstep):
        p = self.p
        ar = self.ar
        B = Build
        stage = self.sb["stage"]
        bk = 2 + bstep[0] % 2
        bstep[0] += 1
        bank = self.ps[bk]
        self.bank_wait(bk)
        p.waits("t", tmax(tw2, *h2_toks))
        for k in range(KH):
            lhsT = ar(B.FF_W2 + k * 1024 + oc * 128, 128)
            rhs = ar(B.FF_H2 + k * CHW, CHW)
            fn = lambda e, lhsT=lhsT, rhs=rhs, k=k: e.matmul(
                bank[:, 0:CHW], lhsT, rhs, start=(k == 0), stop=(k == KH - 1))
            if k == KH - 1:
                tmm = p.op("t", fn)
            else:
                p.raw("t", fn)
        for k in range(KH):
            self.tk[("h2slot", k)] = tmm
        old = self.sb["outacc"][:, oc * CHW:(oc + 1) * CHW]
        p.waits("y", tmax(self.tk.get(("oaslot", oc)), self.tk.get(("xstack_w", c))))
        tld = p.dma("y", "dq2", old,
                    self.dram["xstack"][oc * 128:(oc + 1) * 128, c * CHW:(c + 1) * CHW])
        st = stage[:, (oc % 2) * CHW:(oc % 2 + 1) * CHW]
        p.waits("v", tmax(tmm, tld, self.tk.get(("stageslot", oc % 2))))
        if hf == 0:
            tv = p.op("v", lambda e, st=st, bank=bank, old=old: e.tensor_tensor(
                st, bank[:, 0:CHW], old, op=OP.add))
        else:
            b2_col = self.sb["b2_cols"][:, l * 8 + oc: l * 8 + oc + 1]
            tv = p.op("v", lambda e, st=st, bank=bank, old=old, b2_col=b2_col: e.scalar_tensor_tensor(
                st, bank[:, 0:CHW], b2_col, old, op0=OP.add, op1=OP.add))
        self.bank_guard(bk, tv)
        self.tk[("oaslot", oc)] = tv
        p.wait("y", tv)
        td = p.dma("y", "dq2",
                   self.dram["xstack"][oc * 128:(oc + 1) * 128, c * CHW:(c + 1) * CHW], st)
        self.tk[("stageslot", oc % 2)] = td
        self.tk[("xstack_w", c)] = td
        self.final_toks.append(td)

    def emit_mamba(self, upto="all"):
        p = self.p
        ar = self.ar
        B = Build
        self.phase_barrier()
        mscr = lambda i: ar(B.MM_SCR + i * MTB, MTB)
        mscrf = self.sb["mscrf"]
        psb = self.ps[7]
        ident = self.sb["ident"]
        twdt = p.dma("g", "dq1", ar(B.MM_WDT, 256).rearrange("p (kt o) -> p kt o", o=32),
                     self.dram["winT"][:, 8192:8224].rearrange("(kt p) o -> p kt o", p=128))
        ntb = 1 if upto.endswith("1") else 8
        upto = upto[:-1] if upto.endswith("1") else upto
        for tb in range(ntb):
            self._mamba_dt(tb, twdt)
            if upto == "dt":
                continue
            for ti in range(NTI):
                self._mamba_ti(tb, ti, upto)
            self.tk["mDfree"] = ("pe", self.p.cnt["pe"])
            if upto in ("nowout", "proj", "noscan"):
                continue
            self._mamba_wout(tb)

    def _mamba_dt(self, tb, twdt):
        p = self.p
        ar = self.ar
        B = Build
        bk = self.ps[2]
        mscrf = self.sb["mscrf"]
        self.bank_wait(2)
        p.waits("t", tmax(twdt, self.tk.get("mxn")))
        for kt in range(NT):
            lhsT = ar(B.MM_WDT + kt * 32, 32)
            rhs = ar(B.MM_XN + kt * SEQ + tb * MTB, MTB)
            fn = lambda e, lhsT=lhsT, rhs=rhs, kt=kt: e.matmul(
                bk[0:32, 0:MTB], lhsT, rhs, start=(kt == 0), stop=(kt == NT - 1))
            if kt == NT - 1:
                tdt = p.op("t", fn)
            else:
                p.raw("t", fn)
        U = mscrf[0:32, 0:MTB]
        AB = self.sb["mDrev"][:, :]
        EE = self.sb["mDl"][:, :]
        p.waits("v", tmax(tdt, self.tk.get("mscrf_w"), self.tk.get("mscrf_rs"), self.tk.get("mDfree")))
        t1 = p.op("v", lambda e: e.tensor_scalar(U, bk[0:32, 0:MTB], self.sb["dtb_col"][:], None, op0=OP.add))
        self.bank_guard(2, t1)
        p.waits("s", tmax(t1, self.tk.get("mDfree")))
        t2 = p.op("s", lambda e: e.activation(AB, U, AF.Abs))
        p.waits("s", tmax(t2))
        t3 = p.op("s", lambda e: e.activation(EE, AB, AF.Exp, scale=-1.0))
        p.waits("s", tmax(t3))
        t4 = p.op("s", lambda e: e.activation(AB, EE, AF.Ln, bias=self.sb["c_one"][0:32, :]))
        p.waits("v", tmax(t4))
        t5 = p.op("v", lambda e: e.tensor_scalar(EE, U, 0.0, None, op0=OP.max))
        mdt = self.sb["mdt"]
        p.waits("v", tmax(t5))
        t6 = p.op("v", lambda e: e.tensor_tensor(mdt[:, :], EE, AB, op=OP.add))
        self.tk["mscrf_w"] = t1
        t7 = p.op("v", lambda e: e.tensor_copy(self.sb["mdtb"][:, :], mdt[:, :]))
        mDl = self.sb["mDl"]
        t8 = p.op("v", lambda e: e.tensor_tensor_scan(mDl[:, :], self.sb["scanmask"][:, :], mdt[:, :],
                                                      0.0, op0=OP.mult, op1=OP.add))
        p.waits("v", tmax(t8))
        mDtot = self.sb["mDtot"]
        for cc in range(4):
            p.op("v", lambda e, cc=cc: e.tensor_copy(mDtot[:, cc:cc + 1], mDl[:, cc * MCH + 63:cc * MCH + 64]))
        mDrev = self.sb["mDrev"]
        for cc in range(4):
            p.op("v", lambda e, cc=cc: e.tensor_scalar(
                mDrev[:, cc * MCH:(cc + 1) * MCH], mDl[:, cc * MCH:(cc + 1) * MCH],
                mDtot[:, cc:cc + 1], None, op0=OP.subtract))
        mDtotb = self.sb["mDtotb"]
        p.op("v", lambda e: e.tensor_copy(mDtotb[:, 0:4], mDtot[:, :]))
        p.op("v", lambda e: e.tensor_tensor(mDtotb[:, 4:8], mDtot[:, :], mDtotb[:, 0:4], op=OP.subtract))
        hl = self.sb["mDlhl"]
        p.op("v", lambda e: e.tensor_copy(hl[:, 0:MTB], mDl[:, :]))
        p.op("v", lambda e: e.tensor_tensor(hl[:, MTB:2 * MTB], mDl[:, :], hl[:, 0:MTB], op=OP.subtract))
        hr = self.sb["mDrevhl"]
        p.op("v", lambda e: e.tensor_copy(hr[:, 0:MTB], mDrev[:, :]))
        tlast = p.op("v", lambda e: e.tensor_tensor(hr[:, MTB:2 * MTB], mDrev[:, :], hr[:, 0:MTB], op=OP.subtract))
        self.tk["mdt_done"] = tlast

    def _mamba_ti(self, tb, ti, upto="all"):
        p = self.p
        ar = self.ar
        B = Build
        sel = self.sb["selscan"][:, ti * 128:(ti + 1) * 128]
        psb = self.ps[7]
        frees = tmax(self.tk.get("mscr_free_v"), self.tk.get("mscr_free_t"))
        for eng in "tvs":
            p.waits(eng, frees)
        pp = ti % 2
        prev = self.tk.get(("mwin", pp))
        if prev is not None:
            p.waits("g", tmax(prev))
        twin = p.dma("g", "dq1",
                     ar(B.MM_WIN + pp * 4096, 4096).rearrange("p (kt o) -> p kt o", o=512),
                     self.dram["winT"][:, ti * 512:(ti + 1) * 512].rearrange("(kt p) o -> p kt o", p=128))
        def proj(colofs, bank):
            self.bank_wait(bank)
            bk = self.ps[bank]
            p.waits("t", tmax(twin, self.tk.get("mxn")))
            for kt in range(NT):
                lhsT = ar(B.MM_WIN + pp * 4096 + kt * 512 + colofs, 128)
                rhs = ar(B.MM_XN + kt * SEQ + tb * MTB, MTB)
                fn = lambda e, bk=bk, lhsT=lhsT, rhs=rhs, kt=kt: e.matmul(
                    bk[:, 0:MTB], lhsT, rhs, start=(kt == 0), stop=(kt == NT - 1))
                if kt == NT - 1:
                    tmm = p.op("t", fn)
                else:
                    p.raw("t", fn)
            return tmm
        t_xin = proj(0, 0)
        xin = ar(B.MM_XIN + ti * (MTB + 3), MTB + 3)
        p.waits("v", tmax(t_xin, self.tk.get(("mxin", ti))))
        v1 = p.op("v", lambda e: e.tensor_copy(xin[:, 0:3], self.sb["xtails"][:, ti * 3:(ti + 1) * 3]))
        p.waits("v", tmax(v1))
        v2 = p.op("v", lambda e: e.tensor_copy(xin[:, 3:MTB + 3], self.ps[0][:, 0:MTB]))
        self.bank_guard(0, v2)
        v3 = p.op("v", lambda e: e.tensor_copy(self.sb["xtails"][:, ti * 3:(ti + 1) * 3], xin[:, MTB:MTB + 3]))
        t_B = proj(128, 1)
        p.waits("v", tmax(t_B))
        vB = p.op("v", lambda e: e.tensor_copy(ar(B.MM_SCR + 0 * MTB, MTB), self.ps[1][:, 0:MTB]))
        self.bank_guard(1, vB)
        t_C = proj(256, 0)
        p.waits("v", tmax(t_C))
        vC = p.op("v", lambda e: e.tensor_copy(ar(B.MM_SCR + 1 * MTB, MTB), self.ps[0][:, 0:MTB]))
        self.bank_guard(0, vC)
        mscrf = self.sb["mscrf"]
        acc = mscrf[:, 0:MTB]
        w0 = self.sb["convw_cols"][:, ti * 4:ti * 4 + 1]
        cb = self.sb["convb_cols"][:, ti:ti + 1]
        p.waits("v", tmax(v2, v3, self.tk.get("mscrf_w"), self.tk.get("mscrf_rs")))
        c1 = p.op("v", lambda e: e.tensor_scalar(acc, xin[:, 0:MTB], w0, cb, op0=OP.mult, op1=OP.add))
        prevc = c1
        for k in range(1, 4):
            wk = self.sb["convw_cols"][:, ti * 4 + k:ti * 4 + k + 1]
            p.waits("v", tmax(prevc))
            prevc = p.op("v", lambda e, wk=wk, k=k: e.scalar_tensor_tensor(
                acc, xin[:, k:k + MTB], wk, acc, op0=OP.mult, op1=OP.add))
        self.tk["mscrf_w"] = prevc
        p.waits("s", tmax(prevc))
        tsil = p.op("s", lambda e: e.activation(ar(B.MM_SCR + 2 * MTB, MTB), acc, AF.Silu))
        self.tk["mscrf_rs"] = tsil
        xh_cm = ar(B.MM_SCR + 2 * MTB, MTB)
        ev_toks = []
        for half in range(2):
            for c4 in range(4):
                self.bank_wait(7)
                p.waits("t", tmax(tsil))
                ttr = p.op("t", lambda e, half=half, c4=c4: e.transpose(
                    psb[half * 64:half * 64 + 64, c4 * 128:(c4 + 1) * 128],
                    xh_cm[:, c4 * 64:(c4 + 1) * 64], self.sb["ident"][:],
                    tile_position=(0, half * 64)))
            dstt = ar(B.MM_XHTM + ti * 512, 512)[half * 64:half * 64 + 64, :]
            p.waits("s", tmax(ttr, self.tk.get(("xhtm", ti, half))))
            tev = p.op("s", lambda e, dstt=dstt, half=half: e.activation(
                dstt, psb[half * 64:half * 64 + 64, 0:512], AF.Copy))
            self.bank_guard(7, tev)
            ev_toks.append(tev)
            self.tk[("xhtm_t", ti, half)] = tev
        if upto == "xht":
            self.tk[("yg", tb, ti)] = tsil
            self.tk["mscr_free_v"] = ev_toks[-1]
            self.tk["mscr_free_t"] = ("pe", p.cnt["pe"])
            self.tk[("mxin", ti)] = prevc
            self.tk[("xhtm", ti, 0)] = ("pe", p.cnt["pe"])
            self.tk[("xhtm", ti, 1)] = ("pe", p.cnt["pe"])
            return
        if upto == "proj":
            self.tk[("yg", tb, ti)] = tsil
            return
        acol = self.sb["A_cols"][:, ti:ti + 1]
        nacol = self.sb["negA_cols"][:, ti:ti + 1]
        hl, hr = self.sb["mDlhl"], self.sb["mDrevhl"]
        self.bank_wait(2)
        p.waits("t", tmax(self.tk.get("mdt_done")))
        p.raw("t", lambda e: e.matmul(self.ps[2][:, 0:MTB], sel, hl[:, 0:MTB], start=True, stop=False))
        tDl = p.op("t", lambda e: e.matmul(self.ps[2][:, 0:MTB], sel, hl[:, MTB:2 * MTB], start=False, stop=True))
        p.waits("s", tmax(tDl))
        for c4 in range(4):
            p.op("s", lambda e, c4=c4: e.activation(
                self.sb["mPfull"][:, ti * 4 + c4: ti * 4 + c4 + 1],
                self.ps[2][:, c4 * MCH + 63:c4 * MCH + 64], AF.Exp, scale=acol))
        tE1 = p.op("s", lambda e: e.activation(ar(B.MM_SCR + 3 * MTB, MTB),
                                               self.ps[2][:, 0:MTB], AF.Exp, scale=acol))
        p.waits("v", tmax(tDl, self.tk.get("mscrf_w"), self.tk.get("mscrf_rs")))
        tm2 = p.op("v", lambda e: e.tensor_scalar(mscrf[:, 0:MTB], self.ps[2][:, 0:MTB],
                                                  self.sb["clamp_cols"][:, ti:ti + 1], None, op0=OP.min))
        self.bank_guard(2, tm2, tE1)
        self.tk["mscrf_w"] = tm2
        p.waits("s", tmax(tm2))
        tE2 = p.op("s", lambda e: e.activation(ar(B.MM_SCR + 4 * MTB, MTB), mscrf[:, 0:MTB],
                                               AF.Exp, scale=nacol))
        self.tk["mscrf_rs"] = tE2
        self.bank_wait(3)
        p.raw("t", lambda e: e.matmul(self.ps[3][:, 0:MTB], sel, hr[:, 0:MTB], start=True, stop=False))
        tDr = p.op("t", lambda e: e.matmul(self.ps[3][:, 0:MTB], sel, hr[:, MTB:2 * MTB], start=False, stop=True))
        p.waits("s", tmax(tDr))
        tEe = p.op("s", lambda e: e.activation(ar(B.MM_SCR + 5 * MTB, MTB), self.ps[3][:, 0:MTB],
                                               AF.Exp, scale=nacol))
        self.bank_guard(3, tEe)
        if upto == "decay":
            self.tk[("yg", tb, ti)] = tEe
            self.tk["mscr_free_v"] = tm2
            self.tk["mscr_free_t"] = ("pe", p.cnt["pe"])
            self.tk[("mxin", ti)] = prevc
            self.tk[("xhtm", ti, 0)] = ("pe", p.cnt["pe"])
            self.tk[("xhtm", ti, 1)] = ("pe", p.cnt["pe"])
            return
        self.bank_wait(2)
        p.waits("t", tmax(self.tk.get("mdt_done")))
        tdtB = p.op("t", lambda e: e.matmul(self.ps[2][:, 0:MTB], sel, self.sb["mdtb"][:, :],
                                            start=True, stop=True))
        p.waits("s", tmax(tdtB))
        tE1sm = p.op("s", lambda e: e.activation(ar(B.MM_E1, MTB), self.ps[2][:, 0:MTB],
                                                 AF.Exp, scale=acol))
        p.waits("v", tmax(tdtB, vB))
        vdB = p.op("v", lambda e: e.tensor_tensor(ar(B.MM_SCR + 0 * MTB, MTB), ar(B.MM_SCR + 0 * MTB, MTB),
                                                  self.ps[2][:, 0:MTB], op=OP.mult))
        self.bank_guard(2, vdB, tE1sm)
        p.waits("v", tmax(vdB))
        vP = []
        for k in range(NBAND):
            vP.append(p.op("v", lambda e, k=k: e.tensor_tensor(
                ar(B.MM_P0 + k * MTB, MTB), ar(B.MM_SCR + 1 * MTB, MTB),
                ar(B.MM_SCR + 0 * MTB - k, MTB), op=OP.mult)))
        p.waits("v", tmax(tE1sm))
        p.op("v", lambda e: e.tensor_tensor(ar(B.MM_E1, MTB), ar(B.MM_E1, MTB),
                                            self.sb["bmask"][:], op=OP.mult))
        p.op("v", lambda e: e.tensor_tensor(ar(B.MM_E2, MTB), ar(B.MM_E1, MTB),
                                            ar(B.MM_E1 - 1, MTB), op=OP.mult))
        p.op("v", lambda e: e.tensor_tensor(ar(B.MM_E3, MTB), ar(B.MM_E1, MTB),
                                            ar(B.MM_E2 - 1, MTB), op=OP.mult))
        vPE = None
        for k in range(1, NBAND):
            vPE = p.op("v", lambda e, k=k: e.tensor_tensor(
                ar(B.MM_P0 + k * MTB, MTB), ar(B.MM_P0 + k * MTB, MTB),
                ar(B.MM_E1 + (k - 1) * MTB, MTB), op=OP.mult))
        p.waits("v", tmax(tE1, vC))
        vCt = p.op("v", lambda e: e.tensor_tensor(ar(B.MM_SCR + 1 * MTB, MTB), ar(B.MM_SCR + 1 * MTB, MTB),
                                                  ar(B.MM_SCR + 3 * MTB, MTB), op=OP.mult))
        p.waits("v", tmax(tE2, self.tk.get("mscr2_free")))
        vBt = p.op("v", lambda e: e.tensor_tensor(ar(B.MM_SCR2, MTB), ar(B.MM_SCR + 0 * MTB, MTB),
                                                  ar(B.MM_SCR + 4 * MTB, MTB), op=OP.mult))
        p.waits("v", tmax(tEe))
        vBe = p.op("v", lambda e: e.tensor_tensor(ar(B.MM_SCR + 0 * MTB, MTB), ar(B.MM_SCR + 0 * MTB, MTB),
                                                  ar(B.MM_SCR + 5 * MTB, MTB), op=OP.mult))
        dBe = ar(B.MM_SCR + 0 * MTB, MTB)
        for half in range(2):
            for c4 in range(4):
                self.bank_wait(7)
                p.waits("t", tmax(vBe))
                ttr = p.op("t", lambda e, half=half, c4=c4: e.transpose(
                    psb[half * 64:half * 64 + 64, 512 + c4 * 128:512 + (c4 + 1) * 128],
                    dBe[:, c4 * 64:(c4 + 1) * 64], self.sb["ident"][:],
                    tile_position=(0, half * 64)))
            dstt = ar(B.MM_SCR2 + MTB, 512)[half * 64:half * 64 + 64, :]
            p.waits("s", tmax(ttr))
            tev = p.op("s", lambda e, dstt=dstt, half=half: e.activation(
                dstt, psb[half * 64:half * 64 + 64, 512:1024], AF.Copy))
            self.bank_guard(7, tev)
            ev_toks.append(tev)
        dBT_tok = tev
        self.bank_wait(4)
        self.bank_wait(5)
        p.waits("t", tmax(vBt, vCt))
        for c4 in range(4):
            p.raw("t", lambda e, c4=c4: e.matmul(
                self.ps[4][0:64, c4 * 64:(c4 + 1) * 64],
                ar(B.MM_SCR2, MTB)[0:64, c4 * 64:(c4 + 1) * 64],
                ar(B.MM_SCR + 1 * MTB, MTB)[0:64, c4 * 64:(c4 + 1) * 64],
                start=True, stop=True, tile_position=(0, 0)))
            fn = lambda e, c4=c4: e.matmul(
                self.ps[5][64:128, c4 * 64:(c4 + 1) * 64],
                ar(B.MM_SCR2, MTB)[64:128, c4 * 64:(c4 + 1) * 64],
                ar(B.MM_SCR + 1 * MTB, MTB)[64:128, c4 * 64:(c4 + 1) * 64],
                start=True, stop=True, tile_position=(64, 64))
            if c4 == 3:
                tG = p.op("t", fn)
            else:
                p.raw("t", fn)
        mG = ar(B.MM_G, 512)
        p.waits("v", tmax(tG, self.tk.get("mG_free")))
        vGA = p.op("v", lambda e: e.tensor_tensor(mG[0:64, 0:256], self.ps[4][0:64, 0:256],
                                                  self.sb["LT4"][0:64, :], op=OP.mult))
        vGB = p.op("v", lambda e: e.tensor_tensor(mG[64:128, 256:512], self.ps[5][64:128, 0:256],
                                                  self.sb["LT4"][64:128, :], op=OP.mult))
        self.bank_guard(4, vGA)
        self.bank_guard(5, vGB)
        if upto == "noscan":
            self.tk[("yg", tb, ti)] = vGB
            self.tk["mscr_free_v"] = vGB
            self.tk["mscr_free_t"] = ("pe", p.cnt["pe"])
            self.tk["mscr2_free"] = ("pe", p.cnt["pe"])
            self.tk["mG_free"] = vGB
            self.tk[("mxin", ti)] = prevc
            self.tk[("xhtm", ti, 0)] = ("pe", p.cnt["pe"])
            self.tk[("xhtm", ti, 1)] = ("pe", p.cnt["pe"])
            return
        NGBK = NBAND
        gb_ps = [(3, 0), (3, 256), (1, 256), (6, 256)]
        p.waits("t", tmax(tEe, vPE, vP[0],
                          self.tk.get("mgb1"), self.tk.get("mgb2"), self.tk.get("mgb3")))
        tgb = None
        for k in range(NGBK):
            gbank, gco = gb_ps[k]
            fngb = lambda e, k=k, gbank=gbank, gco=gco: e.matmul(
                self.ps[gbank][:, gco:gco + MTB], self.sb["dup64blk"][:],
                ar(B.MM_P0 + k * MTB, MTB), start=True, stop=True)
            tgb = p.op("t", fngb)
        h_ti = self.sb["mh"][:, ti * 64:(ti + 1) * 64]
        hb_ti = self.sb["mhb"][:, ti * 64:(ti + 1) * 64]
        xt = ar(B.MM_XHTM + ti * 512, 512)
        dBT = ar(B.MM_SCR2 + MTB, 512)
        for c4 in range(4):
            p.waits("t", tmax(vCt, vGA, vGB, self.tk.get(("mh_w", ti)),
                              *[self.tk.get(("xhtm_t", ti, hh2)) for hh2 in range(2)]))
            p.raw("t", lambda e, c4=c4: e.matmul(
                self.ps[4][0:64, 256 + c4 * 64:256 + (c4 + 1) * 64],
                hb_ti[0:64, :], ar(B.MM_SCR + 1 * MTB, MTB)[0:64, c4 * 64:(c4 + 1) * 64],
                start=True, stop=False, tile_position=(0, 0)))
            p.raw("t", lambda e, c4=c4: e.matmul(
                self.ps[4][0:64, 256 + c4 * 64:256 + (c4 + 1) * 64],
                xt[0:64, c4 * 128:c4 * 128 + 64], mG[0:64, c4 * 64:(c4 + 1) * 64],
                start=False, stop=True, tile_position=(0, 0)))
            p.raw("t", lambda e, c4=c4: e.matmul(
                self.ps[5][64:128, 256 + c4 * 64:256 + (c4 + 1) * 64],
                hb_ti[64:128, :], ar(B.MM_SCR + 1 * MTB, MTB)[64:128, c4 * 64:(c4 + 1) * 64],
                start=True, stop=False, tile_position=(64, 64)))
            p.raw("t", lambda e, c4=c4: e.matmul(
                self.ps[5][64:128, 256 + c4 * 64:256 + (c4 + 1) * 64],
                xt[64:128, c4 * 128 + 64:c4 * 128 + 128], mG[64:128, 256 + c4 * 64:256 + (c4 + 1) * 64],
                start=False, stop=True, tile_position=(64, 64)))
            self.bank_wait(6)
            p.waits("t", tmax(dBT_tok))
            tSA = p.op("t", lambda e, c4=c4: e.matmul(
                self.ps[6][0:64, 0:64],
                dBT[0:64, c4 * 128:c4 * 128 + 64], xt[0:64, c4 * 128:c4 * 128 + 64],
                start=True, stop=True, tile_position=(0, 0)))
            self.bank_wait(2)
            tSB = p.op("t", lambda e, c4=c4: e.matmul(
                self.ps[2][64:128, 256:320],
                dBT[64:128, c4 * 128 + 64:c4 * 128 + 128], xt[64:128, c4 * 128 + 64:c4 * 128 + 128],
                start=True, stop=True, tile_position=(64, 64)))
            pf = self.sb["mPfull"][:, ti * 4 + c4: ti * 4 + c4 + 1]
            p.waits("v", tmax(tSA, tSB))
            thA = p.op("v", lambda e, pf=pf: e.scalar_tensor_tensor(
                h_ti[0:64, :], h_ti[0:64, :], pf[0:64, :], self.ps[6][0:64, 0:64],
                op0=OP.mult, op1=OP.add))
            thB = p.op("v", lambda e, pf=pf: e.scalar_tensor_tensor(
                h_ti[64:128, :], h_ti[64:128, :], pf[64:128, :], self.ps[2][64:128, 256:320],
                op0=OP.mult, op1=OP.add))
            self.bank_guard(6, thA)
            self.bank_guard(2, thB)
            p.waits("v", tmax(thA, thB))
            thb = p.op("v", lambda e: e.tensor_copy(hb_ti, h_ti))
            self.tk[("mh_w", ti)] = thb
        tyT = ("pe", p.cnt["pe"])
        if self.debug and tb == 0 and ti == 0:
            p.waits("v", tmax(tyT))
            p.op("v", lambda e: e.tensor_copy(ar(50000, 256)[0:64, :], self.ps[4][0:64, 256:512]))
            p.op("v", lambda e: e.tensor_copy(ar(50000, 256)[64:128, :], self.ps[5][64:128, 256:512]))
            p.op("v", lambda e: e.tensor_copy(ar(50256, 64), self.sb["mhb"][:, 0:64]))
            p.op("v", lambda e: e.tensor_copy(ar(50320, 256), ar(B.MM_SCR + 1 * MTB, MTB)))
        t_z = None
        self.bank_wait(0)
        p.waits("t", tmax(twin))
        for kt in range(NT):
            lhsT = ar(B.MM_WIN + pp * 4096 + kt * 512 + 384, 128)
            rhs = ar(B.MM_XN + kt * SEQ + tb * MTB, MTB)
            fn = lambda e, lhsT=lhsT, rhs=rhs, kt=kt: e.matmul(
                self.ps[0][:, 0:MTB], lhsT, rhs, start=(kt == 0), stop=(kt == NT - 1))
            if kt == NT - 1:
                t_z = p.op("t", fn)
            else:
                p.raw("t", fn)
        p.waits("s", tmax(t_z, vCt))
        tsz = p.op("s", lambda e: e.activation(ar(B.MM_SCR + 3 * MTB, MTB), self.ps[0][:, 0:MTB], AF.Silu))
        self.bank_guard(0, tsz)
        self.tk[("mwin", pp)] = t_z
        dcol = self.sb["d_cols"][:, ti:ti + 1]
        s4 = ar(B.MM_SCR + 4 * MTB, MTB)
        s5 = ar(B.MM_SCR + 5 * MTB, MTB)
        xh0 = B.MM_SCR + 2 * MTB
        p.waits("v", tmax(tsil, vBt, vBe, tgb, t_z))
        if NGBK > 0:
            vdx = p.op("v", lambda e: e.scalar_tensor_tensor(
                s4, self.ps[3][:, 0:MTB], dcol, ar(xh0, MTB), op0=OP.add, op1=OP.mult))
        else:
            vdx = p.op("v", lambda e: e.tensor_scalar(
                s4, ar(xh0, MTB), dcol, None, op0=OP.mult))
        mtoks = []
        for k in range(1, NGBK):
            gbank, gco = gb_ps[k]
            mk = p.op("v", lambda e, gbank=gbank, gco=gco, k=k: e.tensor_tensor(
                s5, self.ps[gbank][:, gco:gco + MTB], ar(xh0 - k, MTB), op=OP.mult))
            mtoks.append(mk)
            vdx = p.op("v", lambda e: e.tensor_tensor(s4, s4, s5, op=OP.add))
        if NGBK > 0:
            self.bank_guard(3, vdx if not mtoks else mtoks[0])
        if len(mtoks) > 0:
            self.tk["mgb1"] = mtoks[0]
        if len(mtoks) > 1:
            self.tk["mgb2"] = mtoks[1]
            self.bank_guard(1, mtoks[1])
        if len(mtoks) > 2:
            self.tk["mgb3"] = mtoks[2]
            self.bank_guard(6, mtoks[2])
        p.waits("v", tmax(tyT, vdx, self.tk.get("mscrf_w"), self.tk.get("mscrf_rs")))
        vy2a = p.op("v", lambda e: e.tensor_tensor(
            self.sb["mscrf"][0:64, 0:MTB], self.ps[4][0:64, 256:512],
            ar(B.MM_SCR + 4 * MTB, MTB)[0:64, :], op=OP.add))
        vy2 = p.op("v", lambda e: e.tensor_tensor(
            self.sb["mscrf"][64:128, 0:MTB], self.ps[5][64:128, 256:512],
            ar(B.MM_SCR + 4 * MTB, MTB)[64:128, :], op=OP.add))
        self.bank_guard(4, vy2a)
        self.bank_guard(5, vy2)
        self.tk["mscrf_w"] = vy2
        yg = ar(B.MM_YG + ti * MTB, MTB)
        p.waits("v", tmax(tsz, self.tk.get(("myg", ti))))
        vyg = p.op("v", lambda e, yg=yg: e.tensor_tensor(yg, self.sb["mscrf"][:, 0:MTB],
                                                         ar(B.MM_SCR + 3 * MTB, MTB), op=OP.mult))
        self.tk[("yg", tb, ti)] = vyg
        self.tk["mscr_free_v"] = vyg
        self.tk["mscr_free_t"] = tyT
        self.tk["mscr2_free"] = tyT
        self.tk["mG_free"] = tyT
        self.tk[("mxin", ti)] = prevc
        self.tk[("xhtm", ti, 0)] = tyT
        self.tk[("xhtm", ti, 1)] = tyT

    def _mamba_wout(self, tb):
        p = self.p
        ar = self.ar
        B = Build
        stage = self.sb["stage"]
        for oc in range(NT):
            pp = oc % 2
            prev = self.tk.get(("mwout", pp))
            if prev is not None:
                p.waits("g", tmax(prev))
            two = p.dma("g", "dq1",
                        ar(B.MM_WOUT + pp * 2048, 2048).rearrange("p (kt o) -> p kt o", o=128),
                        self.dram["woutT"][:, oc * 128:(oc + 1) * 128].rearrange("(kt p) o -> p kt o", p=128))
            bk = self.ps[1]
            self.bank_wait(1)
            p.waits("t", tmax(two, *[self.tk.get(("yg", tb, ti2)) for ti2 in range(NTI)]))
            for kt in range(16):
                lhsT = ar(B.MM_WOUT + pp * 2048 + kt * 128, 128)
                rhs = ar(B.MM_YG + kt * MTB, MTB)
                fn = lambda e, lhsT=lhsT, rhs=rhs, kt=kt: e.matmul(
                    bk[:, 0:MTB], lhsT, rhs, start=(kt == 0), stop=(kt == 15))
                if kt == 15:
                    tmm = p.op("t", fn)
                else:
                    p.raw("t", fn)
            self.tk[("mwout", pp)] = tmm
            if oc == NT - 1:
                for ti2 in range(NTI):
                    self.tk[("myg", ti2)] = tmm
            st = stage[:, (oc % 2) * CHW:(oc % 2) * CHW + MTB]
            p.waits("y", tmax(self.tk.get(("stageslot", oc % 2)), self.tk.get(("stageslot_s", oc % 2))))
            tld = p.dma("y", "dq2", st,
                        self.dram["xres"][oc * 128:(oc + 1) * 128, tb * MTB:(tb + 1) * MTB])
            p.waits("v", tmax(tmm, tld))
            tv = p.op("v", lambda e, st=st: e.tensor_tensor(st, bk[:, 0:MTB], st, op=OP.add))
            self.bank_guard(1, tv)
            p.wait("y", tv)
            td = p.dma("y", "dq2",
                       self.dram["xres"][oc * 128:(oc + 1) * 128, tb * MTB:(tb + 1) * MTB], st)
            self.tk[("stageslot", oc % 2)] = td
            self.final_toks.append(td)

    def finish(self):
        p = self.p
        for q in ["dq0", "dq1", "dq2", "dq3"]:
            for i in range(3):
                s = f"{q}_{i}"
                p.ops["y"].append(lambda e, sem=p.semh[s], val=p.cnt[s]: e.wait_ge(sem, val))


def build_and_run(maps, core_ids, stages=("full",), debug=False, trace=False):
    nc = bass.Bass("TRN2", target_bir_lowering=False)
    ctx = ExitStack()
    with ctx:
        b = Build(nc, ctx, stages, debug)
        b.emit_consts()
        st = set(stages)
        need_ln0 = st & {"full", "ln0", "attn0", "ffn0", "ffnln", "stack1", "mamba"}
        if need_ln0:
            b.emit_outer_ln("x0", "xstack")
        if "attn0" in st:
            b.emit_attn_sub(0)
        if "ffn0" in st:
            b.emit_attn_sub(0)
            b.emit_ffn_sub(0)
        if "ffnln" in st:
            b.emit_attn_sub(0)
            b.emit_ffn_sub(0, ln_only=True)
        if "stack1" in st:
            for l in range(DEPTH):
                b.emit_attn_sub(l)
                b.emit_ffn_sub(l)
        if "mamba" in st:
            for l in range(DEPTH):
                b.emit_attn_sub(l)
                b.emit_ffn_sub(l)
            b.emit_residual_combine("x0", "xstack", 0.5, "xres", mamba_xn=True)
            b.tk["mxn"] = ("ve", b.p.cnt["ve"])
            b.emit_mamba()
        for v in ["monly", "monly_dt", "monly_proj", "monly_xht", "monly_decay", "monly_noscan", "monly_nowout",
                  "monly_decay1", "monly_all1"]:
            if v in st:
                b.emit_residual_combine("x0", "xstack", 0.5, "xres", mamba_xn=True)
                b.tk["mxn"] = ("ve", b.p.cnt["ve"])
                b.emit_mamba(upto=v[6:] if len(v) > 5 else "all")
                if b.debug:
                    pp = b.p
                    frees = tmax(("ve", pp.cnt["ve"]), ("se", pp.cnt["se"]), ("pe", pp.cnt["pe"]))
                    pp.waits("g", frees)
                    dd = b.dram["dbga"]
                    b.final_toks.append(pp.dma("g", "dq1", dd[:, 0:1536], b.ar(Build.MM_SCR, 1536)))
                    b.final_toks.append(pp.dma("g", "dq1", dd[:, 1536:2048], b.ar(Build.MM_G, 512)))
                    b.final_toks.append(pp.dma("g", "dq1", dd[0:32, 2048:2304], b.sb["mdt"][:, :]))
                    b.final_toks.append(pp.dma("g", "dq1", dd[0:32, 2304:2560], b.sb["mDl"][:, :]))
                    b.final_toks.append(pp.dma("g", "dq1", dd[:, 2560:2624], b.sb["mPfull"][:, :]))
                    b.final_toks.append(pp.dma("g", "dq1", dd[:, 2624:2880], b.ar(Build.MM_YG, 256)))
                    b.final_toks.append(pp.dma("g", "dq1", dd[:, 2880:2944], b.sb["mh"][:, 0:64]))
                    b.final_toks.append(pp.dma("g", "dq1", dd[:, 2944:3456], b.ar(Build.MM_SCR2 + 256, 512)))
                    b.final_toks.append(pp.dma("g", "dq1", dd[:, 3456:3968], b.ar(Build.MM_XHTM, 512)))
                    b.final_toks.append(pp.dma("g", "dq1", dd[:, 3968:4224], b.ar(50000, 256)))
                    b.final_toks.append(pp.dma("g", "dq1", dd[:, 4224:4288], b.ar(50256, 64)))
                    b.final_toks.append(pp.dma("g", "dq1", dd[:, 4288:4544], b.ar(50320, 256)))
                    b.final_toks.append(pp.dma("g", "dq1", dd[:, 4544:4800], b.ar(Build.MM_SCR2, 256)))
        if "full" in st:
            for l in range(DEPTH):
                b.emit_attn_sub(l)
                b.emit_ffn_sub(l)
            b.emit_residual_combine("x0", "xstack", 0.5, "xres", mamba_xn=True)
            b.tk["mxn"] = ("ve", b.p.cnt["ve"])
            b.emit_mamba()
            b.phase_barrier()
            b.emit_outer_ln("xres", "xstack")
            for l in range(DEPTH):
                b.emit_attn_sub(l)
                b.emit_ffn_sub(l)
        if "full" in st or "fin" in st:
            b.phase_barrier()
            b.emit_final_combine()
        b.finish()
        with nc.Block() as block:
            b.p.run(block)
        if maps is None:
            ctx.pop_all()
            return nc
        res = run_bass_kernel_spmd(nc, maps, core_ids=core_ids, trace=trace)
    return res


def _in_maps(inputs, n_cores=8):
    consts = _host_consts()
    w = _host_weights(inputs)
    x = inputs["L0_attn_x"]
    maps = []
    for i in range(n_cores):
        m = {}
        m["x0"] = np.ascontiguousarray(x[i].T).astype(np.float32)
        for k, v in w.items():
            m[k] = v
        for k, v in consts.items():
            kk = "LTm" if k == "LT" else k
            m[kk] = v
        maps.append(m)
    return maps


def kernel(**inputs):
    maps = _in_maps(inputs)
    res = build_and_run(maps, list(range(8)), stages=("full",), debug=False)
    out = np.stack([np.ascontiguousarray(np.asarray(r["y"], np.float32).T)
                    for r in res.results], axis=0)
    return out


# version 08
# speedup vs baseline: 1.1276x; 1.1276x over previous
import numpy as np
import ml_dtypes
import concourse.bass as bass
import concourse.mybir as mybir
from concourse.bass_utils import run_bass_kernel_spmd
from concourse.alu_op_type import AluOpType as OP
from contextlib import ExitStack

F32 = mybir.dt.float32
BF16 = mybir.dt.bfloat16
AF = mybir.ActivationFunctionType
bf16 = ml_dtypes.bfloat16

DM = 1024
SEQ = 2048
NT = 8
HEADS = 16
DH = 64
NPAIR = 8
DEPTH = 6
FI = 2730
FIP = 2816
KH = 11
DIN = 2048
NH = 32
MTB = 256
MCH = 64
NTI = 16
CHW = 512
NCHUNK = SEQ // CHW
VS = 66
NBAND = 4


def _host_consts():
    c = {}
    f64 = np.float64
    half = np.arange(0, DH, 2, dtype=f64)
    inv_freq = 1.0 / 10000.0 ** (half / DH)
    t = np.arange(SEQ, dtype=f64)
    freqs = np.concatenate([t[:, None] * inv_freq[None, :]] * 2, -1)
    base_scale = (half + 0.4 * DH) / (1.4 * DH)
    power = (t - SEQ // 2) / 128.0
    scale = np.concatenate([base_scale[None, :] ** power[:, None]] * 2, -1)
    cos, sin = np.cos(freqs), np.sin(freqs)
    tile2 = lambda a: np.ascontiguousarray(np.concatenate([a.T, a.T], 0)).astype(np.float32)
    c["cs_q"], c["ss_q"] = tile2(cos * scale), tile2(sin * scale)
    c["cs_k"], c["ss_k"] = tile2(cos / scale), tile2(sin / scale)
    R = np.zeros((DH, DH), f64)
    for i in range(32):
        R[i, i + 32] = -1.0
        R[i + 32, i] = 1.0
    rt2 = np.zeros((128, 128), f64)
    rt2[:64, :64] = R.T
    rt2[64:, 64:] = R.T
    c["rot"] = rt2.astype(np.float32)
    jj, ii = np.meshgrid(np.arange(128), np.arange(128), indexing="ij")
    c["UT"] = (jj >= ii).astype(np.float32)
    c["LTm"] = (jj <= ii).astype(np.float32)
    p = np.arange(128)[:, None] % 64
    q = np.arange(MTB)[None, :] % 64
    c["LT4"] = (q >= p + NBAND).astype(np.float32)
    bm = np.ones((128, MTB), np.float32)
    bm[:, ::MCH] = 0.0
    c["bmask"] = bm
    dd = np.zeros((128, 128), np.float64)
    dd[:64, :64] = 1.0
    dd[64:, 64:] = 1.0
    c["dup64blk"] = dd.astype(np.float32)
    c["ident"] = np.eye(128).astype(np.float32)
    am = np.zeros((128, 9 * 128), f64)
    jj2, ii2 = np.meshgrid(np.arange(128), np.arange(128), indexing="ij")
    am[:, 3 * 128:4 * 128] = (jj2 <= ii2)
    am[:, 4 * 128:5 * 128] = 1.0
    am[:, 5 * 128:6 * 128] = (jj2 >= ii2)
    c["amask"] = am.astype(np.float32)
    c["onescol"] = np.ones((128, 1), np.float32)
    c["onescol_b"] = np.ones((128, 1)).astype(np.float32)
    sel2 = np.zeros((128, 2), f64)
    sel2[:64, 0] = 1.0
    sel2[64:, 1] = 1.0
    c["sel2"] = sel2.astype(np.float32)
    selt22 = np.zeros((4, 128), f64)
    selt22[0, :64] = 1.0; selt22[1, 64:] = 1.0
    selt22[2, :64] = 1.0; selt22[3, 64:] = 1.0
    c["selt22"] = selt22.astype(np.float32)
    c["ones2T"] = np.ones((2, 128)).astype(np.float32)
    c["onesrow"] = np.ones((1, 128)).astype(np.float32)
    selA = np.zeros((1, 128)); selA[0, :64] = 1.0
    selB = np.zeros((1, 128)); selB[0, 64:] = 1.0
    c["selA"] = selA.astype(np.float32)
    c["selB"] = selB.astype(np.float32)
    selscan = np.zeros((32, NTI * 128), f64)
    for ti in range(NTI):
        for pp in range(128):
            selscan[2 * ti + pp // 64, ti * 128 + pp] = 1.0
    c["selscan"] = selscan.astype(np.float32)
    m = np.ones((32, MTB), np.float32)
    m[:, ::MCH] = 0.0
    c["scanmask"] = m
    c["c_neg30"] = np.full((128, 1), -30.0, np.float32)
    c["c_eps5"] = np.full((128, 1), 1e-5, np.float32)
    c["c_eps24"] = np.full((128, 1), 1e-24, np.float32)
    c["c_one"] = np.full((128, 1), 1.0, np.float32)
    return c


def _host_weights(inp):
    w = {}
    w["wqkv"] = np.ascontiguousarray(inp["L0_ffn_Wqkv"], np.float32)
    w["wo"] = np.ascontiguousarray(inp["L0_ffn_Wo"], np.float32)
    W1 = inp["L0_ffn_W1"]
    w1a = np.zeros((DEPTH, DM, FIP), np.float32); w1a[:, :, :FI] = W1[:, :, :FI]
    w1g = np.zeros((DEPTH, DM, FIP), np.float32); w1g[:, :, :FI] = W1[:, :, FI:]
    w["w1a"], w["w1g"] = w1a, w1g
    w2p = np.zeros((DEPTH, FIP, DM), np.float32); w2p[:, :FI, :] = inp["L0_ffn_W2"]
    w["w2"] = w2p
    qs, ks = inp["L0_ffn_q_scale"], inp["L0_ffn_k_scale"]
    w["qs_cols"] = np.ascontiguousarray(np.concatenate([qs, qs], 1).T, np.float32)
    w["ks_cols"] = np.ascontiguousarray(np.concatenate([ks, ks], 1).T, np.float32)
    g, bb = inp["L0_ffn_ln_g"], inp["L0_ffn_ln_b"]
    tcol = lambda a, k: np.ascontiguousarray(
        a.reshape(DEPTH, k, 128).transpose(2, 0, 1).reshape(128, DEPTH * k), np.float32)
    w["lng_cols"], w["lnb_cols"] = tcol(g, 8), tcol(bb, 8)
    b1 = inp["L0_ffn_b1"]
    b1a = np.zeros((DEPTH, FIP), np.float32); b1a[:, :FI] = b1[:, :FI]
    b1g = np.zeros((DEPTH, FIP), np.float32); b1g[:, :FI] = b1[:, FI:]
    w["b1a_cols"], w["b1g_cols"] = tcol(b1a, 22), tcol(b1g, 22)
    w["b2_cols"] = tcol(inp["L0_ffn_b2"], 8)
    Win = inp["L0_attn_W_in"]
    base = 4096 + 129 * np.arange(NH)
    brows = np.concatenate([base[n] + np.arange(64) for n in range(NH)])
    crows = np.concatenate([base[n] + 64 + np.arange(64) for n in range(NH)])
    perm = np.zeros(8224, np.int64)
    for ti in range(16):
        o = ti * 512
        perm[o:o + 128] = np.arange(ti * 128, ti * 128 + 128)
        perm[o + 128:o + 256] = brows[ti * 128:(ti + 1) * 128]
        perm[o + 256:o + 384] = crows[ti * 128:(ti + 1) * 128]
        perm[o + 384:o + 512] = np.arange(2048 + ti * 128, 2048 + (ti + 1) * 128)
    perm[8192:8224] = base + 128
    w["winT"] = np.ascontiguousarray(Win[perm].T, np.float32)
    w["woutT"] = np.ascontiguousarray(inp["L0_attn_W_out"].T, np.float32)
    cw = inp["L0_attn_conv_w"]
    w["convw_cols"] = np.ascontiguousarray(cw.reshape(16, 128, 4).transpose(1, 0, 2).reshape(128, 64), np.float32)
    w["convb_cols"] = np.ascontiguousarray(inp["L0_attn_conv_b"].reshape(16, 128).T, np.float32)
    w["alog_cols"] = np.ascontiguousarray(inp["L0_attn_A_log"].reshape(2048).reshape(16, 128).T, np.float32)
    w["d_cols"] = np.ascontiguousarray(np.repeat(inp["L0_attn_D"], 64).reshape(16, 128).T, np.float32)
    w["dtb_col"] = np.ascontiguousarray(inp["L0_attn_dt_bias"].reshape(32, 1), np.float32)
    return w


class Prog:
    def __init__(self, nc, ctx):
        self.nc = nc
        self.ops = {e: [] for e in "tvsgy"}
        self.semh = {}
        self.cnt = {}
        self.memo = {}
        names = ["pe", "ve", "se", "ge"]
        for q in ["dq0", "dq1", "dq2", "dq3"]:
            for i in range(3):
                names.append(f"{q}_{i}")
        for name in names:
            self.semh[name] = ctx.enter_context(nc.semaphore(name))
            self.cnt[name] = 0
        self.dqi = {q: 0 for q in ["dq0", "dq1", "dq2", "dq3"]}
        self.eng_sem = {"t": "pe", "v": "ve", "s": "se", "g": "ge"}

    def raw(self, eng, fn):
        if eng == "t":
            self.op(eng, fn)
            return
        self.ops[eng].append(fn)

    def op(self, eng, fn):
        sname = self.eng_sem[eng]
        if eng in "vsg":
            self.wait(eng, (sname, self.cnt[sname]))
        self.cnt[sname] += 1
        val = self.cnt[sname]
        sem = self.semh[sname]
        def run(e, fn=fn, sem=sem):
            fn(e).then_inc(sem, 1)
        self.ops[eng].append(run)
        return (sname, val)

    def tok(self, eng):
        s = self.eng_sem[eng]
        return (s, self.cnt[s])

    def wait(self, eng, token):
        if token is None:
            return
        sname, val = token
        key = (eng, sname)
        if self.memo.get(key, 0) >= val:
            return
        self.memo[key] = val
        sem = self.semh[sname]
        self.ops[eng].append(lambda e, sem=sem, val=val: e.wait_ge(sem, val))

    def waits(self, eng, tokens):
        for tk in tokens:
            self.wait(eng, tk)

    def dma(self, eng, q, dst, src):
        i = self.dqi[q]
        self.dqi[q] = (i + 1) % 3
        sname = f"{q}_{i}"
        self.wait(eng, (sname, self.cnt[sname]))
        self.cnt[sname] += 16
        val = self.cnt[sname]
        sem = self.semh[sname]
        def run(e, dst=dst, src=src, sem=sem):
            e.dma_start(dst, src).then_inc(sem, 16)
        self.ops[eng].append(run)
        return (sname, val)

    def run(self, block):
        secs = {"t": block.tensor, "v": block.vector, "s": block.scalar,
                "g": block.gpsimd, "y": block.sync}
        for e in "tvsgy":
            ops = self.ops[e]
            @secs[e]
            def _(eng, ops=ops):
                for f in ops:
                    f(eng)


def tmax(*toks):
    best = {}
    for t in toks:
        if t is None:
            continue
        s, v = t
        if best.get(s, -1) < v:
            best[s] = v
    return [(s, v) for s, v in best.items()]


class Build:
    def __init__(self, nc, ctx, stages, debug):
        self.nc = nc
        self.p = Prog(nc, ctx)
        self.ctx = ctx
        self.stages = stages
        self.debug = debug
        self.sb = {}
        self.ps = {}
        self.dram = {}
        self.tk = {}
        self.final_toks = []
        self._mk_dram()
        self._mk_sbuf()
        self._mk_psum()

    def _mk_dram(self):
        nc = self.nc
        d = self.dram
        def di(name, shape, dt=F32):
            d[name] = nc.dram_tensor(name, list(shape), dt, kind="ExternalInput").ap()
        di("x0", (DM, SEQ))
        di("wqkv", (DEPTH, DM, 3 * DM)); di("wo", (DEPTH, DM, DM))
        di("w1a", (DEPTH, DM, FIP)); di("w1g", (DEPTH, DM, FIP)); di("w2", (DEPTH, FIP, DM))
        di("winT", (DM, 8224)); di("woutT", (DIN, DM))
        di("qs_cols", (128, 6)); di("ks_cols", (128, 6))
        di("lng_cols", (128, 48)); di("lnb_cols", (128, 48))
        di("b1a_cols", (128, 132)); di("b1g_cols", (128, 132)); di("b2_cols", (128, 48))
        di("convw_cols", (128, 64)); di("convb_cols", (128, 16))
        di("alog_cols", (128, 16)); di("d_cols", (128, 16)); di("dtb_col", (32, 1))
        di("amask", (128, 9 * 128))
        for n, shp in [("cs_q", (128, SEQ)), ("ss_q", (128, SEQ)),
                       ("cs_k", (128, SEQ)), ("ss_k", (128, SEQ)),
                       ("rot", (128, 128)), ("UT", (128, 128)), ("LTm", (128, 128)),
                       ("LT4", (128, MTB)), ("bmask", (128, MTB)),
                       ("dup64blk", (128, 128)), ("ident", (128, 128)),
                       ("onescol_b", (128, 1)), ("sel2", (128, 2)),
                       ("selt22", (4, 128)), ("ones2T", (2, 128)),
                       ("onesrow", (1, 128)), ("selA", (1, 128)), ("selB", (1, 128)),
                       ("selscan", (32, NTI * 128))]:
            di(n, shp)
        for n, shp in [("onescol", (128, 1)), ("scanmask", (32, MTB)),
                       ("c_neg30", (128, 1)), ("c_eps5", (128, 1)), ("c_eps24", (128, 1)),
                       ("c_one", (128, 1))]:
            di(n, shp)
        kd = "ExternalOutput" if self.debug else "Internal"
        d["xres"] = nc.dram_tensor("xres", [DM, SEQ], F32, kind=kd).ap()
        d["xstack"] = nc.dram_tensor("xstack", [DM, SEQ], F32, kind=kd).ap()
        d["y"] = nc.dram_tensor("y", [DM, SEQ], F32, kind="ExternalOutput").ap()
        if self.debug:
            d["dbgrows"] = nc.dram_tensor("dbgrows", [4, 5 * CHW], F32, kind="ExternalOutput").ap()
            d["dbga"] = nc.dram_tensor("dbga", [128, 16384], F32, kind="ExternalOutput").ap()

    def _mk_sbuf(self):
        S = lambda n, s, dt: self.sb.__setitem__(
            n, self.ctx.enter_context(self.nc.sbuf_tensor("sb_" + n, list(s), dt)))
        for n in ["rot", "UT", "LTm", "ident"]:
            S(n, (128, 128), BF16)
        S("LT4", (128, MTB), BF16)
        S("bmask", (128, MTB), BF16); S("dup64blk", (128, 128), BF16)
        S("onescol_b", (128, 1), BF16); S("sel2", (128, 2), BF16)
        S("selt22", (4, 128), BF16); S("ones2T", (2, 128), BF16)
        S("onesrow", (1, 128), BF16); S("selA", (1, 128), BF16); S("selB", (1, 128), BF16)
        S("selscan", (32, NTI * 128), BF16); S("scanmask", (32, MTB), F32)
        for n in ["onescol", "c_neg30", "c_eps5", "c_eps24", "c_one"]:
            S(n, (128, 1), F32)
        S("qs_cols", (128, 6), F32); S("ks_cols", (128, 6), F32)
        S("lng_cols", (128, 48), F32); S("lnb_cols", (128, 48), F32)
        S("b1a_cols", (128, 132), F32); S("b1g_cols", (128, 132), F32)
        S("b2_cols", (128, 48), F32)
        S("convw_cols", (128, 64), F32); S("convb_cols", (128, 16), F32)
        S("alog_cols", (128, 16), F32); S("d_cols", (128, 16), F32)
        S("dtb_col", (32, 1), F32)
        S("A_cols", (128, 16), F32); S("negA_cols", (128, 16), F32)
        S("clamp_cols", (128, 16), F32)
        S("xs", (128, 2 * NT * (CHW + 2)), BF16)
        S("stage", (128, 2 * CHW), F32)
        S("outacc", (128, NT * CHW), F32)
        S("scr", (128, 4 * CHW), BF16)
        S("scrf", (128, 2 * CHW), F32)
        S("rows", (1, 5 * CHW), F32)
        S("rowsb", (1, 4 * CHW), BF16)
        S("attnw", (128, 4 * CHW), BF16)
        S("srecip", (1, 2 * CHW), BF16)
        S("amask", (128, 9 * 128), BF16)
        S("mh", (128, NTI * 64), F32)
        S("mhb", (128, NTI * 64), BF16)
        S("xtails", (128, NTI * 3), BF16)
        S("mdt", (32, MTB), F32); S("mdtb", (32, MTB), BF16)
        S("mDl", (32, MTB), F32); S("mDrev", (32, MTB), F32)
        S("mDlhl", (32, 2 * MTB), BF16)
        S("mDrevhl", (32, 2 * MTB), BF16)
        S("mDtot", (32, 4), F32)
        S("mDtotb", (32, 8), BF16)
        S("mPfull", (128, NTI * 4), F32)
        S("mscrf", (128, MTB), F32)
        S("arena", (128, 59648), BF16)

    def ar(self, off, n):
        return self.sb["arena"][:, off:off + n]

    AT_WQKV = 0
    AT_KR = 24576
    AT_Q = 32768
    AT_VR = 36864
    AT_AO = 45312
    AT_WO = 49408
    AT_TAB = 57600
    FF_W1A = 0
    FF_W1G = 11264
    FF_W2 = 22528
    FF_XN = 33792
    FF_H2 = 50176
    MM_XN = 0
    MM_WIN = 16384
    MM_WOUT = 24576
    MM_XHTM = 28672
    MM_YG = 36864
    MM_XIN = 40960
    MM_SCR = 45104
    MM_SCR2 = 46640
    MM_G = 47408
    MM_WDT = 47920
    MM_E1 = 48176
    MM_E2 = 48432
    MM_E3 = 48688
    MM_P0 = 48944

    def _mk_psum(self):
        for i in range(7):
            self.ps[i] = self.ctx.enter_context(
                self.nc.psum_tensor(f"bank{i}", [128, 512], F32))
        self.ps[7] = self.ctx.enter_context(
            self.nc.psum_tensor("bank7", [128, 1024], BF16))

    def xs_slice(self, parity, tile, shifted, width=CHW):
        base = parity * NT * (CHW + 2) + tile * (CHW + 2)
        off = 1 if shifted else 2
        return self.sb["xs"][:, base + off: base + off + width]

    def bank_guard(self, bank, *toks):
        old = self.tk.get(("bank", bank)) or []
        self.tk[("bank", bank)] = tmax(*(old + list(toks)))

    def bank_wait(self, bank):
        toks = self.tk.get(("bank", bank)) or []
        self.p.waits("t", toks)
        self.tk[("bank", bank)] = []

    def emit_consts(self):
        p = self.p
        toks = []
        for n in ["rot", "UT", "LTm", "LT4", "bmask", "dup64blk",
                  "ident", "onescol_b", "sel2", "selt22", "ones2T",
                  "onesrow", "selA", "selB", "selscan",
                  "scanmask", "onescol", "c_neg30", "c_eps5", "c_eps24", "c_one",
                  "qs_cols", "ks_cols", "lng_cols", "lnb_cols",
                  "b1a_cols", "b1g_cols", "b2_cols", "convw_cols",
                  "convb_cols", "alog_cols", "d_cols", "dtb_col", "amask"]:
            toks.append(p.dma("g", "dq0", self.sb[n][:], self.dram[n]))
        consts_tok = tmax(*toks)
        p.waits("s", consts_tok)
        p.waits("v", consts_tok)
        p.waits("t", consts_tok)
        p.waits("g", consts_tok)
        t1 = p.op("s", lambda e: e.activation(self.sb["negA_cols"][:], self.sb["alog_cols"][:], AF.Exp))
        p.wait("v", t1)
        p.op("v", lambda e: e.tensor_scalar(self.sb["A_cols"][:], self.sb["negA_cols"][:], -1.0, None, op0=OP.mult))
        p.op("v", lambda e: e.reciprocal(self.sb["clamp_cols"][:], self.sb["negA_cols"][:]))
        p.op("v", lambda e: e.tensor_scalar(self.sb["clamp_cols"][:], self.sb["clamp_cols"][:], 78.0, None, op0=OP.mult))
        for nm in ["arena", "scr", "scrf", "rows", "rowsb", "attnw", "srecip",
                   "stage", "outacc", "xs", "mscrf", "mdt", "mdtb",
                   "mDl", "mDrev", "mDlhl", "mDrevhl", "mDtot", "mDtotb", "mPfull", "mhb"]:
            p.op("v", lambda e, nm=nm: e.memset(self.sb[nm][:], 0.0))
        p.op("v", lambda e: e.memset(self.sb["mh"][:], 0.0))
        p.op("v", lambda e: e.memset(self.sb["mhb"][:], 0.0))
        p.op("v", lambda e: e.memset(self.sb["xtails"][:], 0.0))
        self.tk["init"] = p.tok("v")

    def load_xs_chunk(self, src_dram, c, parity, q="dq1"):
        if isinstance(src_dram, str):
            src_dram = self.dram[src_dram]
        p = self.p
        xs = self.sb["xs"]
        base = parity * NT * (CHW + 2)
        t0 = c * CHW
        prev = self.tk.get(("xsfree", parity))
        if prev is not None:
            p.waits("g", tmax(*prev) if isinstance(prev, list) else tmax(prev))
        p.waits("g", tmax(self.tk.get(("xstail", parity))))
        if c == 0:
            zt = p.op("v", lambda e: e.memset(
                xs[:, base:base + NT * (CHW + 2)].rearrange("p (a c) -> p a c", c=CHW + 2)[:, :, 0:2], 0.0))
            p.wait("g", zt)
            src = src_dram[:, 0:CHW].rearrange("(a p) t -> p a t", p=128)
            dst = xs[:, base:base + NT * (CHW + 2)].rearrange("p (a c) -> p a c", c=CHW + 2)[:, :, 2:CHW + 2]
            tmain = p.dma("g", q, dst, src)
            self.tk[("xsload", parity)] = tmain
            return tmain
        grid = xs[:, base:base + NT * (CHW + 2)].rearrange("p (a c) -> p a c", c=CHW + 2)
        src = src_dram[:, t0: t0 + CHW].rearrange("(a p) t -> p a t", p=128)
        tmain = p.dma("g", q, grid[:, :, 2:CHW + 2], src)
        pbase = (1 - parity) * NT * (CHW + 2)
        pgrid = xs[:, pbase:pbase + NT * (CHW + 2)].rearrange("p (a c) -> p a c", c=CHW + 2)
        p.waits("v", tmax(self.tk.get(("xsload", 1 - parity)), tmain))
        tcp = p.op("v", lambda e: e.tensor_copy(grid[:, :, 0:2], pgrid[:, :, CHW:CHW + 2]))
        self.tk[("xsload", parity)] = tmain
        self.tk[("xstail", 1 - parity)] = tcp
        return tcp

    def emit_ln(self, tiles, writer, in_toks=None):
        p = self.p
        sq = self.sb["scr"]
        rows = self.sb["rows"]
        rowsb = self.sb["rowsb"]
        one = self.sb["onescol_b"]
        bs, bb, bq = self.ps[4], self.ps[5], self.ps[6]
        if in_toks is None:
            in_toks = [None] * NT
        R = lambda i: rows[0:1, i * CHW:(i + 1) * CHW]
        Rb = lambda i: rowsb[0:1, i * CHW:(i + 1) * CHW]
        self.bank_wait(4)
        self.bank_wait(6)
        sq_toks = []
        for a in range(NT):
            dst = sq[:, (a % 4) * CHW:(a % 4 + 1) * CHW]
            prev = self.tk.get(("sqslot", a % 4))
            p.waits("s", tmax(prev, in_toks[a]))
            sqt = p.op("s", lambda e, dst=dst, srcap=tiles[a]: e.activation(dst, srcap, AF.Square))
            sq_toks.append(sqt)
            p.waits("t", tmax(in_toks[a]))
            p.raw("t", lambda e, a=a, srcap=tiles[a]: e.matmul(
                bs[0:1, 0:CHW], one[:], srcap, start=(a == 0), stop=(a == NT - 1)))
            p.waits("t", tmax(sqt))
            last = a == NT - 1
            fn = lambda e, dst=dst, a=a, last=last: e.matmul(
                bq[0:1, 0:CHW], one[:], dst, start=(a == 0), stop=last)
            if last:
                tsum = p.op("t", fn)
            else:
                p.raw("t", fn)
            self.tk[("sqslot", a % 4)] = ("pe", p.cnt["pe"])
        p.wait("v", tsum)
        p.waits("v", tmax(self.tk.get("lnrows_s"), self.tk.get("rows_dump")))
        p.op("v", lambda e: e.tensor_scalar(R(0), bs[0:1, 0:CHW], 1.0 / DM, None, op0=OP.mult))
        t2 = p.op("v", lambda e: e.tensor_scalar(R(1), bq[0:1, 0:CHW], 1.0 / DM, None, op0=OP.mult))
        p.wait("v", t2)
        p.op("v", lambda e: e.tensor_tensor(R(4), R(0), R(0), op=OP.mult))
        t3 = p.op("v", lambda e: e.tensor_tensor(R(1), R(4), R(1), op=OP.subtract))
        self.bank_guard(4, t3)
        self.bank_guard(6, t3)
        p.wait("s", t3)
        t4 = p.op("s", lambda e: e.activation(R(2), R(1), AF.Sqrt,
                                              bias=self.sb["c_eps5"][0:1, :], scale=-1.0))
        p.wait("v", t4)
        t5 = p.op("v", lambda e: e.reciprocal(R(2), R(2)))
        t6 = p.op("v", lambda e: e.scalar_tensor_tensor(R(3), R(0), -1.0, R(2),
                                                        op0=OP.mult, op1=OP.mult))
        p.wait("v", t5)
        p.waits("v", tmax(self.tk.get("lnrowsb_t")))
        h1 = p.op("v", lambda e: e.tensor_copy(Rb(0), R(2)))
        p.wait("v", h1)
        h2 = p.op("v", lambda e: e.tensor_tensor(Rb(1), R(2), Rb(0), op=OP.subtract))
        p.wait("v", t6)
        h3 = p.op("v", lambda e: e.tensor_copy(Rb(2), R(3)))
        p.wait("v", h3)
        h4 = p.op("v", lambda e: e.tensor_tensor(Rb(3), R(3), Rb(2), op=OP.subtract))
        orow = self.sb["onesrow"]
        self.bank_wait(5)
        p.waits("t", tmax(h2))
        p.raw("t", lambda e: e.matmul(bb[:, 0:CHW], orow[:], Rb(0), start=True, stop=False))
        p.raw("t", lambda e: e.matmul(bb[:, 0:CHW], orow[:], Rb(1), start=False, stop=True))
        self.bank_wait(4)
        p.waits("t", tmax(h4))
        p.raw("t", lambda e: e.matmul(bs[:, 0:CHW], orow[:], Rb(2), start=True, stop=False))
        tmm = p.op("t", lambda e: e.matmul(bs[:, 0:CHW], orow[:], Rb(3), start=False, stop=True))
        self.tk["lnrows_s"] = t4
        self.tk["lnrowsb_t"] = tmm
        lastc = None
        for a in range(NT):
            lastc = writer(a, tiles[a], bb[:, 0:CHW], bs[:, 0:CHW], tmm)
        self.bank_guard(4, lastc)
        self.bank_guard(5, lastc)
        return {"v": lastc, "s": sq_toks[-1], "t": tsum}

    def emit_outer_ln(self, src, dst):
        p = self.p
        stage = self.sb["stage"]
        for c in range(NCHUNK):
            par = c % 2
            lt = self.load_xs_chunk(src, c, par)
            tiles = [self.xs_slice(par, a, False) for a in range(NT)]

            def writer(a, tile, aB, bB, tmm, c=c, par=par):
                u = self.sb["scrf"][:, (a % 2) * CHW:(a % 2 + 1) * CHW]
                p.waits("v", tmax(tmm, self.tk.get(("scrfslot", a % 2)),
                                  self.tk.get(("stageslot", a % 2)), self.tk.get(("stageslot_s", a % 2))))
                p.op("v", lambda e, u=u, tile=tile, aB=aB: e.tensor_tensor(u, tile, aB, op=OP.mult))
                dstap = stage[:, (a % 2) * CHW:(a % 2 + 1) * CHW]
                tv = p.op("v", lambda e, dstap=dstap, u=u, bB=bB: e.tensor_tensor(dstap, u, bB, op=OP.add))
                self.tk[("scrfslot", a % 2)] = tv
                p.wait("y", tv)
                td = p.dma("y", "dq2",
                           self.dram[dst][a * 128:(a + 1) * 128, c * CHW:(c + 1) * CHW], dstap)
                self.final_toks.append(td)
                self.tk[("stageslot", a % 2)] = td
                return tv
            res = self.emit_ln(tiles, writer, in_toks=[lt] * NT)
            self.tk[("xsfree", par)] = [res["v"], res["s"], res["t"]]
            if self.debug:
                p.waits("y", tmax(res["v"], res["s"], res["t"]))
                tdump = p.dma("y", "dq3", self.dram["dbgrows"][c:c + 1, :], self.sb["rows"][0:1, :])
                self.final_toks.append(tdump)
                self.tk["rows_dump"] = tdump

    def emit_residual_combine(self, srcA, srcB, scaleB, dst, mamba_xn=False):
        p = self.p
        stage = self.sb["stage"]
        if mamba_xn:
            self.phase_barrier()
        for c in range(NCHUNK):
            par = c % 2
            bf_toks = []
            last_td = None
            for a in range(NT):
                ap_a = stage[:, (a % 2) * CHW:(a % 2 + 1) * CHW]
                ap_b = self.sb["outacc"][:, a * CHW:(a + 1) * CHW]
                p.waits("y", tmax(self.tk.get(("stageslot", a % 2)), self.tk.get(("stageslot_s", a % 2))))
                tA = p.dma("y", "dq2", ap_a,
                           self.dram[srcA][a * 128:(a + 1) * 128, c * CHW:(c + 1) * CHW])
                p.waits("y", tmax(self.tk.get(("oaslot", a))))
                tB = p.dma("y", "dq3", ap_b,
                           self.dram[srcB][a * 128:(a + 1) * 128, c * CHW:(c + 1) * CHW])
                p.waits("v", tmax(tA, tB))
                tv = p.op("v", lambda e, ap_a=ap_a, ap_b=ap_b: e.scalar_tensor_tensor(
                    ap_a, ap_b, scaleB, ap_a, op0=OP.mult, op1=OP.add))
                p.wait("y", tv)
                td = p.dma("y", "dq2",
                           self.dram[dst][a * 128:(a + 1) * 128, c * CHW:(c + 1) * CHW], ap_a)
                self.final_toks.append(td)
                last_td = td
                self.tk[("stageslot", a % 2)] = td
                self.tk[("oaslot", a)] = tv
                if mamba_xn:
                    xt = self.xs_slice(par, a, False)
                    p.waits("s", tmax(tv, *(self.tk.get(("xsfree", par)) or []),
                                      self.tk.get(("xstail", par))))
                    bft = p.op("s", lambda e, xt=xt, ap_a=ap_a: e.activation(xt, ap_a, AF.Copy))
                    bf_toks.append(bft)
                    self.tk[("stageslot_s", a % 2)] = bft
            if mamba_xn:
                tiles = [self.xs_slice(par, a, False) for a in range(NT)]

                def writer(a, tile, aB, bB, tmm, c=c):
                    u = self.sb["scrf"][:, (a % 2) * CHW:(a % 2 + 1) * CHW]
                    p.waits("v", tmax(tmm, self.tk.get(("scrfslot", a % 2))))
                    p.op("v", lambda e, u=u, tile=tile, aB=aB: e.tensor_tensor(u, tile, aB, op=OP.mult))
                    dstap = self.ar(Build.MM_XN + a * SEQ + c * CHW, CHW)
                    tv = p.op("v", lambda e, dstap=dstap, u=u, bB=bB: e.tensor_tensor(dstap, u, bB, op=OP.add))
                    self.tk[("scrfslot", a % 2)] = tv
                    return tv
                res = self.emit_ln(tiles, writer, in_toks=bf_toks)
                self.tk[("xsfree", par)] = [res["v"], res["s"], res["t"]]

    def emit_final_combine(self):
        self.emit_residual_combine("xres", "xstack", 0.5, "y", mamba_xn=False)

    def phase_barrier(self):
        p = self.p
        toks = [("pe", p.cnt["pe"]), ("ve", p.cnt["ve"]), ("se", p.cnt["se"]),
                ("ge", p.cnt["ge"])] + [(f"{q}_{i}", p.cnt[f"{q}_{i}"])
                                        for q in ["dq0", "dq1", "dq2", "dq3"] for i in range(3)]
        toks = [t for t in toks if t[1] > 0]
        for e in "tvsgy":
            p.waits(e, toks)

    def emit_attn_sub(self, l):
        p = self.p
        ar = self.ar
        B = Build
        sel2 = self.sb["sel2"]
        scr = self.sb["scr"]
        attnw = self.sb["attnw"]
        rnr = self.sb["rows"]
        rnrb = self.sb["rowsb"]
        self.phase_barrier()
        vones_ap = ar(B.AT_VR, 8 * 1056).rearrange("p (s h c) -> p s h c", h=16, c=66)[:, :, :, 64:65]
        tvones = p.op("v", lambda e: e.memset(vones_ap, 1.0))
        wq_src = self.dram["wqkv"][l].rearrange("(kt p) o -> p kt o", p=128)
        twq = p.dma("g", "dq1", ar(B.AT_WQKV, 24576).rearrange("p (kt o) -> p kt o", o=3072), wq_src)
        wo_toks = []
        for kt in range(NT):
            wo_toks.append(p.dma("g", "dq1", ar(B.AT_WO + kt * 1024, 1024),
                                 self.dram["wo"][l][kt * 128:(kt + 1) * 128, :]))
        qs_col = self.sb["qs_cols"][:, l:l + 1]
        ks_col = self.sb["ks_cols"][:, l:l + 1]
        bstep = [0]

        def prep_A(c, par, pair, is_q, xs_tok):
            bk = bstep[0] % 2
            bank = self.ps[bk]
            bstep[0] += 1
            ot = (0 if is_q else 8) + pair
            self.bank_wait(bk)
            p.waits("t", tmax(twq, xs_tok))
            if is_q:
                dst = ar(B.AT_Q + pair * CHW, CHW)
                p.waits("v", tmax(self.tk.get(("qslot", pair))))
            else:
                dst = ar(B.AT_KR + (c % 2) * 4096 + pair * CHW, CHW)
                p.waits("v", tmax(self.tk.get(("kslot", c % 2, pair))))
            for kt in range(NT):
                rhs = self.xs_slice(par, kt, kt >= 4)
                lhsT = ar(B.AT_WQKV + kt * 3072 + ot * 128, 128)
                fn = lambda e, lhsT=lhsT, rhs=rhs, kt=kt: e.matmul(
                    bank[:, 0:CHW], lhsT, rhs, start=(kt == 0), stop=(kt == NT - 1))
                if kt == NT - 1:
                    tmm = p.op("t", fn)
                else:
                    p.raw("t", fn)
            sA = scr[:, (bk * 2) * CHW:(bk * 2 + 1) * CHW]
            sB = scr[:, (bk * 2 + 1) * CHW:(bk * 2 + 2) * CHW]
            p.waits("s", tmax(tmm, self.tk.get(("scrA", bk)), self.tk.get(("scrB", bk))))
            t_raw = p.op("s", lambda e: e.activation(sA, bank[:, 0:CHW], AF.Copy))
            t_sq = p.op("s", lambda e: e.activation(sB, bank[:, 0:CHW], AF.Square))
            p.waits("t", tmax(t_raw, t_sq))
            p.raw("t", lambda e: e.matmul(bank[0:1, 0:CHW], sel2[:, 0:1], sB, start=True, stop=True))
            tnsq = p.op("t", lambda e: e.matmul(bank[32:33, 0:CHW], sel2[:, 1:2], sB,
                                                start=True, stop=True, tile_position=(0, 32)))
            return (c, bk, bank, sA, sB, dst, tnsq, pair, is_q)

        def prep_B(st, tab_tok):
            c, bk, bank, sA, sB, dst, tnsq, pair, is_q = st
            RA = rnr[0:1, 0:CHW]
            RB = rnr[0:1, CHW:2 * CHW]
            p.waits("s", tmax(tnsq, self.tk.get("rnfree")))
            s1 = p.op("s", lambda e: e.activation(RA, bank[0:1, 0:CHW], AF.Sqrt,
                                                  bias=self.sb["c_eps24"][0:1, :]))
            s2 = p.op("s", lambda e: e.activation(RB, bank[32:33, 0:CHW], AF.Sqrt,
                                                  bias=self.sb["c_eps24"][0:1, :]))
            p.waits("v", tmax(s1, s2, self.tk.get("rnbfree")))
            v1 = p.op("v", lambda e: e.reciprocal(RA, RA))
            v2 = p.op("v", lambda e: e.reciprocal(RB, RB))
            Hb = lambda i: rnrb[0:1, i * CHW:(i + 1) * CHW]
            p.wait("v", v1)
            h1 = p.op("v", lambda e: e.tensor_copy(Hb(0), RA))
            p.wait("v", h1)
            h2 = p.op("v", lambda e: e.tensor_tensor(Hb(1), RA, Hb(0), op=OP.subtract))
            p.wait("v", v2)
            h3 = p.op("v", lambda e: e.tensor_copy(Hb(2), RB))
            p.wait("v", h3)
            h4 = p.op("v", lambda e: e.tensor_tensor(Hb(3), RB, Hb(2), op=OP.subtract))
            self.tk["rnfree"] = h4
            p.waits("t", tmax(h2, h4))
            selA, selB = self.sb["selA"], self.sb["selB"]
            p.raw("t", lambda e: e.matmul(bank[:, 0:CHW], selA[:], Hb(0), start=True, stop=False))
            p.raw("t", lambda e: e.matmul(bank[:, 0:CHW], selA[:], Hb(1), start=False, stop=False))
            p.raw("t", lambda e: e.matmul(bank[:, 0:CHW], selB[:], Hb(2), start=False, stop=False))
            trn = p.op("t", lambda e: e.matmul(bank[:, 0:CHW], selB[:], Hb(3), start=False, stop=True))
            self.tk["rnbfree"] = trn
            scol = qs_col if is_q else ks_col
            p.waits("v", tmax(trn))
            vq1 = p.op("v", lambda e: e.scalar_tensor_tensor(sB, sA, scol, bank[:, 0:CHW],
                                                             op0=OP.mult, op1=OP.mult))
            p.waits("t", tmax(vq1))
            tq1r = p.op("t", lambda e: e.matmul(bank[:, 0:CHW], self.sb["rot"][:], sB,
                                                start=True, stop=True))
            tb = B.AT_TAB
            cs = ar(tb + (0 if is_q else 2) * CHW, CHW)
            ss = ar(tb + (1 if is_q else 3) * CHW, CHW)
            p.waits("v", tmax(tq1r, *tab_tok) if isinstance(tab_tok, list) else tmax(tq1r, tab_tok))
            vt2 = p.op("v", lambda e: e.tensor_tensor(sA, bank[:, 0:CHW], ss, op=OP.mult))
            vt1 = p.op("v", lambda e: e.tensor_tensor(sB, sB, cs, op=OP.mult))
            p.waits("v", tmax(vt1, vt2))
            vd = p.op("v", lambda e: e.tensor_tensor(dst, sB, sA, op=OP.add))
            self.tk[("scrA", bk)] = vd
            self.tk[("scrB", bk)] = vd
            self.bank_guard(bk, vt2)
            if is_q:
                self.tk[("qprep", pair)] = vd
            else:
                self.tk[("kprep", c % 2, pair)] = vd
            return vd

        def prep_tiles(c, par, xs_tok, tab_tok):
            plist = [(pair, True) for pair in range(NPAIR)] + \
                    [(pair, False) for pair in range(NPAIR)]
            pend = None
            for pair, is_q in plist:
                stA = prep_A(c, par, pair, is_q, xs_tok)
                if pend is not None:
                    prep_B(pend, tab_tok)
                pend = stA
            prep_B(pend, tab_tok)

        def v_tiles(c, par, xs_tok):
            for tt in range(4):
                for vc in range(2):
                    v_one(c, par, xs_tok, tt, vc)

        def v_one(c, par, xs_tok, tt, vc):
                    jt = 4 * c + tt
                    slot = jt % 8
                    bk = bstep[0] % 2
                    bank = self.ps[bk]
                    bstep[0] += 1
                    self.bank_wait(bk)
                    p.waits("t", tmax(twq, xs_tok))
                    for kt in range(NT):
                        lhsT = self.xs_slice(par, kt, kt >= 4)[:, tt * 128:(tt + 1) * 128]
                        rhs = ar(B.AT_WQKV + kt * 3072 + 2048 + vc * CHW, CHW)
                        fn = lambda e, bank=bank, lhsT=lhsT, rhs=rhs, kt=kt: e.matmul(
                            bank[:, 0:CHW], lhsT, rhs, start=(kt == 0), stop=(kt == NT - 1))
                        if kt == NT - 1:
                            tmm = p.op("t", fn)
                        else:
                            p.raw("t", fn)
                    dst = ar(B.AT_VR + slot * 1056 + vc * 8 * 66, 8 * 66).rearrange(
                        "p (h c) -> p h c", c=66)[:, :, 0:64]
                    p.waits("v", tmax(tmm, tvones, self.tk.get(("vslot", slot, vc))))
                    vev = p.op("v", lambda e, dst=dst, bank=bank: e.tensor_copy(
                        dst, bank[:, 0:CHW].rearrange("p (h c) -> p h c", c=64)))
                    self.bank_guard(bk, vev)
                    self.tk[("vdata", jt, vc)] = vev

        def attention(c):
            for pair in range(NPAIR):
                attn_pair(c, pair)

        def attn_pair(c, pair):
            Jlist = [J for J in range(4 * c - 2, 4 * c + 4) if J >= 0]
            nJ = len(Jlist)
            state = {}
            tavs = {}

            def emit_sim(pr, idx):
                J = Jlist[idx]
                b = 2 + pr
                simb = self.ps[b]
                rpar = (J // 4) % 2
                r = J - 4 * c + 2
                b0 = max(0, r - 2) * 128
                self.bank_wait(b)
                p.waits("t", tmax(self.tk.get(("qprep", pair)), self.tk.get(("kprep", rpar, pair))))
                lhsT = ar(B.AT_KR + rpar * 4096 + pair * CHW + (J % 4) * 128, 128)[pr * 64: pr * 64 + 64, :]
                rhs = ar(B.AT_Q + pair * CHW, CHW)[pr * 64: pr * 64 + 64, b0:CHW]
                tsim = p.op("t", lambda e, simb=simb, lhsT=lhsT, rhs=rhs, pr=pr, b0=b0: e.matmul(
                    simb[:, b0:CHW], lhsT, rhs, start=True, stop=True, tile_position=(pr * 64, 0)))
                self.tk[("kslot", rpar, pair)] = tsim
                self.tk[("qslot", pair)] = tsim
                slot = (idx % 2) * 2 + pr
                aw = attnw[:, slot * CHW:(slot + 1) * CHW]
                p.waits("s", tmax(tsim, self.tk.get(("awfree", slot))))
                texp = p.op("s", lambda e, aw=aw, simb=simb, b0=b0: e.activation(
                    aw[:, b0:CHW], simb[:, b0:CHW], AF.Exp, bias=self.sb["c_neg30"][:, 0:1], scale=8.0))
                self.bank_guard(b, texp)
                msk = self.sb["amask"][:, (5 - r) * 128 + b0:(5 - r) * 128 + CHW]
                p.waits("v", tmax(texp))
                tmsk = p.op("v", lambda e, aw=aw, msk=msk, b0=b0: e.tensor_tensor(
                    aw[:, b0:CHW], aw[:, b0:CHW], msk, op=OP.mult))
                state[(pr, idx)] = (aw, slot, tmsk, J, b0)

            def emit_av(pr, idx):
                aw, slot, tmsk, J, b0 = state[(pr, idx)]
                first, last = idx == 0, idx == nJ - 1
                if first:
                    self.bank_wait(4 + pr)
                p.waits("t", tmax(tmsk, *[self.tk.get(("vdata", J, vc)) for vc in range(2)]))
                h = 2 * pair + pr
                outb = self.ps[4 + pr]
                lhsT = ar(B.AT_VR + (J % 8) * 1056 + h * 66, 65)
                tav = p.op("t", lambda e, outb=outb, lhsT=lhsT, aw=aw, first=first, last=last, b0=b0: e.matmul(
                    outb[0:65, b0:CHW], lhsT, aw[:, b0:CHW], start=first, stop=last))
                self.tk[("awfree", slot)] = tav
                for vc in range(2):
                    self.tk[("vslot", J % 8, vc)] = tav
                tavs[pr] = tav

            for idx in range(nJ + 1):
                for pr in range(2):
                    if idx < nJ:
                        emit_sim(pr, idx)
                for pr in range(2):
                    if idx >= 1:
                        emit_av(pr, idx - 1)
            for pr in range(2):
                normalize(c, pair, pr, tavs[pr])

        def normalize(c, pair, pr, tav):
            outb = self.ps[4 + pr]
            srecip = self.sb["srecip"]
            rf = self.sb["rows"][0:1, (2 + pr) * CHW:(3 + pr) * CHW]
            p.waits("v", tmax(tav, self.tk.get("srfree")))
            vr = p.op("v", lambda e, rf=rf, outb=outb: e.reciprocal(rf, outb[64:65, 0:CHW]))
            p.wait("v", vr)
            vh = p.op("v", lambda e, rf=rf: e.tensor_copy(srecip[0:1, 0:CHW], rf))
            p.wait("v", vh)
            vlo = p.op("v", lambda e, rf=rf: e.tensor_tensor(srecip[0:1, CHW:2 * CHW], rf,
                                                             srecip[0:1, 0:CHW], op=OP.subtract))
            self.bank_wait(6)
            p.waits("t", tmax(vh, vlo))
            p.raw("t", lambda e: e.matmul(self.ps[6][0:64, 0:CHW], self.sb["onesrow"][0:1, 0:64],
                                          srecip[0:1, 0:CHW], start=True, stop=False))
            trb = p.op("t", lambda e: e.matmul(self.ps[6][0:64, 0:CHW], self.sb["onesrow"][0:1, 0:64],
                                               srecip[0:1, CHW:2 * CHW], start=False, stop=True))
            self.tk["srfree"] = trb
            recS = self.sb["scrf"][0:64, pr * CHW:(pr + 1) * CHW]
            p.waits("v", tmax(trb, self.tk.get(("recSfree", pr))))
            vcp = p.op("v", lambda e, recS=recS: e.tensor_copy(recS, self.ps[6][0:64, 0:CHW]))
            self.bank_guard(6, vcp)
            dst = ar(B.AT_AO + pair * CHW, CHW)[pr * 64: pr * 64 + 64, :]
            p.waits("v", tmax(self.tk.get("aoutfree")))
            vout = p.op("v", lambda e, dst=dst, outb=outb, recS=recS: e.tensor_tensor(
                dst, outb[0:64, 0:CHW], recS, op=OP.mult))
            self.bank_guard(4 + pr, vout)
            self.tk[("recSfree", pr)] = vout

        def wo_chunk(c):
            tlds = []
            for oc in range(NT):
                old = self.sb["outacc"][:, oc * CHW:(oc + 1) * CHW]
                p.waits("y", tmax(self.tk.get(("oaslot", oc))))
                tlds.append(p.dma("y", "dq2", old,
                                  self.dram["xstack"][oc * 128:(oc + 1) * 128, c * CHW:(c + 1) * CHW]))
            for oc in range(NT):
                wo_one(c, oc, tlds[oc])

        def wo_one(c, oc, tld):
                stage = self.sb["stage"]
                bk = bstep[0] % 2
                bank = self.ps[bk]
                bstep[0] += 1
                self.bank_wait(bk)
                p.waits("t", tmax(wo_toks[0], wo_toks[-1]))
                p.waits("t", tmax(self.tk.get(("aout_done", c))))
                for kt in range(NT):
                    lhsT = ar(B.AT_WO + kt * 1024 + oc * 128, 128)
                    rhs = ar(B.AT_AO + kt * CHW, CHW)
                    fn = lambda e, lhsT=lhsT, rhs=rhs, kt=kt: e.matmul(
                        bank[:, 0:CHW], lhsT, rhs, start=(kt == 0), stop=(kt == NT - 1))
                    if kt == NT - 1:
                        two = p.op("t", fn)
                    else:
                        p.raw("t", fn)
                self.tk["aoutfree"] = two
                old = self.sb["outacc"][:, oc * CHW:(oc + 1) * CHW]
                st = stage[:, (oc % 2) * CHW:(oc % 2 + 1) * CHW]
                p.waits("v", tmax(two, tld, self.tk.get(("stageslot", oc % 2))))
                tv = p.op("v", lambda e: e.tensor_tensor(st, bank[:, 0:CHW], old, op=OP.add))
                self.bank_guard(bk, tv)
                self.tk[("oaslot", oc)] = tv
                p.wait("y", tv)
                td = p.dma("y", "dq2",
                           self.dram["xstack"][oc * 128:(oc + 1) * 128, c * CHW:(c + 1) * CHW], st)
                self.tk[("stageslot", oc % 2)] = td
                self.tk[("xstack_w", c)] = td
                self.final_toks.append(td)

        for c in range(NCHUNK):
            par = c % 2
            xs_tok = self.load_xs_chunk("xstack", c, par)
            tabs = []
            tb = B.AT_TAB
            p.waits("g", tmax(self.tk.get("tabfree")))
            for i, nm in enumerate(["cs_q", "ss_q", "cs_k", "ss_k"]):
                tabs.append(p.dma("g", "dq1", ar(tb + i * CHW, CHW),
                                  self.dram[nm][:, c * CHW:(c + 1) * CHW]))
            tab_tok = tabs
            prep_tiles(c, par, xs_tok, tab_tok)
            self.tk["tabfree"] = ("pe", p.cnt["pe"])
            self.tk["tabfree"] = ("ve", p.cnt["ve"])
            v_tiles(c, par, xs_tok)
            self.tk[("xsfree", par)] = [("pe", p.cnt["pe"]), ("se", p.cnt["se"])]
            attention(c)
            self.tk[("aout_done", c)] = ("ve", p.cnt["ve"])
            wo_chunk(c)

    def emit_ffn_sub(self, l, ln_only=False, upto="all"):
        p = self.p
        ar = self.ar
        B = Build
        scr = self.sb["scr"]
        stage = self.sb["stage"]
        if upto == "nothing":
            return
        if upto == "barrier2":
            self.p.waits("t", [("ve", self.p.cnt["ve"])])
            return
        if upto == "barrier3":
            self.p.waits("v", [("pe", self.p.cnt["pe"])])
            return
        if upto == "barrier4":
            self.p.waits("y", [("pe", self.p.cnt["pe"]), ("ve", self.p.cnt["ve"]), ("se", self.p.cnt["se"])])
            return
        if upto == "barrier5":
            for q in ["dq0", "dq1", "dq2", "dq3"]:
                self.p.waits("t", [(q, self.p.cnt[q])])
            return
        self.phase_barrier()
        if upto == "barrier":
            return
        xn_toks = {}
        for c in range(NCHUNK):
            par = c % 2
            lt = self.load_xs_chunk("xstack", c, par)
            if upto == "loads":
                xn_toks[c] = lt
                continue
            tiles = [self.xs_slice(par, a, a >= 4) for a in range(NT)]
            ctoks = []

            def writer(a, tile, aB, bB, tmm, c=c, ctoks=ctoks):
                u = self.sb["scrf"][:, (a % 2) * CHW:(a % 2 + 1) * CHW]
                p.waits("v", tmax(tmm, self.tk.get(("scrfslot", a % 2))))
                p.op("v", lambda e, u=u, tile=tile, aB=aB: e.tensor_tensor(u, tile, aB, op=OP.mult))
                p.op("v", lambda e, u=u, bB=bB: e.tensor_tensor(u, u, bB, op=OP.add))
                g_col = self.sb["lng_cols"][:, l * 8 + a: l * 8 + a + 1]
                b_col = self.sb["lnb_cols"][:, l * 8 + a: l * 8 + a + 1]
                dstap = ar(B.FF_XN + a * SEQ + c * CHW, CHW)
                tv = p.op("v", lambda e, dstap=dstap, u=u, g_col=g_col, b_col=b_col: e.tensor_scalar(
                    dstap, u, g_col, b_col, op0=OP.mult, op1=OP.add))
                self.tk[("scrfslot", a % 2)] = tv
                ctoks.append(tv)
                return tv
            res = self.emit_ln(tiles, writer, in_toks=[lt] * NT)
            self.tk[("xsfree", par)] = [res["v"], res["s"], res["t"]]
            xn_toks[c] = tmax(*ctoks)[0]
        if self.debug:
            p.waits("g", tmax(*[xn_toks[c] for c in range(NCHUNK)]))
            self.final_toks.append(p.dma("g", "dq1", self.dram["dbga"][:, 0:8192], ar(B.FF_XN, 8192)))
            self.final_toks.append(p.dma("g", "dq1", self.dram["dbga"][:, 8192:8704],
                                         self.xs_slice(0, 0, False)))
            self.final_toks.append(p.dma("g", "dq1", self.dram["dbga"][0:1, 8704:8704 + 2048],
                                         self.sb["rowsb"][0:1, 0:4 * CHW]))
            self.final_toks.append(p.dma("g", "dq1", self.dram["dbga"][0:1, 11000:11000 + 2048],
                                         self.sb["rows"][0:1, 0:4 * CHW]))
        if ln_only:
            return
        bstep = [0]
        for hf in range(2):
            if hf == 1:
                p.waits("g", tmax(("pe", p.cnt["pe"]), ("se", p.cnt["se"]), ("ve", p.cnt["ve"])))
            tw1a = p.dma("g", "dq1",
                         ar(B.FF_W1A, 11264).rearrange("p (ki o) -> p ki o", o=1408),
                         self.dram["w1a"][l].rearrange("(ki p) o -> p ki o", p=128)[:, :, hf * 1408:(hf + 1) * 1408])
            tw1g = p.dma("g", "dq1",
                         ar(B.FF_W1G, 11264).rearrange("p (ki o) -> p ki o", o=1408),
                         self.dram["w1g"][l].rearrange("(ki p) o -> p ki o", p=128)[:, :, hf * 1408:(hf + 1) * 1408])
            tw2 = p.dma("g", "dq1",
                        ar(B.FF_W2, 11264).rearrange("p (kk o) -> p kk o", o=1024),
                        self.dram["w2"][l][hf * 1408:(hf + 1) * 1408, :].rearrange("(kk p) o -> p kk o", p=128))
            for c in range(NCHUNK):
                h2_toks = []
                for k in range(KH):
                    h2_toks.append(self._ffn_h2(l, hf, c, k, tw1a, tw1g, xn_toks[c]))
                tlds = []
                for oc in range(NT):
                    old = self.sb["outacc"][:, oc * CHW:(oc + 1) * CHW]
                    p.waits("y", tmax(self.tk.get(("oaslot", oc)), self.tk.get(("xstack_w", c))))
                    tlds.append(p.dma("y", "dq2", old,
                                      self.dram["xstack"][oc * 128:(oc + 1) * 128, c * CHW:(c + 1) * CHW]))
                for oc in range(NT):
                    self._ffn_w2(l, hf, c, oc, tw2, h2_toks, bstep, tlds[oc])

    def _ffn_h2(self, l, hf, c, k, tw1a, tw1g, xn_tok):
        p = self.p
        ar = self.ar
        B = Build
        scr = self.sb["scr"]
        ba, bg = self.ps[(k % 2) * 2], self.ps[(k % 2) * 2 + 1]
        self.bank_wait((k % 2) * 2)
        p.waits("t", tmax(tw1a, xn_tok))
        for kt in range(NT):
            lhsT = ar(B.FF_W1A + kt * 1408 + k * 128, 128)
            rhs = ar(B.FF_XN + kt * SEQ + c * CHW, CHW)
            fn = lambda e, lhsT=lhsT, rhs=rhs, kt=kt: e.matmul(
                ba[:, 0:CHW], lhsT, rhs, start=(kt == 0), stop=(kt == NT - 1))
            if kt == NT - 1:
                tma = p.op("t", fn)
            else:
                p.raw("t", fn)
        self.bank_wait((k % 2) * 2 + 1)
        p.waits("t", tmax(tw1g))
        for kt in range(NT):
            lhsT = ar(B.FF_W1G + kt * 1408 + k * 128, 128)
            rhs = ar(B.FF_XN + kt * SEQ + c * CHW, CHW)
            fn = lambda e, lhsT=lhsT, rhs=rhs, kt=kt: e.matmul(
                bg[:, 0:CHW], lhsT, rhs, start=(kt == 0), stop=(kt == NT - 1))
            if kt == NT - 1:
                tmg = p.op("t", fn)
            else:
                p.raw("t", fn)
        sl = scr[:, (k % 2) * CHW:(k % 2 + 1) * CHW]
        b1g_col = self.sb["b1g_cols"][:, l * 22 + hf * 11 + k: l * 22 + hf * 11 + k + 1]
        p.waits("s", tmax(tmg, self.tk.get(("ffnscr", k % 2)), self.tk.get(("sqslot", k % 2))))
        tgel = p.op("s", lambda e, sl=sl, b1g_col=b1g_col: e.activation(
            sl, bg[:, 0:CHW], AF.Gelu, bias=b1g_col))
        self.bank_guard((k % 2) * 2 + 1, tgel)
        b1a_col = self.sb["b1a_cols"][:, l * 22 + hf * 11 + k: l * 22 + hf * 11 + k + 1]
        dst = ar(B.FF_H2 + k * CHW, CHW)
        p.waits("v", tmax(tma, tgel, self.tk.get(("h2slot", k))))
        th2 = p.op("v", lambda e, dst=dst, sl=sl, b1a_col=b1a_col: e.scalar_tensor_tensor(
            dst, ba[:, 0:CHW], b1a_col, sl, op0=OP.add, op1=OP.mult))
        self.bank_guard((k % 2) * 2, th2)
        self.tk[("ffnscr", k % 2)] = th2
        self.tk[("sqslot", k % 2)] = th2
        return th2

    def _ffn_w2(self, l, hf, c, oc, tw2, h2_toks, bstep, tld):
        p = self.p
        ar = self.ar
        B = Build
        stage = self.sb["stage"]
        bk = 2 + bstep[0] % 2
        bstep[0] += 1
        bank = self.ps[bk]
        self.bank_wait(bk)
        p.waits("t", tmax(tw2, *h2_toks))
        for k in range(KH):
            lhsT = ar(B.FF_W2 + k * 1024 + oc * 128, 128)
            rhs = ar(B.FF_H2 + k * CHW, CHW)
            fn = lambda e, lhsT=lhsT, rhs=rhs, k=k: e.matmul(
                bank[:, 0:CHW], lhsT, rhs, start=(k == 0), stop=(k == KH - 1))
            if k == KH - 1:
                tmm = p.op("t", fn)
            else:
                p.raw("t", fn)
        for k in range(KH):
            self.tk[("h2slot", k)] = tmm
        old = self.sb["outacc"][:, oc * CHW:(oc + 1) * CHW]
        st = stage[:, (oc % 2) * CHW:(oc % 2 + 1) * CHW]
        p.waits("v", tmax(tmm, tld, self.tk.get(("stageslot", oc % 2))))
        if hf == 0:
            tv = p.op("v", lambda e, st=st, bank=bank, old=old: e.tensor_tensor(
                st, bank[:, 0:CHW], old, op=OP.add))
        else:
            b2_col = self.sb["b2_cols"][:, l * 8 + oc: l * 8 + oc + 1]
            tv = p.op("v", lambda e, st=st, bank=bank, old=old, b2_col=b2_col: e.scalar_tensor_tensor(
                st, bank[:, 0:CHW], b2_col, old, op0=OP.add, op1=OP.add))
        self.bank_guard(bk, tv)
        self.tk[("oaslot", oc)] = tv
        p.wait("y", tv)
        td = p.dma("y", "dq2",
                   self.dram["xstack"][oc * 128:(oc + 1) * 128, c * CHW:(c + 1) * CHW], st)
        self.tk[("stageslot", oc % 2)] = td
        self.tk[("xstack_w", c)] = td
        self.final_toks.append(td)

    def emit_mamba(self, upto="all"):
        p = self.p
        ar = self.ar
        B = Build
        self.phase_barrier()
        mscr = lambda i: ar(B.MM_SCR + i * MTB, MTB)
        mscrf = self.sb["mscrf"]
        psb = self.ps[7]
        ident = self.sb["ident"]
        twdt = p.dma("g", "dq1", ar(B.MM_WDT, 256).rearrange("p (kt o) -> p kt o", o=32),
                     self.dram["winT"][:, 8192:8224].rearrange("(kt p) o -> p kt o", p=128))
        ntb = 1 if upto.endswith("1") else 8
        upto = upto[:-1] if upto.endswith("1") else upto
        for tb in range(ntb):
            self._mamba_dt(tb, twdt)
            if upto == "dt":
                continue
            for ti in range(NTI):
                self._mamba_ti(tb, ti, upto)
            self.tk["mDfree"] = ("pe", self.p.cnt["pe"])
            if upto in ("nowout", "proj", "noscan"):
                continue
            self._mamba_wout(tb)

    def _mamba_dt(self, tb, twdt):
        p = self.p
        ar = self.ar
        B = Build
        bk = self.ps[2]
        mscrf = self.sb["mscrf"]
        self.bank_wait(2)
        p.waits("t", tmax(twdt, self.tk.get("mxn")))
        for kt in range(NT):
            lhsT = ar(B.MM_WDT + kt * 32, 32)
            rhs = ar(B.MM_XN + kt * SEQ + tb * MTB, MTB)
            fn = lambda e, lhsT=lhsT, rhs=rhs, kt=kt: e.matmul(
                bk[0:32, 0:MTB], lhsT, rhs, start=(kt == 0), stop=(kt == NT - 1))
            if kt == NT - 1:
                tdt = p.op("t", fn)
            else:
                p.raw("t", fn)
        U = mscrf[0:32, 0:MTB]
        AB = self.sb["mDrev"][:, :]
        EE = self.sb["mDl"][:, :]
        p.waits("v", tmax(tdt, self.tk.get("mscrf_w"), self.tk.get("mscrf_rs"), self.tk.get("mDfree")))
        t1 = p.op("v", lambda e: e.tensor_scalar(U, bk[0:32, 0:MTB], self.sb["dtb_col"][:], None, op0=OP.add))
        self.bank_guard(2, t1)
        p.waits("s", tmax(t1, self.tk.get("mDfree")))
        t2 = p.op("s", lambda e: e.activation(AB, U, AF.Abs))
        p.waits("s", tmax(t2))
        t3 = p.op("s", lambda e: e.activation(EE, AB, AF.Exp, scale=-1.0))
        p.waits("s", tmax(t3))
        t4 = p.op("s", lambda e: e.activation(AB, EE, AF.Ln, bias=self.sb["c_one"][0:32, :]))
        p.waits("v", tmax(t4))
        t5 = p.op("v", lambda e: e.tensor_scalar(EE, U, 0.0, None, op0=OP.max))
        mdt = self.sb["mdt"]
        p.waits("v", tmax(t5))
        t6 = p.op("v", lambda e: e.tensor_tensor(mdt[:, :], EE, AB, op=OP.add))
        self.tk["mscrf_w"] = t1
        t7 = p.op("v", lambda e: e.tensor_copy(self.sb["mdtb"][:, :], mdt[:, :]))
        mDl = self.sb["mDl"]
        t8 = p.op("v", lambda e: e.tensor_tensor_scan(mDl[:, :], self.sb["scanmask"][:, :], mdt[:, :],
                                                      0.0, op0=OP.mult, op1=OP.add))
        p.waits("v", tmax(t8))
        mDtot = self.sb["mDtot"]
        for cc in range(4):
            p.op("v", lambda e, cc=cc: e.tensor_copy(mDtot[:, cc:cc + 1], mDl[:, cc * MCH + 63:cc * MCH + 64]))
        mDrev = self.sb["mDrev"]
        for cc in range(4):
            p.op("v", lambda e, cc=cc: e.tensor_scalar(
                mDrev[:, cc * MCH:(cc + 1) * MCH], mDl[:, cc * MCH:(cc + 1) * MCH],
                mDtot[:, cc:cc + 1], None, op0=OP.subtract))
        mDtotb = self.sb["mDtotb"]
        p.op("v", lambda e: e.tensor_copy(mDtotb[:, 0:4], mDtot[:, :]))
        p.op("v", lambda e: e.tensor_tensor(mDtotb[:, 4:8], mDtot[:, :], mDtotb[:, 0:4], op=OP.subtract))
        hl = self.sb["mDlhl"]
        p.op("v", lambda e: e.tensor_copy(hl[:, 0:MTB], mDl[:, :]))
        p.op("v", lambda e: e.tensor_tensor(hl[:, MTB:2 * MTB], mDl[:, :], hl[:, 0:MTB], op=OP.subtract))
        hr = self.sb["mDrevhl"]
        p.op("v", lambda e: e.tensor_copy(hr[:, 0:MTB], mDrev[:, :]))
        tlast = p.op("v", lambda e: e.tensor_tensor(hr[:, MTB:2 * MTB], mDrev[:, :], hr[:, 0:MTB], op=OP.subtract))
        self.tk["mdt_done"] = tlast

    def _mamba_ti(self, tb, ti, upto="all"):
        p = self.p
        ar = self.ar
        B = Build
        sel = self.sb["selscan"][:, ti * 128:(ti + 1) * 128]
        psb = self.ps[7]
        frees = tmax(self.tk.get("mscr_free_v"), self.tk.get("mscr_free_t"))
        for eng in "tvs":
            p.waits(eng, frees)
        pp = ti % 2
        prev = self.tk.get(("mwin", pp))
        if prev is not None:
            p.waits("g", tmax(prev))
        twin = p.dma("g", "dq1",
                     ar(B.MM_WIN + pp * 4096, 4096).rearrange("p (kt o) -> p kt o", o=512),
                     self.dram["winT"][:, ti * 512:(ti + 1) * 512].rearrange("(kt p) o -> p kt o", p=128))
        def proj(colofs, bank):
            self.bank_wait(bank)
            bk = self.ps[bank]
            p.waits("t", tmax(twin, self.tk.get("mxn")))
            for kt in range(NT):
                lhsT = ar(B.MM_WIN + pp * 4096 + kt * 512 + colofs, 128)
                rhs = ar(B.MM_XN + kt * SEQ + tb * MTB, MTB)
                fn = lambda e, bk=bk, lhsT=lhsT, rhs=rhs, kt=kt: e.matmul(
                    bk[:, 0:MTB], lhsT, rhs, start=(kt == 0), stop=(kt == NT - 1))
                if kt == NT - 1:
                    tmm = p.op("t", fn)
                else:
                    p.raw("t", fn)
            return tmm
        t_xin = proj(0, 0)
        xin = ar(B.MM_XIN + ti * (MTB + 3), MTB + 3)
        p.waits("v", tmax(t_xin, self.tk.get(("mxin", ti))))
        v1 = p.op("v", lambda e: e.tensor_copy(xin[:, 0:3], self.sb["xtails"][:, ti * 3:(ti + 1) * 3]))
        p.waits("v", tmax(v1))
        v2 = p.op("v", lambda e: e.tensor_copy(xin[:, 3:MTB + 3], self.ps[0][:, 0:MTB]))
        self.bank_guard(0, v2)
        v3 = p.op("v", lambda e: e.tensor_copy(self.sb["xtails"][:, ti * 3:(ti + 1) * 3], xin[:, MTB:MTB + 3]))
        t_B = proj(128, 1)
        p.waits("v", tmax(t_B))
        vB = p.op("v", lambda e: e.tensor_copy(ar(B.MM_SCR + 0 * MTB, MTB), self.ps[1][:, 0:MTB]))
        self.bank_guard(1, vB)
        t_C = proj(256, 0)
        p.waits("v", tmax(t_C))
        vC = p.op("v", lambda e: e.tensor_copy(ar(B.MM_SCR + 1 * MTB, MTB), self.ps[0][:, 0:MTB]))
        self.bank_guard(0, vC)
        mscrf = self.sb["mscrf"]
        acc = mscrf[:, 0:MTB]
        w0 = self.sb["convw_cols"][:, ti * 4:ti * 4 + 1]
        cb = self.sb["convb_cols"][:, ti:ti + 1]
        p.waits("v", tmax(v2, v3, self.tk.get("mscrf_w"), self.tk.get("mscrf_rs")))
        c1 = p.op("v", lambda e: e.tensor_scalar(acc, xin[:, 0:MTB], w0, cb, op0=OP.mult, op1=OP.add))
        prevc = c1
        for k in range(1, 4):
            wk = self.sb["convw_cols"][:, ti * 4 + k:ti * 4 + k + 1]
            p.waits("v", tmax(prevc))
            prevc = p.op("v", lambda e, wk=wk, k=k: e.scalar_tensor_tensor(
                acc, xin[:, k:k + MTB], wk, acc, op0=OP.mult, op1=OP.add))
        self.tk["mscrf_w"] = prevc
        p.waits("s", tmax(prevc))
        tsil = p.op("s", lambda e: e.activation(ar(B.MM_SCR + 2 * MTB, MTB), acc, AF.Silu))
        self.tk["mscrf_rs"] = tsil
        xh_cm = ar(B.MM_SCR + 2 * MTB, MTB)
        ev_toks = []
        for half in range(2):
            for c4 in range(4):
                self.bank_wait(7)
                p.waits("t", tmax(tsil))
                ttr = p.op("t", lambda e, half=half, c4=c4: e.transpose(
                    psb[half * 64:half * 64 + 64, c4 * 128:(c4 + 1) * 128],
                    xh_cm[:, c4 * 64:(c4 + 1) * 64], self.sb["ident"][:],
                    tile_position=(0, half * 64)))
            dstt = ar(B.MM_XHTM + ti * 512, 512)[half * 64:half * 64 + 64, :]
            p.waits("v", tmax(ttr, self.tk.get(("xhtm", ti, half))))
            tev = p.op("v", lambda e, dstt=dstt, half=half: e.tensor_copy(
                dstt, psb[half * 64:half * 64 + 64, 0:512]))
            self.bank_guard(7, tev)
            ev_toks.append(tev)
            self.tk[("xhtm_t", ti, half)] = tev
        if upto == "xht":
            self.tk[("yg", tb, ti)] = tsil
            self.tk["mscr_free_v"] = ev_toks[-1]
            self.tk["mscr_free_t"] = ("pe", p.cnt["pe"])
            self.tk[("mxin", ti)] = prevc
            self.tk[("xhtm", ti, 0)] = ("pe", p.cnt["pe"])
            self.tk[("xhtm", ti, 1)] = ("pe", p.cnt["pe"])
            return
        if upto == "proj":
            self.tk[("yg", tb, ti)] = tsil
            return
        acol = self.sb["A_cols"][:, ti:ti + 1]
        nacol = self.sb["negA_cols"][:, ti:ti + 1]
        hl, hr = self.sb["mDlhl"], self.sb["mDrevhl"]
        self.bank_wait(2)
        p.waits("t", tmax(self.tk.get("mdt_done")))
        p.raw("t", lambda e: e.matmul(self.ps[2][:, 0:MTB], sel, hl[:, 0:MTB], start=True, stop=False))
        tDl = p.op("t", lambda e: e.matmul(self.ps[2][:, 0:MTB], sel, hl[:, MTB:2 * MTB], start=False, stop=True))
        p.waits("s", tmax(tDl))
        for c4 in range(4):
            p.op("s", lambda e, c4=c4: e.activation(
                self.sb["mPfull"][:, ti * 4 + c4: ti * 4 + c4 + 1],
                self.ps[2][:, c4 * MCH + 63:c4 * MCH + 64], AF.Exp, scale=acol))
        tE1 = p.op("s", lambda e: e.activation(ar(B.MM_SCR + 3 * MTB, MTB),
                                               self.ps[2][:, 0:MTB], AF.Exp, scale=acol))
        p.waits("v", tmax(tDl, self.tk.get("mscrf_w"), self.tk.get("mscrf_rs")))
        tm2 = p.op("v", lambda e: e.tensor_scalar(mscrf[:, 0:MTB], self.ps[2][:, 0:MTB],
                                                  self.sb["clamp_cols"][:, ti:ti + 1], None, op0=OP.min))
        self.bank_guard(2, tm2, tE1)
        self.tk["mscrf_w"] = tm2
        p.waits("s", tmax(tm2))
        tE2 = p.op("s", lambda e: e.activation(ar(B.MM_SCR + 4 * MTB, MTB), mscrf[:, 0:MTB],
                                               AF.Exp, scale=nacol))
        self.tk["mscrf_rs"] = tE2
        self.bank_wait(3)
        p.raw("t", lambda e: e.matmul(self.ps[3][:, 0:MTB], sel, hr[:, 0:MTB], start=True, stop=False))
        tDr = p.op("t", lambda e: e.matmul(self.ps[3][:, 0:MTB], sel, hr[:, MTB:2 * MTB], start=False, stop=True))
        p.waits("s", tmax(tDr))
        tEe = p.op("s", lambda e: e.activation(ar(B.MM_SCR + 5 * MTB, MTB), self.ps[3][:, 0:MTB],
                                               AF.Exp, scale=nacol))
        self.bank_guard(3, tEe)
        if upto == "decay":
            self.tk[("yg", tb, ti)] = tEe
            self.tk["mscr_free_v"] = tm2
            self.tk["mscr_free_t"] = ("pe", p.cnt["pe"])
            self.tk[("mxin", ti)] = prevc
            self.tk[("xhtm", ti, 0)] = ("pe", p.cnt["pe"])
            self.tk[("xhtm", ti, 1)] = ("pe", p.cnt["pe"])
            return
        self.bank_wait(2)
        p.waits("t", tmax(self.tk.get("mdt_done")))
        tdtB = p.op("t", lambda e: e.matmul(self.ps[2][:, 0:MTB], sel, self.sb["mdtb"][:, :],
                                            start=True, stop=True))
        p.waits("s", tmax(tdtB))
        tE1sm = p.op("s", lambda e: e.activation(ar(B.MM_E1, MTB), self.ps[2][:, 0:MTB],
                                                 AF.Exp, scale=acol))
        p.waits("v", tmax(tdtB, vB))
        vdB = p.op("v", lambda e: e.tensor_tensor(ar(B.MM_SCR + 0 * MTB, MTB), ar(B.MM_SCR + 0 * MTB, MTB),
                                                  self.ps[2][:, 0:MTB], op=OP.mult))
        self.bank_guard(2, vdB, tE1sm)
        p.waits("v", tmax(vdB))
        vP = []
        for k in range(NBAND):
            vP.append(p.op("v", lambda e, k=k: e.tensor_tensor(
                ar(B.MM_P0 + k * MTB, MTB), ar(B.MM_SCR + 1 * MTB, MTB),
                ar(B.MM_SCR + 0 * MTB - k, MTB), op=OP.mult)))
        p.waits("v", tmax(tE1sm))
        p.op("v", lambda e: e.tensor_tensor(ar(B.MM_E1, MTB), ar(B.MM_E1, MTB),
                                            self.sb["bmask"][:], op=OP.mult))
        p.op("v", lambda e: e.tensor_tensor(ar(B.MM_E2, MTB), ar(B.MM_E1, MTB),
                                            ar(B.MM_E1 - 1, MTB), op=OP.mult))
        p.op("v", lambda e: e.tensor_tensor(ar(B.MM_E3, MTB), ar(B.MM_E1, MTB),
                                            ar(B.MM_E2 - 1, MTB), op=OP.mult))
        vPE = None
        for k in range(1, NBAND):
            vPE = p.op("v", lambda e, k=k: e.tensor_tensor(
                ar(B.MM_P0 + k * MTB, MTB), ar(B.MM_P0 + k * MTB, MTB),
                ar(B.MM_E1 + (k - 1) * MTB, MTB), op=OP.mult))
        p.waits("v", tmax(tE1, vC))
        vCt = p.op("v", lambda e: e.tensor_tensor(ar(B.MM_SCR + 1 * MTB, MTB), ar(B.MM_SCR + 1 * MTB, MTB),
                                                  ar(B.MM_SCR + 3 * MTB, MTB), op=OP.mult))
        p.waits("v", tmax(tE2, self.tk.get("mscr2_free")))
        vBt = p.op("v", lambda e: e.tensor_tensor(ar(B.MM_SCR2, MTB), ar(B.MM_SCR + 0 * MTB, MTB),
                                                  ar(B.MM_SCR + 4 * MTB, MTB), op=OP.mult))
        p.waits("v", tmax(tEe))
        vBe = p.op("v", lambda e: e.tensor_tensor(ar(B.MM_SCR + 0 * MTB, MTB), ar(B.MM_SCR + 0 * MTB, MTB),
                                                  ar(B.MM_SCR + 5 * MTB, MTB), op=OP.mult))
        dBe = ar(B.MM_SCR + 0 * MTB, MTB)
        for half in range(2):
            for c4 in range(4):
                self.bank_wait(7)
                p.waits("t", tmax(vBe))
                ttr = p.op("t", lambda e, half=half, c4=c4: e.transpose(
                    psb[half * 64:half * 64 + 64, 512 + c4 * 128:512 + (c4 + 1) * 128],
                    dBe[:, c4 * 64:(c4 + 1) * 64], self.sb["ident"][:],
                    tile_position=(0, half * 64)))
            dstt = ar(B.MM_SCR2 + MTB, 512)[half * 64:half * 64 + 64, :]
            p.waits("v", tmax(ttr))
            tev = p.op("v", lambda e, dstt=dstt, half=half: e.tensor_copy(
                dstt, psb[half * 64:half * 64 + 64, 512:1024]))
            self.bank_guard(7, tev)
            ev_toks.append(tev)
        dBT_tok = tev
        self.bank_wait(4)
        self.bank_wait(5)
        p.waits("t", tmax(vBt, vCt))
        for c4 in range(4):
            p.raw("t", lambda e, c4=c4: e.matmul(
                self.ps[4][0:64, c4 * 64:(c4 + 1) * 64],
                ar(B.MM_SCR2, MTB)[0:64, c4 * 64:(c4 + 1) * 64],
                ar(B.MM_SCR + 1 * MTB, MTB)[0:64, c4 * 64:(c4 + 1) * 64],
                start=True, stop=True, tile_position=(0, 0)))
            fn = lambda e, c4=c4: e.matmul(
                self.ps[5][64:128, c4 * 64:(c4 + 1) * 64],
                ar(B.MM_SCR2, MTB)[64:128, c4 * 64:(c4 + 1) * 64],
                ar(B.MM_SCR + 1 * MTB, MTB)[64:128, c4 * 64:(c4 + 1) * 64],
                start=True, stop=True, tile_position=(64, 64))
            if c4 == 3:
                tG = p.op("t", fn)
            else:
                p.raw("t", fn)
        mG = ar(B.MM_G, 512)
        p.waits("v", tmax(tG, self.tk.get("mG_free")))
        vGA = p.op("v", lambda e: e.tensor_tensor(mG[0:64, 0:256], self.ps[4][0:64, 0:256],
                                                  self.sb["LT4"][0:64, :], op=OP.mult))
        vGB = p.op("v", lambda e: e.tensor_tensor(mG[64:128, 256:512], self.ps[5][64:128, 0:256],
                                                  self.sb["LT4"][64:128, :], op=OP.mult))
        self.bank_guard(4, vGA)
        self.bank_guard(5, vGB)
        if upto == "noscan":
            self.tk[("yg", tb, ti)] = vGB
            self.tk["mscr_free_v"] = vGB
            self.tk["mscr_free_t"] = ("pe", p.cnt["pe"])
            self.tk["mscr2_free"] = ("pe", p.cnt["pe"])
            self.tk["mG_free"] = vGB
            self.tk[("mxin", ti)] = prevc
            self.tk[("xhtm", ti, 0)] = ("pe", p.cnt["pe"])
            self.tk[("xhtm", ti, 1)] = ("pe", p.cnt["pe"])
            return
        NGBK = NBAND
        gb_ps = [(3, 0), (3, 256), (1, 256), (6, 256)]
        p.waits("t", tmax(tEe, vPE, vP[0],
                          self.tk.get("mgb1"), self.tk.get("mgb2"), self.tk.get("mgb3")))
        tgb = None
        for k in range(NGBK):
            gbank, gco = gb_ps[k]
            fngb = lambda e, k=k, gbank=gbank, gco=gco: e.matmul(
                self.ps[gbank][:, gco:gco + MTB], self.sb["dup64blk"][:],
                ar(B.MM_P0 + k * MTB, MTB), start=True, stop=True)
            tgb = p.op("t", fngb)
        h_ti = self.sb["mh"][:, ti * 64:(ti + 1) * 64]
        hb_ti = self.sb["mhb"][:, ti * 64:(ti + 1) * 64]
        xt = ar(B.MM_XHTM + ti * 512, 512)
        dBT = ar(B.MM_SCR2 + MTB, 512)
        for c4 in range(4):
            p.waits("t", tmax(vCt, vGA, vGB, self.tk.get(("mh_w", ti)),
                              *[self.tk.get(("xhtm_t", ti, hh2)) for hh2 in range(2)]))
            p.raw("t", lambda e, c4=c4: e.matmul(
                self.ps[4][0:64, 256 + c4 * 64:256 + (c4 + 1) * 64],
                hb_ti[0:64, :], ar(B.MM_SCR + 1 * MTB, MTB)[0:64, c4 * 64:(c4 + 1) * 64],
                start=True, stop=False, tile_position=(0, 0)))
            p.raw("t", lambda e, c4=c4: e.matmul(
                self.ps[4][0:64, 256 + c4 * 64:256 + (c4 + 1) * 64],
                xt[0:64, c4 * 128:c4 * 128 + 64], mG[0:64, c4 * 64:(c4 + 1) * 64],
                start=False, stop=True, tile_position=(0, 0)))
            p.raw("t", lambda e, c4=c4: e.matmul(
                self.ps[5][64:128, 256 + c4 * 64:256 + (c4 + 1) * 64],
                hb_ti[64:128, :], ar(B.MM_SCR + 1 * MTB, MTB)[64:128, c4 * 64:(c4 + 1) * 64],
                start=True, stop=False, tile_position=(64, 64)))
            p.raw("t", lambda e, c4=c4: e.matmul(
                self.ps[5][64:128, 256 + c4 * 64:256 + (c4 + 1) * 64],
                xt[64:128, c4 * 128 + 64:c4 * 128 + 128], mG[64:128, 256 + c4 * 64:256 + (c4 + 1) * 64],
                start=False, stop=True, tile_position=(64, 64)))
            self.bank_wait(6)
            p.waits("t", tmax(dBT_tok))
            tSA = p.op("t", lambda e, c4=c4: e.matmul(
                self.ps[6][0:64, 0:64],
                dBT[0:64, c4 * 128:c4 * 128 + 64], xt[0:64, c4 * 128:c4 * 128 + 64],
                start=True, stop=True, tile_position=(0, 0)))
            self.bank_wait(2)
            tSB = p.op("t", lambda e, c4=c4: e.matmul(
                self.ps[2][64:128, 256:320],
                dBT[64:128, c4 * 128 + 64:c4 * 128 + 128], xt[64:128, c4 * 128 + 64:c4 * 128 + 128],
                start=True, stop=True, tile_position=(64, 64)))
            pf = self.sb["mPfull"][:, ti * 4 + c4: ti * 4 + c4 + 1]
            p.waits("v", tmax(tSA, tSB))
            thA = p.op("v", lambda e, pf=pf: e.scalar_tensor_tensor(
                h_ti[0:64, :], h_ti[0:64, :], pf[0:64, :], self.ps[6][0:64, 0:64],
                op0=OP.mult, op1=OP.add))
            thB = p.op("v", lambda e, pf=pf: e.scalar_tensor_tensor(
                h_ti[64:128, :], h_ti[64:128, :], pf[64:128, :], self.ps[2][64:128, 256:320],
                op0=OP.mult, op1=OP.add))
            self.bank_guard(6, thA)
            self.bank_guard(2, thB)
            p.waits("v", tmax(thA, thB))
            thb = p.op("v", lambda e: e.tensor_copy(hb_ti, h_ti))
            self.tk[("mh_w", ti)] = thb
        tyT = ("pe", p.cnt["pe"])
        if self.debug and tb == 0 and ti == 0:
            p.waits("v", tmax(tyT))
            p.op("v", lambda e: e.tensor_copy(ar(50000, 256)[0:64, :], self.ps[4][0:64, 256:512]))
            p.op("v", lambda e: e.tensor_copy(ar(50000, 256)[64:128, :], self.ps[5][64:128, 256:512]))
            p.op("v", lambda e: e.tensor_copy(ar(50256, 64), self.sb["mhb"][:, 0:64]))
            p.op("v", lambda e: e.tensor_copy(ar(50320, 256), ar(B.MM_SCR + 1 * MTB, MTB)))
        t_z = None
        self.bank_wait(0)
        p.waits("t", tmax(twin))
        for kt in range(NT):
            lhsT = ar(B.MM_WIN + pp * 4096 + kt * 512 + 384, 128)
            rhs = ar(B.MM_XN + kt * SEQ + tb * MTB, MTB)
            fn = lambda e, lhsT=lhsT, rhs=rhs, kt=kt: e.matmul(
                self.ps[0][:, 0:MTB], lhsT, rhs, start=(kt == 0), stop=(kt == NT - 1))
            if kt == NT - 1:
                t_z = p.op("t", fn)
            else:
                p.raw("t", fn)
        p.waits("s", tmax(t_z, vCt))
        tsz = p.op("s", lambda e: e.activation(ar(B.MM_SCR + 3 * MTB, MTB), self.ps[0][:, 0:MTB], AF.Silu))
        self.bank_guard(0, tsz)
        self.tk[("mwin", pp)] = t_z
        dcol = self.sb["d_cols"][:, ti:ti + 1]
        s4 = ar(B.MM_SCR + 4 * MTB, MTB)
        s5 = ar(B.MM_SCR + 5 * MTB, MTB)
        xh0 = B.MM_SCR + 2 * MTB
        p.waits("v", tmax(tsil, vBt, vBe, tgb, t_z))
        if NGBK > 0:
            vdx = p.op("v", lambda e: e.scalar_tensor_tensor(
                s4, self.ps[3][:, 0:MTB], dcol, ar(xh0, MTB), op0=OP.add, op1=OP.mult))
        else:
            vdx = p.op("v", lambda e: e.tensor_scalar(
                s4, ar(xh0, MTB), dcol, None, op0=OP.mult))
        mtoks = []
        for k in range(1, NGBK):
            gbank, gco = gb_ps[k]
            mk = p.op("v", lambda e, gbank=gbank, gco=gco, k=k: e.tensor_tensor(
                s5, self.ps[gbank][:, gco:gco + MTB], ar(xh0 - k, MTB), op=OP.mult))
            mtoks.append(mk)
            vdx = p.op("v", lambda e: e.tensor_tensor(s4, s4, s5, op=OP.add))
        if NGBK > 0:
            self.bank_guard(3, vdx if not mtoks else mtoks[0])
        if len(mtoks) > 0:
            self.tk["mgb1"] = mtoks[0]
        if len(mtoks) > 1:
            self.tk["mgb2"] = mtoks[1]
            self.bank_guard(1, mtoks[1])
        if len(mtoks) > 2:
            self.tk["mgb3"] = mtoks[2]
            self.bank_guard(6, mtoks[2])
        p.waits("v", tmax(tyT, vdx, self.tk.get("mscrf_w"), self.tk.get("mscrf_rs")))
        vy2a = p.op("v", lambda e: e.tensor_tensor(
            self.sb["mscrf"][0:64, 0:MTB], self.ps[4][0:64, 256:512],
            ar(B.MM_SCR + 4 * MTB, MTB)[0:64, :], op=OP.add))
        vy2 = p.op("v", lambda e: e.tensor_tensor(
            self.sb["mscrf"][64:128, 0:MTB], self.ps[5][64:128, 256:512],
            ar(B.MM_SCR + 4 * MTB, MTB)[64:128, :], op=OP.add))
        self.bank_guard(4, vy2a)
        self.bank_guard(5, vy2)
        self.tk["mscrf_w"] = vy2
        yg = ar(B.MM_YG + ti * MTB, MTB)
        p.waits("v", tmax(tsz, self.tk.get(("myg", ti))))
        vyg = p.op("v", lambda e, yg=yg: e.tensor_tensor(yg, self.sb["mscrf"][:, 0:MTB],
                                                         ar(B.MM_SCR + 3 * MTB, MTB), op=OP.mult))
        self.tk[("yg", tb, ti)] = vyg
        self.tk["mscr_free_v"] = vyg
        self.tk["mscr_free_t"] = tyT
        self.tk["mscr2_free"] = tyT
        self.tk["mG_free"] = tyT
        self.tk[("mxin", ti)] = prevc
        self.tk[("xhtm", ti, 0)] = tyT
        self.tk[("xhtm", ti, 1)] = tyT

    def _mamba_wout(self, tb):
        p = self.p
        ar = self.ar
        B = Build
        stage = self.sb["stage"]
        for oc in range(NT):
            pp = oc % 2
            prev = self.tk.get(("mwout", pp))
            if prev is not None:
                p.waits("g", tmax(prev))
            two = p.dma("g", "dq1",
                        ar(B.MM_WOUT + pp * 2048, 2048).rearrange("p (kt o) -> p kt o", o=128),
                        self.dram["woutT"][:, oc * 128:(oc + 1) * 128].rearrange("(kt p) o -> p kt o", p=128))
            bk = self.ps[1]
            self.bank_wait(1)
            p.waits("t", tmax(two, *[self.tk.get(("yg", tb, ti2)) for ti2 in range(NTI)]))
            for kt in range(16):
                lhsT = ar(B.MM_WOUT + pp * 2048 + kt * 128, 128)
                rhs = ar(B.MM_YG + kt * MTB, MTB)
                fn = lambda e, lhsT=lhsT, rhs=rhs, kt=kt: e.matmul(
                    bk[:, 0:MTB], lhsT, rhs, start=(kt == 0), stop=(kt == 15))
                if kt == 15:
                    tmm = p.op("t", fn)
                else:
                    p.raw("t", fn)
            self.tk[("mwout", pp)] = tmm
            if oc == NT - 1:
                for ti2 in range(NTI):
                    self.tk[("myg", ti2)] = tmm
            st = stage[:, (oc % 2) * CHW:(oc % 2) * CHW + MTB]
            p.waits("y", tmax(self.tk.get(("stageslot", oc % 2)), self.tk.get(("stageslot_s", oc % 2))))
            tld = p.dma("y", "dq2", st,
                        self.dram["xres"][oc * 128:(oc + 1) * 128, tb * MTB:(tb + 1) * MTB])
            p.waits("v", tmax(tmm, tld))
            tv = p.op("v", lambda e, st=st: e.tensor_tensor(st, bk[:, 0:MTB], st, op=OP.add))
            self.bank_guard(1, tv)
            p.wait("y", tv)
            td = p.dma("y", "dq2",
                       self.dram["xres"][oc * 128:(oc + 1) * 128, tb * MTB:(tb + 1) * MTB], st)
            self.tk[("stageslot", oc % 2)] = td
            self.final_toks.append(td)

    def finish(self):
        p = self.p
        for q in ["dq0", "dq1", "dq2", "dq3"]:
            for i in range(3):
                s = f"{q}_{i}"
                p.ops["y"].append(lambda e, sem=p.semh[s], val=p.cnt[s]: e.wait_ge(sem, val))


def build_and_run(maps, core_ids, stages=("full",), debug=False, trace=False):
    nc = bass.Bass("TRN2", target_bir_lowering=False)
    ctx = ExitStack()
    with ctx:
        b = Build(nc, ctx, stages, debug)
        b.emit_consts()
        st = set(stages)
        need_ln0 = st & {"full", "ln0", "attn0", "ffn0", "ffnln", "stack1", "mamba"}
        if need_ln0:
            b.emit_outer_ln("x0", "xstack")
        if "attn0" in st:
            b.emit_attn_sub(0)
        if "ffn0" in st:
            b.emit_attn_sub(0)
            b.emit_ffn_sub(0)
        if "ffnln" in st:
            b.emit_attn_sub(0)
            b.emit_ffn_sub(0, ln_only=True)
        if "stack1" in st:
            for l in range(DEPTH):
                b.emit_attn_sub(l)
                b.emit_ffn_sub(l)
        if "mamba" in st:
            for l in range(DEPTH):
                b.emit_attn_sub(l)
                b.emit_ffn_sub(l)
            b.emit_residual_combine("x0", "xstack", 0.5, "xres", mamba_xn=True)
            b.tk["mxn"] = ("ve", b.p.cnt["ve"])
            b.emit_mamba()
        for v in ["monly", "monly_dt", "monly_proj", "monly_xht", "monly_decay", "monly_noscan", "monly_nowout",
                  "monly_decay1", "monly_all1"]:
            if v in st:
                b.emit_residual_combine("x0", "xstack", 0.5, "xres", mamba_xn=True)
                b.tk["mxn"] = ("ve", b.p.cnt["ve"])
                b.emit_mamba(upto=v[6:] if len(v) > 5 else "all")
                if b.debug:
                    pp = b.p
                    frees = tmax(("ve", pp.cnt["ve"]), ("se", pp.cnt["se"]), ("pe", pp.cnt["pe"]))
                    pp.waits("g", frees)
                    dd = b.dram["dbga"]
                    b.final_toks.append(pp.dma("g", "dq1", dd[:, 0:1536], b.ar(Build.MM_SCR, 1536)))
                    b.final_toks.append(pp.dma("g", "dq1", dd[:, 1536:2048], b.ar(Build.MM_G, 512)))
                    b.final_toks.append(pp.dma("g", "dq1", dd[0:32, 2048:2304], b.sb["mdt"][:, :]))
                    b.final_toks.append(pp.dma("g", "dq1", dd[0:32, 2304:2560], b.sb["mDl"][:, :]))
                    b.final_toks.append(pp.dma("g", "dq1", dd[:, 2560:2624], b.sb["mPfull"][:, :]))
                    b.final_toks.append(pp.dma("g", "dq1", dd[:, 2624:2880], b.ar(Build.MM_YG, 256)))
                    b.final_toks.append(pp.dma("g", "dq1", dd[:, 2880:2944], b.sb["mh"][:, 0:64]))
                    b.final_toks.append(pp.dma("g", "dq1", dd[:, 2944:3456], b.ar(Build.MM_SCR2 + 256, 512)))
                    b.final_toks.append(pp.dma("g", "dq1", dd[:, 3456:3968], b.ar(Build.MM_XHTM, 512)))
                    b.final_toks.append(pp.dma("g", "dq1", dd[:, 3968:4224], b.ar(50000, 256)))
                    b.final_toks.append(pp.dma("g", "dq1", dd[:, 4224:4288], b.ar(50256, 64)))
                    b.final_toks.append(pp.dma("g", "dq1", dd[:, 4288:4544], b.ar(50320, 256)))
                    b.final_toks.append(pp.dma("g", "dq1", dd[:, 4544:4800], b.ar(Build.MM_SCR2, 256)))
        if "full" in st:
            for l in range(DEPTH):
                b.emit_attn_sub(l)
                b.emit_ffn_sub(l)
            b.emit_residual_combine("x0", "xstack", 0.5, "xres", mamba_xn=True)
            b.tk["mxn"] = ("ve", b.p.cnt["ve"])
            b.emit_mamba()
            b.phase_barrier()
            b.emit_outer_ln("xres", "xstack")
            for l in range(DEPTH):
                b.emit_attn_sub(l)
                b.emit_ffn_sub(l)
        if "full" in st or "fin" in st:
            b.phase_barrier()
            b.emit_final_combine()
        b.finish()
        with nc.Block() as block:
            b.p.run(block)
        if maps is None:
            ctx.pop_all()
            return nc
        res = run_bass_kernel_spmd(nc, maps, core_ids=core_ids, trace=trace)
    return res


def _in_maps(inputs, n_cores=8):
    consts = _host_consts()
    w = _host_weights(inputs)
    x = inputs["L0_attn_x"]
    maps = []
    for i in range(n_cores):
        m = {}
        m["x0"] = np.ascontiguousarray(x[i].T).astype(np.float32)
        for k, v in w.items():
            m[k] = v
        for k, v in consts.items():
            kk = "LTm" if k == "LT" else k
            m[kk] = v
        maps.append(m)
    return maps


def kernel(**inputs):
    maps = _in_maps(inputs)
    res = build_and_run(maps, list(range(8)), stages=("full",), debug=False)
    out = np.stack([np.ascontiguousarray(np.asarray(r["y"], np.float32).T)
                    for r in res.results], axis=0)
    return out


# version 09
# speedup vs baseline: 1.1319x; 1.0038x over previous
import numpy as np
import ml_dtypes
import concourse.bass as bass
import concourse.mybir as mybir
from concourse.bass_utils import run_bass_kernel_spmd
from concourse.alu_op_type import AluOpType as OP
from contextlib import ExitStack

F32 = mybir.dt.float32
BF16 = mybir.dt.bfloat16
AF = mybir.ActivationFunctionType
bf16 = ml_dtypes.bfloat16

DM = 1024
SEQ = 2048
NT = 8
HEADS = 16
DH = 64
NPAIR = 8
DEPTH = 6
FI = 2730
FIP = 2816
KH = 11
DIN = 2048
NH = 32
MTB = 256
MCH = 64
NTI = 16
CHW = 512
NCHUNK = SEQ // CHW
VS = 66
NBAND = 4


def _host_consts():
    c = {}
    f64 = np.float64
    half = np.arange(0, DH, 2, dtype=f64)
    inv_freq = 1.0 / 10000.0 ** (half / DH)
    t = np.arange(SEQ, dtype=f64)
    freqs = np.concatenate([t[:, None] * inv_freq[None, :]] * 2, -1)
    base_scale = (half + 0.4 * DH) / (1.4 * DH)
    power = (t - SEQ // 2) / 128.0
    scale = np.concatenate([base_scale[None, :] ** power[:, None]] * 2, -1)
    cos, sin = np.cos(freqs), np.sin(freqs)
    tile2 = lambda a: np.ascontiguousarray(np.concatenate([a.T, a.T], 0)).astype(np.float32)
    c["cs_q"], c["ss_q"] = tile2(cos * scale), tile2(sin * scale)
    c["cs_k"], c["ss_k"] = tile2(cos / scale), tile2(sin / scale)
    R = np.zeros((DH, DH), f64)
    for i in range(32):
        R[i, i + 32] = -1.0
        R[i + 32, i] = 1.0
    rt2 = np.zeros((128, 128), f64)
    rt2[:64, :64] = R.T
    rt2[64:, 64:] = R.T
    c["rot"] = rt2.astype(np.float32)
    jj, ii = np.meshgrid(np.arange(128), np.arange(128), indexing="ij")
    c["UT"] = (jj >= ii).astype(np.float32)
    c["LTm"] = (jj <= ii).astype(np.float32)
    p = np.arange(128)[:, None] % 64
    q = np.arange(MTB)[None, :] % 64
    c["LT4"] = (q >= p + NBAND).astype(np.float32)
    bm = np.ones((128, MTB), np.float32)
    bm[:, ::MCH] = 0.0
    c["bmask"] = bm
    dd = np.zeros((128, 128), np.float64)
    dd[:64, :64] = 1.0
    dd[64:, 64:] = 1.0
    c["dup64blk"] = dd.astype(np.float32)
    c["ident"] = np.eye(128).astype(np.float32)
    am = np.zeros((128, 9 * 128), f64)
    jj2, ii2 = np.meshgrid(np.arange(128), np.arange(128), indexing="ij")
    am[:, 3 * 128:4 * 128] = (jj2 <= ii2)
    am[:, 4 * 128:5 * 128] = 1.0
    am[:, 5 * 128:6 * 128] = (jj2 >= ii2)
    c["amask"] = am.astype(np.float32)
    c["onescol"] = np.ones((128, 1), np.float32)
    c["onescol_b"] = np.ones((128, 1)).astype(np.float32)
    sel2 = np.zeros((128, 2), f64)
    sel2[:64, 0] = 1.0
    sel2[64:, 1] = 1.0
    c["sel2"] = sel2.astype(np.float32)
    selt22 = np.zeros((4, 128), f64)
    selt22[0, :64] = 1.0; selt22[1, 64:] = 1.0
    selt22[2, :64] = 1.0; selt22[3, 64:] = 1.0
    c["selt22"] = selt22.astype(np.float32)
    c["ones2T"] = np.ones((2, 128)).astype(np.float32)
    c["onesrow"] = np.ones((1, 128)).astype(np.float32)
    selA = np.zeros((1, 128)); selA[0, :64] = 1.0
    selB = np.zeros((1, 128)); selB[0, 64:] = 1.0
    c["selA"] = selA.astype(np.float32)
    c["selB"] = selB.astype(np.float32)
    selscan = np.zeros((32, NTI * 128), f64)
    for ti in range(NTI):
        for pp in range(128):
            selscan[2 * ti + pp // 64, ti * 128 + pp] = 1.0
    c["selscan"] = selscan.astype(np.float32)
    m = np.ones((32, MTB), np.float32)
    m[:, ::MCH] = 0.0
    c["scanmask"] = m
    c["c_neg30"] = np.full((128, 1), -30.0, np.float32)
    c["c_eps5"] = np.full((128, 1), 1e-5, np.float32)
    c["c_eps24"] = np.full((128, 1), 1e-24, np.float32)
    c["c_one"] = np.full((128, 1), 1.0, np.float32)
    return c


def _host_weights(inp):
    w = {}
    w["wqkv"] = np.ascontiguousarray(inp["L0_ffn_Wqkv"], np.float32)
    w["wo"] = np.ascontiguousarray(inp["L0_ffn_Wo"], np.float32)
    W1 = inp["L0_ffn_W1"]
    w1a = np.zeros((DEPTH, DM, FIP), np.float32); w1a[:, :, :FI] = W1[:, :, :FI]
    w1g = np.zeros((DEPTH, DM, FIP), np.float32); w1g[:, :, :FI] = W1[:, :, FI:]
    w["w1a"], w["w1g"] = w1a, w1g
    w2p = np.zeros((DEPTH, FIP, DM), np.float32); w2p[:, :FI, :] = inp["L0_ffn_W2"]
    w["w2"] = w2p
    qs, ks = inp["L0_ffn_q_scale"], inp["L0_ffn_k_scale"]
    w["qs_cols"] = np.ascontiguousarray(np.concatenate([qs, qs], 1).T, np.float32)
    w["ks_cols"] = np.ascontiguousarray(np.concatenate([ks, ks], 1).T, np.float32)
    g, bb = inp["L0_ffn_ln_g"], inp["L0_ffn_ln_b"]
    tcol = lambda a, k: np.ascontiguousarray(
        a.reshape(DEPTH, k, 128).transpose(2, 0, 1).reshape(128, DEPTH * k), np.float32)
    w["lng_cols"], w["lnb_cols"] = tcol(g, 8), tcol(bb, 8)
    b1 = inp["L0_ffn_b1"]
    b1a = np.zeros((DEPTH, FIP), np.float32); b1a[:, :FI] = b1[:, :FI]
    b1g = np.zeros((DEPTH, FIP), np.float32); b1g[:, :FI] = b1[:, FI:]
    w["b1a_cols"], w["b1g_cols"] = tcol(b1a, 22), tcol(b1g, 22)
    w["b2_cols"] = tcol(inp["L0_ffn_b2"], 8)
    Win = inp["L0_attn_W_in"]
    base = 4096 + 129 * np.arange(NH)
    brows = np.concatenate([base[n] + np.arange(64) for n in range(NH)])
    crows = np.concatenate([base[n] + 64 + np.arange(64) for n in range(NH)])
    perm = np.zeros(8224, np.int64)
    for ti in range(16):
        o = ti * 512
        perm[o:o + 128] = np.arange(ti * 128, ti * 128 + 128)
        perm[o + 128:o + 256] = brows[ti * 128:(ti + 1) * 128]
        perm[o + 256:o + 384] = crows[ti * 128:(ti + 1) * 128]
        perm[o + 384:o + 512] = np.arange(2048 + ti * 128, 2048 + (ti + 1) * 128)
    perm[8192:8224] = base + 128
    w["winT"] = np.ascontiguousarray(Win[perm].T, np.float32)
    w["woutT"] = np.ascontiguousarray(inp["L0_attn_W_out"].T, np.float32)
    cw = inp["L0_attn_conv_w"]
    w["convw_cols"] = np.ascontiguousarray(cw.reshape(16, 128, 4).transpose(1, 0, 2).reshape(128, 64), np.float32)
    w["convb_cols"] = np.ascontiguousarray(inp["L0_attn_conv_b"].reshape(16, 128).T, np.float32)
    w["alog_cols"] = np.ascontiguousarray(inp["L0_attn_A_log"].reshape(2048).reshape(16, 128).T, np.float32)
    w["d_cols"] = np.ascontiguousarray(np.repeat(inp["L0_attn_D"], 64).reshape(16, 128).T, np.float32)
    w["dtb_col"] = np.ascontiguousarray(inp["L0_attn_dt_bias"].reshape(32, 1), np.float32)
    return w


class Prog:
    def __init__(self, nc, ctx):
        self.nc = nc
        self.ops = {e: [] for e in "tvsgy"}
        self.semh = {}
        self.cnt = {}
        self.memo = {}
        names = ["pe", "ve", "se", "ge"]
        for q in ["dq0", "dq1", "dq2", "dq3"]:
            for i in range(3):
                names.append(f"{q}_{i}")
        for name in names:
            self.semh[name] = ctx.enter_context(nc.semaphore(name))
            self.cnt[name] = 0
        self.dqi = {q: 0 for q in ["dq0", "dq1", "dq2", "dq3"]}
        self.eng_sem = {"t": "pe", "v": "ve", "s": "se", "g": "ge"}

    def raw(self, eng, fn):
        if eng == "t":
            self.op(eng, fn)
            return
        self.ops[eng].append(fn)

    def op(self, eng, fn):
        sname = self.eng_sem[eng]
        if eng in "vsg":
            self.wait(eng, (sname, self.cnt[sname]))
        self.cnt[sname] += 1
        val = self.cnt[sname]
        sem = self.semh[sname]
        def run(e, fn=fn, sem=sem):
            fn(e).then_inc(sem, 1)
        self.ops[eng].append(run)
        return (sname, val)

    def tok(self, eng):
        s = self.eng_sem[eng]
        return (s, self.cnt[s])

    def wait(self, eng, token):
        if token is None:
            return
        sname, val = token
        key = (eng, sname)
        if self.memo.get(key, 0) >= val:
            return
        self.memo[key] = val
        sem = self.semh[sname]
        self.ops[eng].append(lambda e, sem=sem, val=val: e.wait_ge(sem, val))

    def waits(self, eng, tokens):
        for tk in tokens:
            self.wait(eng, tk)

    def dma(self, eng, q, dst, src):
        i = self.dqi[q]
        self.dqi[q] = (i + 1) % 3
        sname = f"{q}_{i}"
        self.wait(eng, (sname, self.cnt[sname]))
        self.cnt[sname] += 16
        val = self.cnt[sname]
        sem = self.semh[sname]
        def run(e, dst=dst, src=src, sem=sem):
            e.dma_start(dst, src).then_inc(sem, 16)
        self.ops[eng].append(run)
        return (sname, val)

    def run(self, block):
        secs = {"t": block.tensor, "v": block.vector, "s": block.scalar,
                "g": block.gpsimd, "y": block.sync}
        for e in "tvsgy":
            ops = self.ops[e]
            @secs[e]
            def _(eng, ops=ops):
                for f in ops:
                    f(eng)


def tmax(*toks):
    best = {}
    for t in toks:
        if t is None:
            continue
        s, v = t
        if best.get(s, -1) < v:
            best[s] = v
    return [(s, v) for s, v in best.items()]


class Build:
    def __init__(self, nc, ctx, stages, debug):
        self.nc = nc
        self.p = Prog(nc, ctx)
        self.ctx = ctx
        self.stages = stages
        self.debug = debug
        self.sb = {}
        self.ps = {}
        self.dram = {}
        self.tk = {}
        self.final_toks = []
        self._mk_dram()
        self._mk_sbuf()
        self._mk_psum()

    def _mk_dram(self):
        nc = self.nc
        d = self.dram
        def di(name, shape, dt=F32):
            d[name] = nc.dram_tensor(name, list(shape), dt, kind="ExternalInput").ap()
        di("x0", (DM, SEQ))
        di("wqkv", (DEPTH, DM, 3 * DM)); di("wo", (DEPTH, DM, DM))
        di("w1a", (DEPTH, DM, FIP)); di("w1g", (DEPTH, DM, FIP)); di("w2", (DEPTH, FIP, DM))
        di("winT", (DM, 8224)); di("woutT", (DIN, DM))
        di("qs_cols", (128, 6)); di("ks_cols", (128, 6))
        di("lng_cols", (128, 48)); di("lnb_cols", (128, 48))
        di("b1a_cols", (128, 132)); di("b1g_cols", (128, 132)); di("b2_cols", (128, 48))
        di("convw_cols", (128, 64)); di("convb_cols", (128, 16))
        di("alog_cols", (128, 16)); di("d_cols", (128, 16)); di("dtb_col", (32, 1))
        di("amask", (128, 9 * 128))
        for n, shp in [("cs_q", (128, SEQ)), ("ss_q", (128, SEQ)),
                       ("cs_k", (128, SEQ)), ("ss_k", (128, SEQ)),
                       ("rot", (128, 128)), ("UT", (128, 128)), ("LTm", (128, 128)),
                       ("LT4", (128, MTB)), ("bmask", (128, MTB)),
                       ("dup64blk", (128, 128)), ("ident", (128, 128)),
                       ("onescol_b", (128, 1)), ("sel2", (128, 2)),
                       ("selt22", (4, 128)), ("ones2T", (2, 128)),
                       ("onesrow", (1, 128)), ("selA", (1, 128)), ("selB", (1, 128)),
                       ("selscan", (32, NTI * 128))]:
            di(n, shp)
        for n, shp in [("onescol", (128, 1)), ("scanmask", (32, MTB)),
                       ("c_neg30", (128, 1)), ("c_eps5", (128, 1)), ("c_eps24", (128, 1)),
                       ("c_one", (128, 1))]:
            di(n, shp)
        kd = "ExternalOutput" if self.debug else "Internal"
        d["xres"] = nc.dram_tensor("xres", [DM, SEQ], F32, kind=kd).ap()
        d["xstack"] = nc.dram_tensor("xstack", [DM, SEQ], F32, kind=kd).ap()
        d["y"] = nc.dram_tensor("y", [DM, SEQ], F32, kind="ExternalOutput").ap()
        if self.debug:
            d["dbgrows"] = nc.dram_tensor("dbgrows", [4, 5 * CHW], F32, kind="ExternalOutput").ap()
            d["dbga"] = nc.dram_tensor("dbga", [128, 16384], F32, kind="ExternalOutput").ap()

    def _mk_sbuf(self):
        S = lambda n, s, dt: self.sb.__setitem__(
            n, self.ctx.enter_context(self.nc.sbuf_tensor("sb_" + n, list(s), dt)))
        for n in ["rot", "UT", "LTm", "ident"]:
            S(n, (128, 128), BF16)
        S("LT4", (128, MTB), BF16)
        S("bmask", (128, MTB), BF16); S("dup64blk", (128, 128), BF16)
        S("onescol_b", (128, 1), BF16); S("sel2", (128, 2), BF16)
        S("selt22", (4, 128), BF16); S("ones2T", (2, 128), BF16)
        S("onesrow", (1, 128), BF16); S("selA", (1, 128), BF16); S("selB", (1, 128), BF16)
        S("selscan", (32, NTI * 128), BF16); S("scanmask", (32, MTB), F32)
        for n in ["onescol", "c_neg30", "c_eps5", "c_eps24", "c_one"]:
            S(n, (128, 1), F32)
        S("qs_cols", (128, 6), F32); S("ks_cols", (128, 6), F32)
        S("lng_cols", (128, 48), F32); S("lnb_cols", (128, 48), F32)
        S("b1a_cols", (128, 132), F32); S("b1g_cols", (128, 132), F32)
        S("b2_cols", (128, 48), F32)
        S("convw_cols", (128, 64), F32); S("convb_cols", (128, 16), F32)
        S("alog_cols", (128, 16), F32); S("d_cols", (128, 16), F32)
        S("dtb_col", (32, 1), F32)
        S("A_cols", (128, 16), F32); S("negA_cols", (128, 16), F32)
        S("clamp_cols", (128, 16), F32)
        S("xs", (128, 2 * NT * (CHW + 2)), BF16)
        S("stage", (128, 2 * CHW), F32)
        S("outacc", (128, NT * CHW), F32)
        S("scr", (128, 4 * CHW), BF16)
        S("scrf", (128, 2 * CHW), F32)
        S("rows", (1, 5 * CHW), F32)
        S("rowsb", (1, 4 * CHW), BF16)
        S("attnw", (128, 4 * CHW), BF16)
        S("srecip", (1, 2 * CHW), BF16)
        S("amask", (128, 9 * 128), BF16)
        S("mh", (128, NTI * 64), F32)
        S("mhb", (128, NTI * 64), BF16)
        S("xtails", (128, NTI * 3), BF16)
        S("mdt", (32, MTB), F32); S("mdtb", (32, MTB), BF16)
        S("mDl", (32, MTB), F32); S("mDrev", (32, MTB), F32)
        S("mDlhl", (32, 2 * MTB), BF16)
        S("mDrevhl", (32, 2 * MTB), BF16)
        S("mDtot", (32, 4), F32)
        S("mDtotb", (32, 8), BF16)
        S("mPfull", (128, NTI * 4), F32)
        S("mscrf", (128, MTB), F32)
        S("arena", (128, 59648), BF16)

    def ar(self, off, n):
        return self.sb["arena"][:, off:off + n]

    AT_WQKV = 0
    AT_KR = 24576
    AT_Q = 32768
    AT_VR = 36864
    AT_AO = 45312
    AT_WO = 49408
    AT_TAB = 57600
    FF_W1A = 0
    FF_W1G = 11264
    FF_W2 = 22528
    FF_XN = 33792
    FF_H2 = 50176
    MM_XN = 0
    MM_WIN = 16384
    MM_WOUT = 24576
    MM_XHTM = 28672
    MM_YG = 36864
    MM_XIN = 40960
    MM_SCR = 45104
    MM_SCR2 = 46640
    MM_G = 47408
    MM_WDT = 47920
    MM_E1 = 48176
    MM_E2 = 48432
    MM_E3 = 48688
    MM_P0 = 48944

    def _mk_psum(self):
        for i in range(7):
            self.ps[i] = self.ctx.enter_context(
                self.nc.psum_tensor(f"bank{i}", [128, 512], F32))
        self.ps[7] = self.ctx.enter_context(
            self.nc.psum_tensor("bank7", [128, 1024], BF16))

    def xs_slice(self, parity, tile, shifted, width=CHW):
        base = parity * NT * (CHW + 2) + tile * (CHW + 2)
        off = 1 if shifted else 2
        return self.sb["xs"][:, base + off: base + off + width]

    def bank_guard(self, bank, *toks):
        old = self.tk.get(("bank", bank)) or []
        self.tk[("bank", bank)] = tmax(*(old + list(toks)))

    def bank_wait(self, bank):
        toks = self.tk.get(("bank", bank)) or []
        self.p.waits("t", toks)
        self.tk[("bank", bank)] = []

    def emit_consts(self):
        p = self.p
        toks = []
        for n in ["rot", "UT", "LTm", "LT4", "bmask", "dup64blk",
                  "ident", "onescol_b", "sel2", "selt22", "ones2T",
                  "onesrow", "selA", "selB", "selscan",
                  "scanmask", "onescol", "c_neg30", "c_eps5", "c_eps24", "c_one",
                  "qs_cols", "ks_cols", "lng_cols", "lnb_cols",
                  "b1a_cols", "b1g_cols", "b2_cols", "convw_cols",
                  "convb_cols", "alog_cols", "d_cols", "dtb_col", "amask"]:
            toks.append(p.dma("g", "dq0", self.sb[n][:], self.dram[n]))
        consts_tok = tmax(*toks)
        p.waits("s", consts_tok)
        p.waits("v", consts_tok)
        p.waits("t", consts_tok)
        p.waits("g", consts_tok)
        t1 = p.op("s", lambda e: e.activation(self.sb["negA_cols"][:], self.sb["alog_cols"][:], AF.Exp))
        p.wait("v", t1)
        p.op("v", lambda e: e.tensor_scalar(self.sb["A_cols"][:], self.sb["negA_cols"][:], -1.0, None, op0=OP.mult))
        p.op("v", lambda e: e.reciprocal(self.sb["clamp_cols"][:], self.sb["negA_cols"][:]))
        p.op("v", lambda e: e.tensor_scalar(self.sb["clamp_cols"][:], self.sb["clamp_cols"][:], 78.0, None, op0=OP.mult))
        for nm in ["arena", "scr", "scrf", "rows", "rowsb", "attnw", "srecip",
                   "stage", "outacc", "xs", "mscrf", "mdt", "mdtb",
                   "mDl", "mDrev", "mDlhl", "mDrevhl", "mDtot", "mDtotb", "mPfull", "mhb"]:
            p.op("v", lambda e, nm=nm: e.memset(self.sb[nm][:], 0.0))
        p.op("v", lambda e: e.memset(self.sb["mh"][:], 0.0))
        p.op("v", lambda e: e.memset(self.sb["mhb"][:], 0.0))
        p.op("v", lambda e: e.memset(self.sb["xtails"][:], 0.0))
        self.tk["init"] = p.tok("v")

    def load_xs_chunk(self, src_dram, c, parity, q="dq1"):
        if isinstance(src_dram, str):
            src_dram = self.dram[src_dram]
        p = self.p
        xs = self.sb["xs"]
        base = parity * NT * (CHW + 2)
        t0 = c * CHW
        prev = self.tk.get(("xsfree", parity))
        if prev is not None:
            p.waits("g", tmax(*prev) if isinstance(prev, list) else tmax(prev))
        p.waits("g", tmax(self.tk.get(("xstail", parity))))
        if c == 0:
            zt = p.op("v", lambda e: e.memset(
                xs[:, base:base + NT * (CHW + 2)].rearrange("p (a c) -> p a c", c=CHW + 2)[:, :, 0:2], 0.0))
            p.wait("g", zt)
            src = src_dram[:, 0:CHW].rearrange("(a p) t -> p a t", p=128)
            dst = xs[:, base:base + NT * (CHW + 2)].rearrange("p (a c) -> p a c", c=CHW + 2)[:, :, 2:CHW + 2]
            tmain = p.dma("g", q, dst, src)
            self.tk[("xsload", parity)] = tmain
            return tmain
        grid = xs[:, base:base + NT * (CHW + 2)].rearrange("p (a c) -> p a c", c=CHW + 2)
        src = src_dram[:, t0: t0 + CHW].rearrange("(a p) t -> p a t", p=128)
        tmain = p.dma("g", q, grid[:, :, 2:CHW + 2], src)
        pbase = (1 - parity) * NT * (CHW + 2)
        pgrid = xs[:, pbase:pbase + NT * (CHW + 2)].rearrange("p (a c) -> p a c", c=CHW + 2)
        p.waits("v", tmax(self.tk.get(("xsload", 1 - parity)), tmain))
        tcp = p.op("v", lambda e: e.tensor_copy(grid[:, :, 0:2], pgrid[:, :, CHW:CHW + 2]))
        self.tk[("xsload", parity)] = tmain
        self.tk[("xstail", 1 - parity)] = tcp
        return tcp

    def emit_ln(self, tiles, writer, in_toks=None):
        p = self.p
        sq = self.sb["scr"]
        rows = self.sb["rows"]
        rowsb = self.sb["rowsb"]
        one = self.sb["onescol_b"]
        bs, bb, bq = self.ps[4], self.ps[5], self.ps[6]
        if in_toks is None:
            in_toks = [None] * NT
        R = lambda i: rows[0:1, i * CHW:(i + 1) * CHW]
        Rb = lambda i: rowsb[0:1, i * CHW:(i + 1) * CHW]
        self.bank_wait(4)
        self.bank_wait(6)
        sq_toks = []
        for a in range(NT):
            dst = sq[:, (a % 4) * CHW:(a % 4 + 1) * CHW]
            prev = self.tk.get(("sqslot", a % 4))
            p.waits("s", tmax(prev, in_toks[a]))
            sqt = p.op("s", lambda e, dst=dst, srcap=tiles[a]: e.activation(dst, srcap, AF.Square))
            sq_toks.append(sqt)
            p.waits("t", tmax(in_toks[a]))
            p.raw("t", lambda e, a=a, srcap=tiles[a]: e.matmul(
                bs[0:1, 0:CHW], one[:], srcap, start=(a == 0), stop=(a == NT - 1)))
            p.waits("t", tmax(sqt))
            last = a == NT - 1
            fn = lambda e, dst=dst, a=a, last=last: e.matmul(
                bq[0:1, 0:CHW], one[:], dst, start=(a == 0), stop=last)
            if last:
                tsum = p.op("t", fn)
            else:
                p.raw("t", fn)
            self.tk[("sqslot", a % 4)] = ("pe", p.cnt["pe"])
        p.wait("v", tsum)
        p.waits("v", tmax(self.tk.get("lnrows_s"), self.tk.get("rows_dump")))
        p.op("v", lambda e: e.tensor_scalar(R(0), bs[0:1, 0:CHW], 1.0 / DM, None, op0=OP.mult))
        t2 = p.op("v", lambda e: e.tensor_scalar(R(1), bq[0:1, 0:CHW], 1.0 / DM, None, op0=OP.mult))
        p.wait("v", t2)
        p.op("v", lambda e: e.tensor_tensor(R(4), R(0), R(0), op=OP.mult))
        t3 = p.op("v", lambda e: e.tensor_tensor(R(1), R(4), R(1), op=OP.subtract))
        self.bank_guard(4, t3)
        self.bank_guard(6, t3)
        p.wait("s", t3)
        t4 = p.op("s", lambda e: e.activation(R(2), R(1), AF.Sqrt,
                                              bias=self.sb["c_eps5"][0:1, :], scale=-1.0))
        p.wait("v", t4)
        t5 = p.op("v", lambda e: e.reciprocal(R(2), R(2)))
        t6 = p.op("v", lambda e: e.scalar_tensor_tensor(R(3), R(0), -1.0, R(2),
                                                        op0=OP.mult, op1=OP.mult))
        p.wait("v", t5)
        p.waits("v", tmax(self.tk.get("lnrowsb_t")))
        h1 = p.op("v", lambda e: e.tensor_copy(Rb(0), R(2)))
        p.wait("v", h1)
        h2 = p.op("v", lambda e: e.tensor_tensor(Rb(1), R(2), Rb(0), op=OP.subtract))
        p.wait("v", t6)
        h3 = p.op("v", lambda e: e.tensor_copy(Rb(2), R(3)))
        p.wait("v", h3)
        h4 = p.op("v", lambda e: e.tensor_tensor(Rb(3), R(3), Rb(2), op=OP.subtract))
        orow = self.sb["onesrow"]
        self.bank_wait(5)
        p.waits("t", tmax(h2))
        p.raw("t", lambda e: e.matmul(bb[:, 0:CHW], orow[:], Rb(0), start=True, stop=False))
        p.raw("t", lambda e: e.matmul(bb[:, 0:CHW], orow[:], Rb(1), start=False, stop=True))
        self.bank_wait(4)
        p.waits("t", tmax(h4))
        p.raw("t", lambda e: e.matmul(bs[:, 0:CHW], orow[:], Rb(2), start=True, stop=False))
        tmm = p.op("t", lambda e: e.matmul(bs[:, 0:CHW], orow[:], Rb(3), start=False, stop=True))
        self.tk["lnrows_s"] = t4
        self.tk["lnrowsb_t"] = tmm
        lastc = None
        for a in range(NT):
            lastc = writer(a, tiles[a], bb[:, 0:CHW], bs[:, 0:CHW], tmm)
        self.bank_guard(4, lastc)
        self.bank_guard(5, lastc)
        return {"v": lastc, "s": sq_toks[-1], "t": tsum}

    def emit_outer_ln(self, src, dst):
        p = self.p
        stage = self.sb["stage"]
        for c in range(NCHUNK):
            par = c % 2
            lt = self.load_xs_chunk(src, c, par)
            tiles = [self.xs_slice(par, a, False) for a in range(NT)]

            def writer(a, tile, aB, bB, tmm, c=c, par=par):
                u = self.sb["scrf"][:, (a % 2) * CHW:(a % 2 + 1) * CHW]
                p.waits("v", tmax(tmm, self.tk.get(("scrfslot", a % 2)),
                                  self.tk.get(("stageslot", a % 2)), self.tk.get(("stageslot_s", a % 2))))
                p.op("v", lambda e, u=u, tile=tile, aB=aB: e.tensor_tensor(u, tile, aB, op=OP.mult))
                dstap = stage[:, (a % 2) * CHW:(a % 2 + 1) * CHW]
                tv = p.op("v", lambda e, dstap=dstap, u=u, bB=bB: e.tensor_tensor(dstap, u, bB, op=OP.add))
                self.tk[("scrfslot", a % 2)] = tv
                p.wait("y", tv)
                td = p.dma("y", "dq2",
                           self.dram[dst][a * 128:(a + 1) * 128, c * CHW:(c + 1) * CHW], dstap)
                self.final_toks.append(td)
                self.tk[("stageslot", a % 2)] = td
                return tv
            res = self.emit_ln(tiles, writer, in_toks=[lt] * NT)
            self.tk[("xsfree", par)] = [res["v"], res["s"], res["t"]]
            if self.debug:
                p.waits("y", tmax(res["v"], res["s"], res["t"]))
                tdump = p.dma("y", "dq3", self.dram["dbgrows"][c:c + 1, :], self.sb["rows"][0:1, :])
                self.final_toks.append(tdump)
                self.tk["rows_dump"] = tdump

    def emit_residual_combine(self, srcA, srcB, scaleB, dst, mamba_xn=False):
        p = self.p
        stage = self.sb["stage"]
        if mamba_xn:
            self.phase_barrier()
        for c in range(NCHUNK):
            par = c % 2
            bf_toks = []
            last_td = None
            tBs = []
            for a in range(NT):
                ap_b = self.sb["outacc"][:, a * CHW:(a + 1) * CHW]
                p.waits("y", tmax(self.tk.get(("oaslot", a))))
                tBs.append(p.dma("y", "dq3", ap_b,
                                 self.dram[srcB][a * 128:(a + 1) * 128, c * CHW:(c + 1) * CHW]))
            for a in range(NT):
                ap_a = stage[:, (a % 2) * CHW:(a % 2 + 1) * CHW]
                ap_b = self.sb["outacc"][:, a * CHW:(a + 1) * CHW]
                p.waits("y", tmax(self.tk.get(("stageslot", a % 2)), self.tk.get(("stageslot_s", a % 2))))
                tA = p.dma("y", "dq2", ap_a,
                           self.dram[srcA][a * 128:(a + 1) * 128, c * CHW:(c + 1) * CHW])
                tB = tBs[a]
                p.waits("v", tmax(tA, tB))
                tv = p.op("v", lambda e, ap_a=ap_a, ap_b=ap_b: e.scalar_tensor_tensor(
                    ap_a, ap_b, scaleB, ap_a, op0=OP.mult, op1=OP.add))
                p.wait("y", tv)
                td = p.dma("y", "dq2",
                           self.dram[dst][a * 128:(a + 1) * 128, c * CHW:(c + 1) * CHW], ap_a)
                self.final_toks.append(td)
                last_td = td
                self.tk[("stageslot", a % 2)] = td
                self.tk[("oaslot", a)] = tv
                if mamba_xn:
                    xt = self.xs_slice(par, a, False)
                    p.waits("s", tmax(tv, *(self.tk.get(("xsfree", par)) or []),
                                      self.tk.get(("xstail", par))))
                    bft = p.op("s", lambda e, xt=xt, ap_a=ap_a: e.activation(xt, ap_a, AF.Copy))
                    bf_toks.append(bft)
                    self.tk[("stageslot_s", a % 2)] = bft
            if mamba_xn:
                tiles = [self.xs_slice(par, a, False) for a in range(NT)]

                def writer(a, tile, aB, bB, tmm, c=c):
                    u = self.sb["scrf"][:, (a % 2) * CHW:(a % 2 + 1) * CHW]
                    p.waits("v", tmax(tmm, self.tk.get(("scrfslot", a % 2))))
                    p.op("v", lambda e, u=u, tile=tile, aB=aB: e.tensor_tensor(u, tile, aB, op=OP.mult))
                    dstap = self.ar(Build.MM_XN + a * SEQ + c * CHW, CHW)
                    tv = p.op("v", lambda e, dstap=dstap, u=u, bB=bB: e.tensor_tensor(dstap, u, bB, op=OP.add))
                    self.tk[("scrfslot", a % 2)] = tv
                    return tv
                res = self.emit_ln(tiles, writer, in_toks=bf_toks)
                self.tk[("xsfree", par)] = [res["v"], res["s"], res["t"]]

    def emit_final_combine(self):
        self.emit_residual_combine("xres", "xstack", 0.5, "y", mamba_xn=False)

    def phase_barrier(self):
        p = self.p
        toks = [("pe", p.cnt["pe"]), ("ve", p.cnt["ve"]), ("se", p.cnt["se"]),
                ("ge", p.cnt["ge"])] + [(f"{q}_{i}", p.cnt[f"{q}_{i}"])
                                        for q in ["dq0", "dq1", "dq2", "dq3"] for i in range(3)]
        toks = [t for t in toks if t[1] > 0]
        for e in "tvsgy":
            p.waits(e, toks)

    def emit_attn_sub(self, l):
        p = self.p
        ar = self.ar
        B = Build
        sel2 = self.sb["sel2"]
        scr = self.sb["scr"]
        attnw = self.sb["attnw"]
        rnr = self.sb["rows"]
        rnrb = self.sb["rowsb"]
        self.phase_barrier()
        vones_ap = ar(B.AT_VR, 8 * 1056).rearrange("p (s h c) -> p s h c", h=16, c=66)[:, :, :, 64:65]
        tvones = p.op("v", lambda e: e.memset(vones_ap, 1.0))
        wq_src = self.dram["wqkv"][l].rearrange("(kt p) o -> p kt o", p=128)
        twq = p.dma("g", "dq1", ar(B.AT_WQKV, 24576).rearrange("p (kt o) -> p kt o", o=3072), wq_src)
        wo_toks = []
        for kt in range(NT):
            wo_toks.append(p.dma("g", "dq1", ar(B.AT_WO + kt * 1024, 1024),
                                 self.dram["wo"][l][kt * 128:(kt + 1) * 128, :]))
        qs_col = self.sb["qs_cols"][:, l:l + 1]
        ks_col = self.sb["ks_cols"][:, l:l + 1]
        bstep = [0]

        def prep_A(c, par, pair, is_q, xs_tok):
            bk = bstep[0] % 2
            bank = self.ps[bk]
            bstep[0] += 1
            ot = (0 if is_q else 8) + pair
            self.bank_wait(bk)
            p.waits("t", tmax(twq, xs_tok))
            if is_q:
                dst = ar(B.AT_Q + pair * CHW, CHW)
                p.waits("v", tmax(self.tk.get(("qslot", pair))))
            else:
                dst = ar(B.AT_KR + (c % 2) * 4096 + pair * CHW, CHW)
                p.waits("v", tmax(self.tk.get(("kslot", c % 2, pair))))
            for kt in range(NT):
                rhs = self.xs_slice(par, kt, kt >= 4)
                lhsT = ar(B.AT_WQKV + kt * 3072 + ot * 128, 128)
                fn = lambda e, lhsT=lhsT, rhs=rhs, kt=kt: e.matmul(
                    bank[:, 0:CHW], lhsT, rhs, start=(kt == 0), stop=(kt == NT - 1))
                if kt == NT - 1:
                    tmm = p.op("t", fn)
                else:
                    p.raw("t", fn)
            sA = scr[:, (bk * 2) * CHW:(bk * 2 + 1) * CHW]
            sB = scr[:, (bk * 2 + 1) * CHW:(bk * 2 + 2) * CHW]
            p.waits("s", tmax(tmm, self.tk.get(("scrA", bk)), self.tk.get(("scrB", bk))))
            t_raw = p.op("s", lambda e: e.activation(sA, bank[:, 0:CHW], AF.Copy))
            t_sq = p.op("s", lambda e: e.activation(sB, bank[:, 0:CHW], AF.Square))
            p.waits("t", tmax(t_raw, t_sq))
            p.raw("t", lambda e: e.matmul(bank[0:1, 0:CHW], sel2[:, 0:1], sB, start=True, stop=True))
            tnsq = p.op("t", lambda e: e.matmul(bank[32:33, 0:CHW], sel2[:, 1:2], sB,
                                                start=True, stop=True, tile_position=(0, 32)))
            return (c, bk, bank, sA, sB, dst, tnsq, pair, is_q)

        def prep_B(st, tab_tok):
            c, bk, bank, sA, sB, dst, tnsq, pair, is_q = st
            RA = rnr[0:1, 0:CHW]
            RB = rnr[0:1, CHW:2 * CHW]
            p.waits("s", tmax(tnsq, self.tk.get("rnfree")))
            s1 = p.op("s", lambda e: e.activation(RA, bank[0:1, 0:CHW], AF.Sqrt,
                                                  bias=self.sb["c_eps24"][0:1, :]))
            s2 = p.op("s", lambda e: e.activation(RB, bank[32:33, 0:CHW], AF.Sqrt,
                                                  bias=self.sb["c_eps24"][0:1, :]))
            p.waits("v", tmax(s1, s2, self.tk.get("rnbfree")))
            v1 = p.op("v", lambda e: e.reciprocal(RA, RA))
            v2 = p.op("v", lambda e: e.reciprocal(RB, RB))
            Hb = lambda i: rnrb[0:1, i * CHW:(i + 1) * CHW]
            p.wait("v", v1)
            h1 = p.op("v", lambda e: e.tensor_copy(Hb(0), RA))
            p.wait("v", h1)
            h2 = p.op("v", lambda e: e.tensor_tensor(Hb(1), RA, Hb(0), op=OP.subtract))
            p.wait("v", v2)
            h3 = p.op("v", lambda e: e.tensor_copy(Hb(2), RB))
            p.wait("v", h3)
            h4 = p.op("v", lambda e: e.tensor_tensor(Hb(3), RB, Hb(2), op=OP.subtract))
            self.tk["rnfree"] = h4
            p.waits("t", tmax(h2, h4))
            selA, selB = self.sb["selA"], self.sb["selB"]
            p.raw("t", lambda e: e.matmul(bank[:, 0:CHW], selA[:], Hb(0), start=True, stop=False))
            p.raw("t", lambda e: e.matmul(bank[:, 0:CHW], selA[:], Hb(1), start=False, stop=False))
            p.raw("t", lambda e: e.matmul(bank[:, 0:CHW], selB[:], Hb(2), start=False, stop=False))
            trn = p.op("t", lambda e: e.matmul(bank[:, 0:CHW], selB[:], Hb(3), start=False, stop=True))
            self.tk["rnbfree"] = trn
            scol = qs_col if is_q else ks_col
            p.waits("v", tmax(trn))
            vq1 = p.op("v", lambda e: e.scalar_tensor_tensor(sB, sA, scol, bank[:, 0:CHW],
                                                             op0=OP.mult, op1=OP.mult))
            p.waits("t", tmax(vq1))
            tq1r = p.op("t", lambda e: e.matmul(bank[:, 0:CHW], self.sb["rot"][:], sB,
                                                start=True, stop=True))
            tb = B.AT_TAB
            cs = ar(tb + (0 if is_q else 2) * CHW, CHW)
            ss = ar(tb + (1 if is_q else 3) * CHW, CHW)
            p.waits("v", tmax(tq1r, *tab_tok) if isinstance(tab_tok, list) else tmax(tq1r, tab_tok))
            vt2 = p.op("v", lambda e: e.tensor_tensor(sA, bank[:, 0:CHW], ss, op=OP.mult))
            vt1 = p.op("v", lambda e: e.tensor_tensor(sB, sB, cs, op=OP.mult))
            p.waits("v", tmax(vt1, vt2))
            vd = p.op("v", lambda e: e.tensor_tensor(dst, sB, sA, op=OP.add))
            self.tk[("scrA", bk)] = vd
            self.tk[("scrB", bk)] = vd
            self.bank_guard(bk, vt2)
            if is_q:
                self.tk[("qprep", pair)] = vd
            else:
                self.tk[("kprep", c % 2, pair)] = vd
            return vd

        def prep_tiles(c, par, xs_tok, tab_tok):
            plist = [(pair, True) for pair in range(NPAIR)] + \
                    [(pair, False) for pair in range(NPAIR)]
            pend = None
            for pair, is_q in plist:
                stA = prep_A(c, par, pair, is_q, xs_tok)
                if pend is not None:
                    prep_B(pend, tab_tok)
                pend = stA
            prep_B(pend, tab_tok)

        def v_tiles(c, par, xs_tok):
            for tt in range(4):
                for vc in range(2):
                    v_one(c, par, xs_tok, tt, vc)

        def v_one(c, par, xs_tok, tt, vc):
                    jt = 4 * c + tt
                    slot = jt % 8
                    bk = bstep[0] % 2
                    bank = self.ps[bk]
                    bstep[0] += 1
                    self.bank_wait(bk)
                    p.waits("t", tmax(twq, xs_tok))
                    for kt in range(NT):
                        lhsT = self.xs_slice(par, kt, kt >= 4)[:, tt * 128:(tt + 1) * 128]
                        rhs = ar(B.AT_WQKV + kt * 3072 + 2048 + vc * CHW, CHW)
                        fn = lambda e, bank=bank, lhsT=lhsT, rhs=rhs, kt=kt: e.matmul(
                            bank[:, 0:CHW], lhsT, rhs, start=(kt == 0), stop=(kt == NT - 1))
                        if kt == NT - 1:
                            tmm = p.op("t", fn)
                        else:
                            p.raw("t", fn)
                    dst = ar(B.AT_VR + slot * 1056 + vc * 8 * 66, 8 * 66).rearrange(
                        "p (h c) -> p h c", c=66)[:, :, 0:64]
                    p.waits("v", tmax(tmm, tvones, self.tk.get(("vslot", slot, vc))))
                    vev = p.op("v", lambda e, dst=dst, bank=bank: e.tensor_copy(
                        dst, bank[:, 0:CHW].rearrange("p (h c) -> p h c", c=64)))
                    self.bank_guard(bk, vev)
                    self.tk[("vdata", jt, vc)] = vev

        def attention(c):
            for pair in range(NPAIR):
                attn_pair(c, pair)

        def attn_pair(c, pair):
            Jlist = [J for J in range(4 * c - 2, 4 * c + 4) if J >= 0]
            nJ = len(Jlist)
            state = {}
            tavs = {}

            def emit_sim(pr, idx):
                J = Jlist[idx]
                b = 2 + pr
                simb = self.ps[b]
                rpar = (J // 4) % 2
                r = J - 4 * c + 2
                b0 = max(0, r - 2) * 128
                self.bank_wait(b)
                p.waits("t", tmax(self.tk.get(("qprep", pair)), self.tk.get(("kprep", rpar, pair))))
                lhsT = ar(B.AT_KR + rpar * 4096 + pair * CHW + (J % 4) * 128, 128)[pr * 64: pr * 64 + 64, :]
                rhs = ar(B.AT_Q + pair * CHW, CHW)[pr * 64: pr * 64 + 64, b0:CHW]
                tsim = p.op("t", lambda e, simb=simb, lhsT=lhsT, rhs=rhs, pr=pr, b0=b0: e.matmul(
                    simb[:, b0:CHW], lhsT, rhs, start=True, stop=True, tile_position=(pr * 64, 0)))
                self.tk[("kslot", rpar, pair)] = tsim
                self.tk[("qslot", pair)] = tsim
                slot = (idx % 2) * 2 + pr
                aw = attnw[:, slot * CHW:(slot + 1) * CHW]
                p.waits("s", tmax(tsim, self.tk.get(("awfree", slot))))
                texp = p.op("s", lambda e, aw=aw, simb=simb, b0=b0: e.activation(
                    aw[:, b0:CHW], simb[:, b0:CHW], AF.Exp, bias=self.sb["c_neg30"][:, 0:1], scale=8.0))
                self.bank_guard(b, texp)
                msk = self.sb["amask"][:, (5 - r) * 128 + b0:(5 - r) * 128 + CHW]
                p.waits("v", tmax(texp))
                tmsk = p.op("v", lambda e, aw=aw, msk=msk, b0=b0: e.tensor_tensor(
                    aw[:, b0:CHW], aw[:, b0:CHW], msk, op=OP.mult))
                state[(pr, idx)] = (aw, slot, tmsk, J, b0)

            def emit_av(pr, idx):
                aw, slot, tmsk, J, b0 = state[(pr, idx)]
                first, last = idx == 0, idx == nJ - 1
                if first:
                    self.bank_wait(4 + pr)
                p.waits("t", tmax(tmsk, *[self.tk.get(("vdata", J, vc)) for vc in range(2)]))
                h = 2 * pair + pr
                outb = self.ps[4 + pr]
                lhsT = ar(B.AT_VR + (J % 8) * 1056 + h * 66, 65)
                tav = p.op("t", lambda e, outb=outb, lhsT=lhsT, aw=aw, first=first, last=last, b0=b0: e.matmul(
                    outb[0:65, b0:CHW], lhsT, aw[:, b0:CHW], start=first, stop=last))
                self.tk[("awfree", slot)] = tav
                for vc in range(2):
                    self.tk[("vslot", J % 8, vc)] = tav
                tavs[pr] = tav

            for idx in range(nJ + 1):
                for pr in range(2):
                    if idx < nJ:
                        emit_sim(pr, idx)
                for pr in range(2):
                    if idx >= 1:
                        emit_av(pr, idx - 1)
            for pr in range(2):
                normalize(c, pair, pr, tavs[pr])

        def normalize(c, pair, pr, tav):
            outb = self.ps[4 + pr]
            srecip = self.sb["srecip"]
            rf = self.sb["rows"][0:1, (2 + pr) * CHW:(3 + pr) * CHW]
            p.waits("v", tmax(tav, self.tk.get("srfree")))
            vr = p.op("v", lambda e, rf=rf, outb=outb: e.reciprocal(rf, outb[64:65, 0:CHW]))
            p.wait("v", vr)
            vh = p.op("v", lambda e, rf=rf: e.tensor_copy(srecip[0:1, 0:CHW], rf))
            p.wait("v", vh)
            vlo = p.op("v", lambda e, rf=rf: e.tensor_tensor(srecip[0:1, CHW:2 * CHW], rf,
                                                             srecip[0:1, 0:CHW], op=OP.subtract))
            self.bank_wait(6)
            p.waits("t", tmax(vh, vlo))
            p.raw("t", lambda e: e.matmul(self.ps[6][0:64, 0:CHW], self.sb["onesrow"][0:1, 0:64],
                                          srecip[0:1, 0:CHW], start=True, stop=False))
            trb = p.op("t", lambda e: e.matmul(self.ps[6][0:64, 0:CHW], self.sb["onesrow"][0:1, 0:64],
                                               srecip[0:1, CHW:2 * CHW], start=False, stop=True))
            self.tk["srfree"] = trb
            recS = self.sb["scrf"][0:64, pr * CHW:(pr + 1) * CHW]
            p.waits("v", tmax(trb, self.tk.get(("recSfree", pr))))
            vcp = p.op("v", lambda e, recS=recS: e.tensor_copy(recS, self.ps[6][0:64, 0:CHW]))
            self.bank_guard(6, vcp)
            dst = ar(B.AT_AO + pair * CHW, CHW)[pr * 64: pr * 64 + 64, :]
            p.waits("v", tmax(self.tk.get("aoutfree")))
            vout = p.op("v", lambda e, dst=dst, outb=outb, recS=recS: e.tensor_tensor(
                dst, outb[0:64, 0:CHW], recS, op=OP.mult))
            self.bank_guard(4 + pr, vout)
            self.tk[("recSfree", pr)] = vout

        def wo_chunk(c):
            tlds = []
            for oc in range(NT):
                old = self.sb["outacc"][:, oc * CHW:(oc + 1) * CHW]
                p.waits("y", tmax(self.tk.get(("oaslot", oc))))
                tlds.append(p.dma("y", "dq2", old,
                                  self.dram["xstack"][oc * 128:(oc + 1) * 128, c * CHW:(c + 1) * CHW]))
            for oc in range(NT):
                wo_one(c, oc, tlds[oc])

        def wo_one(c, oc, tld):
                stage = self.sb["stage"]
                bk = bstep[0] % 2
                bank = self.ps[bk]
                bstep[0] += 1
                self.bank_wait(bk)
                p.waits("t", tmax(wo_toks[0], wo_toks[-1]))
                p.waits("t", tmax(self.tk.get(("aout_done", c))))
                for kt in range(NT):
                    lhsT = ar(B.AT_WO + kt * 1024 + oc * 128, 128)
                    rhs = ar(B.AT_AO + kt * CHW, CHW)
                    fn = lambda e, lhsT=lhsT, rhs=rhs, kt=kt: e.matmul(
                        bank[:, 0:CHW], lhsT, rhs, start=(kt == 0), stop=(kt == NT - 1))
                    if kt == NT - 1:
                        two = p.op("t", fn)
                    else:
                        p.raw("t", fn)
                self.tk["aoutfree"] = two
                old = self.sb["outacc"][:, oc * CHW:(oc + 1) * CHW]
                st = stage[:, (oc % 2) * CHW:(oc % 2 + 1) * CHW]
                p.waits("v", tmax(two, tld, self.tk.get(("stageslot", oc % 2))))
                tv = p.op("v", lambda e: e.tensor_tensor(st, bank[:, 0:CHW], old, op=OP.add))
                self.bank_guard(bk, tv)
                self.tk[("oaslot", oc)] = tv
                p.wait("y", tv)
                td = p.dma("y", "dq2",
                           self.dram["xstack"][oc * 128:(oc + 1) * 128, c * CHW:(c + 1) * CHW], st)
                self.tk[("stageslot", oc % 2)] = td
                self.tk[("xstack_w", c)] = td
                self.final_toks.append(td)

        for c in range(NCHUNK):
            par = c % 2
            xs_tok = self.load_xs_chunk("xstack", c, par)
            tabs = []
            tb = B.AT_TAB
            p.waits("g", tmax(self.tk.get("tabfree")))
            for i, nm in enumerate(["cs_q", "ss_q", "cs_k", "ss_k"]):
                tabs.append(p.dma("g", "dq1", ar(tb + i * CHW, CHW),
                                  self.dram[nm][:, c * CHW:(c + 1) * CHW]))
            tab_tok = tabs
            prep_tiles(c, par, xs_tok, tab_tok)
            self.tk["tabfree"] = ("pe", p.cnt["pe"])
            self.tk["tabfree"] = ("ve", p.cnt["ve"])
            v_tiles(c, par, xs_tok)
            self.tk[("xsfree", par)] = [("pe", p.cnt["pe"]), ("se", p.cnt["se"])]
            attention(c)
            self.tk[("aout_done", c)] = ("ve", p.cnt["ve"])
            wo_chunk(c)

    def emit_ffn_sub(self, l, ln_only=False, upto="all"):
        p = self.p
        ar = self.ar
        B = Build
        scr = self.sb["scr"]
        stage = self.sb["stage"]
        if upto == "nothing":
            return
        if upto == "barrier2":
            self.p.waits("t", [("ve", self.p.cnt["ve"])])
            return
        if upto == "barrier3":
            self.p.waits("v", [("pe", self.p.cnt["pe"])])
            return
        if upto == "barrier4":
            self.p.waits("y", [("pe", self.p.cnt["pe"]), ("ve", self.p.cnt["ve"]), ("se", self.p.cnt["se"])])
            return
        if upto == "barrier5":
            for q in ["dq0", "dq1", "dq2", "dq3"]:
                self.p.waits("t", [(q, self.p.cnt[q])])
            return
        self.phase_barrier()
        if upto == "barrier":
            return
        xn_toks = {}
        for c in range(NCHUNK):
            par = c % 2
            lt = self.load_xs_chunk("xstack", c, par)
            if upto == "loads":
                xn_toks[c] = lt
                continue
            tiles = [self.xs_slice(par, a, a >= 4) for a in range(NT)]
            ctoks = []

            def writer(a, tile, aB, bB, tmm, c=c, ctoks=ctoks):
                u = self.sb["scrf"][:, (a % 2) * CHW:(a % 2 + 1) * CHW]
                p.waits("v", tmax(tmm, self.tk.get(("scrfslot", a % 2))))
                p.op("v", lambda e, u=u, tile=tile, aB=aB: e.tensor_tensor(u, tile, aB, op=OP.mult))
                p.op("v", lambda e, u=u, bB=bB: e.tensor_tensor(u, u, bB, op=OP.add))
                g_col = self.sb["lng_cols"][:, l * 8 + a: l * 8 + a + 1]
                b_col = self.sb["lnb_cols"][:, l * 8 + a: l * 8 + a + 1]
                dstap = ar(B.FF_XN + a * SEQ + c * CHW, CHW)
                tv = p.op("v", lambda e, dstap=dstap, u=u, g_col=g_col, b_col=b_col: e.tensor_scalar(
                    dstap, u, g_col, b_col, op0=OP.mult, op1=OP.add))
                self.tk[("scrfslot", a % 2)] = tv
                ctoks.append(tv)
                return tv
            res = self.emit_ln(tiles, writer, in_toks=[lt] * NT)
            self.tk[("xsfree", par)] = [res["v"], res["s"], res["t"]]
            xn_toks[c] = tmax(*ctoks)[0]
        if self.debug:
            p.waits("g", tmax(*[xn_toks[c] for c in range(NCHUNK)]))
            self.final_toks.append(p.dma("g", "dq1", self.dram["dbga"][:, 0:8192], ar(B.FF_XN, 8192)))
            self.final_toks.append(p.dma("g", "dq1", self.dram["dbga"][:, 8192:8704],
                                         self.xs_slice(0, 0, False)))
            self.final_toks.append(p.dma("g", "dq1", self.dram["dbga"][0:1, 8704:8704 + 2048],
                                         self.sb["rowsb"][0:1, 0:4 * CHW]))
            self.final_toks.append(p.dma("g", "dq1", self.dram["dbga"][0:1, 11000:11000 + 2048],
                                         self.sb["rows"][0:1, 0:4 * CHW]))
        if ln_only:
            return
        bstep = [0]
        for hf in range(2):
            if hf == 1:
                p.waits("g", tmax(("pe", p.cnt["pe"]), ("se", p.cnt["se"]), ("ve", p.cnt["ve"])))
            tw1a = p.dma("g", "dq1",
                         ar(B.FF_W1A, 11264).rearrange("p (ki o) -> p ki o", o=1408),
                         self.dram["w1a"][l].rearrange("(ki p) o -> p ki o", p=128)[:, :, hf * 1408:(hf + 1) * 1408])
            tw1g = p.dma("g", "dq1",
                         ar(B.FF_W1G, 11264).rearrange("p (ki o) -> p ki o", o=1408),
                         self.dram["w1g"][l].rearrange("(ki p) o -> p ki o", p=128)[:, :, hf * 1408:(hf + 1) * 1408])
            tw2 = p.dma("g", "dq1",
                        ar(B.FF_W2, 11264).rearrange("p (kk o) -> p kk o", o=1024),
                        self.dram["w2"][l][hf * 1408:(hf + 1) * 1408, :].rearrange("(kk p) o -> p kk o", p=128))
            for c in range(NCHUNK):
                h2_toks = []
                for k in range(KH):
                    h2_toks.append(self._ffn_h2(l, hf, c, k, tw1a, tw1g, xn_toks[c]))
                tlds = []
                for oc in range(NT):
                    old = self.sb["outacc"][:, oc * CHW:(oc + 1) * CHW]
                    p.waits("y", tmax(self.tk.get(("oaslot", oc)), self.tk.get(("xstack_w", c))))
                    tlds.append(p.dma("y", "dq2", old,
                                      self.dram["xstack"][oc * 128:(oc + 1) * 128, c * CHW:(c + 1) * CHW]))
                for oc in range(NT):
                    self._ffn_w2(l, hf, c, oc, tw2, h2_toks, bstep, tlds[oc])

    def _ffn_h2(self, l, hf, c, k, tw1a, tw1g, xn_tok):
        p = self.p
        ar = self.ar
        B = Build
        scr = self.sb["scr"]
        ba, bg = self.ps[(k % 2) * 2], self.ps[(k % 2) * 2 + 1]
        self.bank_wait((k % 2) * 2)
        p.waits("t", tmax(tw1a, xn_tok))
        for kt in range(NT):
            lhsT = ar(B.FF_W1A + kt * 1408 + k * 128, 128)
            rhs = ar(B.FF_XN + kt * SEQ + c * CHW, CHW)
            fn = lambda e, lhsT=lhsT, rhs=rhs, kt=kt: e.matmul(
                ba[:, 0:CHW], lhsT, rhs, start=(kt == 0), stop=(kt == NT - 1))
            if kt == NT - 1:
                tma = p.op("t", fn)
            else:
                p.raw("t", fn)
        self.bank_wait((k % 2) * 2 + 1)
        p.waits("t", tmax(tw1g))
        for kt in range(NT):
            lhsT = ar(B.FF_W1G + kt * 1408 + k * 128, 128)
            rhs = ar(B.FF_XN + kt * SEQ + c * CHW, CHW)
            fn = lambda e, lhsT=lhsT, rhs=rhs, kt=kt: e.matmul(
                bg[:, 0:CHW], lhsT, rhs, start=(kt == 0), stop=(kt == NT - 1))
            if kt == NT - 1:
                tmg = p.op("t", fn)
            else:
                p.raw("t", fn)
        sl = scr[:, (k % 2) * CHW:(k % 2 + 1) * CHW]
        b1g_col = self.sb["b1g_cols"][:, l * 22 + hf * 11 + k: l * 22 + hf * 11 + k + 1]
        p.waits("s", tmax(tmg, self.tk.get(("ffnscr", k % 2)), self.tk.get(("sqslot", k % 2))))
        tgel = p.op("s", lambda e, sl=sl, b1g_col=b1g_col: e.activation(
            sl, bg[:, 0:CHW], AF.Gelu, bias=b1g_col))
        self.bank_guard((k % 2) * 2 + 1, tgel)
        b1a_col = self.sb["b1a_cols"][:, l * 22 + hf * 11 + k: l * 22 + hf * 11 + k + 1]
        dst = ar(B.FF_H2 + k * CHW, CHW)
        p.waits("v", tmax(tma, tgel, self.tk.get(("h2slot", k))))
        th2 = p.op("v", lambda e, dst=dst, sl=sl, b1a_col=b1a_col: e.scalar_tensor_tensor(
            dst, ba[:, 0:CHW], b1a_col, sl, op0=OP.add, op1=OP.mult))
        self.bank_guard((k % 2) * 2, th2)
        self.tk[("ffnscr", k % 2)] = th2
        self.tk[("sqslot", k % 2)] = th2
        return th2

    def _ffn_w2(self, l, hf, c, oc, tw2, h2_toks, bstep, tld):
        p = self.p
        ar = self.ar
        B = Build
        stage = self.sb["stage"]
        bk = 2 + bstep[0] % 2
        bstep[0] += 1
        bank = self.ps[bk]
        self.bank_wait(bk)
        p.waits("t", tmax(tw2, *h2_toks))
        for k in range(KH):
            lhsT = ar(B.FF_W2 + k * 1024 + oc * 128, 128)
            rhs = ar(B.FF_H2 + k * CHW, CHW)
            fn = lambda e, lhsT=lhsT, rhs=rhs, k=k: e.matmul(
                bank[:, 0:CHW], lhsT, rhs, start=(k == 0), stop=(k == KH - 1))
            if k == KH - 1:
                tmm = p.op("t", fn)
            else:
                p.raw("t", fn)
        for k in range(KH):
            self.tk[("h2slot", k)] = tmm
        old = self.sb["outacc"][:, oc * CHW:(oc + 1) * CHW]
        st = stage[:, (oc % 2) * CHW:(oc % 2 + 1) * CHW]
        p.waits("v", tmax(tmm, tld, self.tk.get(("stageslot", oc % 2))))
        if hf == 0:
            tv = p.op("v", lambda e, st=st, bank=bank, old=old: e.tensor_tensor(
                st, bank[:, 0:CHW], old, op=OP.add))
        else:
            b2_col = self.sb["b2_cols"][:, l * 8 + oc: l * 8 + oc + 1]
            tv = p.op("v", lambda e, st=st, bank=bank, old=old, b2_col=b2_col: e.scalar_tensor_tensor(
                st, bank[:, 0:CHW], b2_col, old, op0=OP.add, op1=OP.add))
        self.bank_guard(bk, tv)
        self.tk[("oaslot", oc)] = tv
        p.wait("y", tv)
        td = p.dma("y", "dq2",
                   self.dram["xstack"][oc * 128:(oc + 1) * 128, c * CHW:(c + 1) * CHW], st)
        self.tk[("stageslot", oc % 2)] = td
        self.tk[("xstack_w", c)] = td
        self.final_toks.append(td)

    def emit_mamba(self, upto="all"):
        p = self.p
        ar = self.ar
        B = Build
        self.phase_barrier()
        mscr = lambda i: ar(B.MM_SCR + i * MTB, MTB)
        mscrf = self.sb["mscrf"]
        psb = self.ps[7]
        ident = self.sb["ident"]
        twdt = p.dma("g", "dq1", ar(B.MM_WDT, 256).rearrange("p (kt o) -> p kt o", o=32),
                     self.dram["winT"][:, 8192:8224].rearrange("(kt p) o -> p kt o", p=128))
        ntb = 1 if upto.endswith("1") else 8
        upto = upto[:-1] if upto.endswith("1") else upto
        for tb in range(ntb):
            self._mamba_dt(tb, twdt)
            if upto == "dt":
                continue
            for ti in range(NTI):
                self._mamba_ti(tb, ti, upto)
            self.tk["mDfree"] = ("pe", self.p.cnt["pe"])
            if upto in ("nowout", "proj", "noscan"):
                continue
            self._mamba_wout(tb)

    def _mamba_dt(self, tb, twdt):
        p = self.p
        ar = self.ar
        B = Build
        bk = self.ps[2]
        mscrf = self.sb["mscrf"]
        self.bank_wait(2)
        p.waits("t", tmax(twdt, self.tk.get("mxn")))
        for kt in range(NT):
            lhsT = ar(B.MM_WDT + kt * 32, 32)
            rhs = ar(B.MM_XN + kt * SEQ + tb * MTB, MTB)
            fn = lambda e, lhsT=lhsT, rhs=rhs, kt=kt: e.matmul(
                bk[0:32, 0:MTB], lhsT, rhs, start=(kt == 0), stop=(kt == NT - 1))
            if kt == NT - 1:
                tdt = p.op("t", fn)
            else:
                p.raw("t", fn)
        U = mscrf[0:32, 0:MTB]
        AB = self.sb["mDrev"][:, :]
        EE = self.sb["mDl"][:, :]
        p.waits("v", tmax(tdt, self.tk.get("mscrf_w"), self.tk.get("mscrf_rs"), self.tk.get("mDfree")))
        t1 = p.op("v", lambda e: e.tensor_scalar(U, bk[0:32, 0:MTB], self.sb["dtb_col"][:], None, op0=OP.add))
        self.bank_guard(2, t1)
        p.waits("s", tmax(t1, self.tk.get("mDfree")))
        t2 = p.op("s", lambda e: e.activation(AB, U, AF.Abs))
        p.waits("s", tmax(t2))
        t3 = p.op("s", lambda e: e.activation(EE, AB, AF.Exp, scale=-1.0))
        p.waits("s", tmax(t3))
        t4 = p.op("s", lambda e: e.activation(AB, EE, AF.Ln, bias=self.sb["c_one"][0:32, :]))
        p.waits("v", tmax(t4))
        t5 = p.op("v", lambda e: e.tensor_scalar(EE, U, 0.0, None, op0=OP.max))
        mdt = self.sb["mdt"]
        p.waits("v", tmax(t5))
        t6 = p.op("v", lambda e: e.tensor_tensor(mdt[:, :], EE, AB, op=OP.add))
        self.tk["mscrf_w"] = t1
        t7 = p.op("v", lambda e: e.tensor_copy(self.sb["mdtb"][:, :], mdt[:, :]))
        mDl = self.sb["mDl"]
        t8 = p.op("v", lambda e: e.tensor_tensor_scan(mDl[:, :], self.sb["scanmask"][:, :], mdt[:, :],
                                                      0.0, op0=OP.mult, op1=OP.add))
        p.waits("v", tmax(t8))
        mDtot = self.sb["mDtot"]
        for cc in range(4):
            p.op("v", lambda e, cc=cc: e.tensor_copy(mDtot[:, cc:cc + 1], mDl[:, cc * MCH + 63:cc * MCH + 64]))
        mDrev = self.sb["mDrev"]
        for cc in range(4):
            p.op("v", lambda e, cc=cc: e.tensor_scalar(
                mDrev[:, cc * MCH:(cc + 1) * MCH], mDl[:, cc * MCH:(cc + 1) * MCH],
                mDtot[:, cc:cc + 1], None, op0=OP.subtract))
        mDtotb = self.sb["mDtotb"]
        p.op("v", lambda e: e.tensor_copy(mDtotb[:, 0:4], mDtot[:, :]))
        p.op("v", lambda e: e.tensor_tensor(mDtotb[:, 4:8], mDtot[:, :], mDtotb[:, 0:4], op=OP.subtract))
        hl = self.sb["mDlhl"]
        p.op("v", lambda e: e.tensor_copy(hl[:, 0:MTB], mDl[:, :]))
        p.op("v", lambda e: e.tensor_tensor(hl[:, MTB:2 * MTB], mDl[:, :], hl[:, 0:MTB], op=OP.subtract))
        hr = self.sb["mDrevhl"]
        p.op("v", lambda e: e.tensor_copy(hr[:, 0:MTB], mDrev[:, :]))
        tlast = p.op("v", lambda e: e.tensor_tensor(hr[:, MTB:2 * MTB], mDrev[:, :], hr[:, 0:MTB], op=OP.subtract))
        self.tk["mdt_done"] = tlast

    def _mamba_ti(self, tb, ti, upto="all"):
        p = self.p
        ar = self.ar
        B = Build
        sel = self.sb["selscan"][:, ti * 128:(ti + 1) * 128]
        psb = self.ps[7]
        frees = tmax(self.tk.get("mscr_free_v"), self.tk.get("mscr_free_t"))
        for eng in "tvs":
            p.waits(eng, frees)
        pp = ti % 2
        prev = self.tk.get(("mwin", pp))
        if prev is not None:
            p.waits("g", tmax(prev))
        twin = p.dma("g", "dq1",
                     ar(B.MM_WIN + pp * 4096, 4096).rearrange("p (kt o) -> p kt o", o=512),
                     self.dram["winT"][:, ti * 512:(ti + 1) * 512].rearrange("(kt p) o -> p kt o", p=128))
        def proj(colofs, bank):
            self.bank_wait(bank)
            bk = self.ps[bank]
            p.waits("t", tmax(twin, self.tk.get("mxn")))
            for kt in range(NT):
                lhsT = ar(B.MM_WIN + pp * 4096 + kt * 512 + colofs, 128)
                rhs = ar(B.MM_XN + kt * SEQ + tb * MTB, MTB)
                fn = lambda e, bk=bk, lhsT=lhsT, rhs=rhs, kt=kt: e.matmul(
                    bk[:, 0:MTB], lhsT, rhs, start=(kt == 0), stop=(kt == NT - 1))
                if kt == NT - 1:
                    tmm = p.op("t", fn)
                else:
                    p.raw("t", fn)
            return tmm
        t_xin = proj(0, 0)
        xin = ar(B.MM_XIN + ti * (MTB + 3), MTB + 3)
        p.waits("v", tmax(t_xin, self.tk.get(("mxin", ti))))
        v1 = p.op("v", lambda e: e.tensor_copy(xin[:, 0:3], self.sb["xtails"][:, ti * 3:(ti + 1) * 3]))
        p.waits("v", tmax(v1))
        v2 = p.op("v", lambda e: e.tensor_copy(xin[:, 3:MTB + 3], self.ps[0][:, 0:MTB]))
        self.bank_guard(0, v2)
        v3 = p.op("v", lambda e: e.tensor_copy(self.sb["xtails"][:, ti * 3:(ti + 1) * 3], xin[:, MTB:MTB + 3]))
        t_B = proj(128, 1)
        p.waits("v", tmax(t_B))
        vB = p.op("v", lambda e: e.tensor_copy(ar(B.MM_SCR + 0 * MTB, MTB), self.ps[1][:, 0:MTB]))
        self.bank_guard(1, vB)
        t_C = proj(256, 0)
        p.waits("v", tmax(t_C))
        vC = p.op("v", lambda e: e.tensor_copy(ar(B.MM_SCR + 1 * MTB, MTB), self.ps[0][:, 0:MTB]))
        self.bank_guard(0, vC)
        mscrf = self.sb["mscrf"]
        acc = mscrf[:, 0:MTB]
        w0 = self.sb["convw_cols"][:, ti * 4:ti * 4 + 1]
        cb = self.sb["convb_cols"][:, ti:ti + 1]
        p.waits("v", tmax(v2, v3, self.tk.get("mscrf_w"), self.tk.get("mscrf_rs")))
        c1 = p.op("v", lambda e: e.tensor_scalar(acc, xin[:, 0:MTB], w0, cb, op0=OP.mult, op1=OP.add))
        prevc = c1
        for k in range(1, 4):
            wk = self.sb["convw_cols"][:, ti * 4 + k:ti * 4 + k + 1]
            p.waits("v", tmax(prevc))
            prevc = p.op("v", lambda e, wk=wk, k=k: e.scalar_tensor_tensor(
                acc, xin[:, k:k + MTB], wk, acc, op0=OP.mult, op1=OP.add))
        self.tk["mscrf_w"] = prevc
        p.waits("s", tmax(prevc))
        tsil = p.op("s", lambda e: e.activation(ar(B.MM_SCR + 2 * MTB, MTB), acc, AF.Silu))
        self.tk["mscrf_rs"] = tsil
        xh_cm = ar(B.MM_SCR + 2 * MTB, MTB)
        ev_toks = []
        for half in range(2):
            for c4 in range(4):
                self.bank_wait(7)
                p.waits("t", tmax(tsil))
                ttr = p.op("t", lambda e, half=half, c4=c4: e.transpose(
                    psb[half * 64:half * 64 + 64, c4 * 128:(c4 + 1) * 128],
                    xh_cm[:, c4 * 64:(c4 + 1) * 64], self.sb["ident"][:],
                    tile_position=(0, half * 64)))
            dstt = ar(B.MM_XHTM + ti * 512, 512)[half * 64:half * 64 + 64, :]
            p.waits("v", tmax(ttr, self.tk.get(("xhtm", ti, half))))
            tev = p.op("v", lambda e, dstt=dstt, half=half: e.tensor_copy(
                dstt, psb[half * 64:half * 64 + 64, 0:512]))
            self.bank_guard(7, tev)
            ev_toks.append(tev)
            self.tk[("xhtm_t", ti, half)] = tev
        if upto == "xht":
            self.tk[("yg", tb, ti)] = tsil
            self.tk["mscr_free_v"] = ev_toks[-1]
            self.tk["mscr_free_t"] = ("pe", p.cnt["pe"])
            self.tk[("mxin", ti)] = prevc
            self.tk[("xhtm", ti, 0)] = ("pe", p.cnt["pe"])
            self.tk[("xhtm", ti, 1)] = ("pe", p.cnt["pe"])
            return
        if upto == "proj":
            self.tk[("yg", tb, ti)] = tsil
            return
        acol = self.sb["A_cols"][:, ti:ti + 1]
        nacol = self.sb["negA_cols"][:, ti:ti + 1]
        hl, hr = self.sb["mDlhl"], self.sb["mDrevhl"]
        self.bank_wait(2)
        p.waits("t", tmax(self.tk.get("mdt_done")))
        p.raw("t", lambda e: e.matmul(self.ps[2][:, 0:MTB], sel, hl[:, 0:MTB], start=True, stop=False))
        tDl = p.op("t", lambda e: e.matmul(self.ps[2][:, 0:MTB], sel, hl[:, MTB:2 * MTB], start=False, stop=True))
        p.waits("s", tmax(tDl))
        for c4 in range(4):
            p.op("s", lambda e, c4=c4: e.activation(
                self.sb["mPfull"][:, ti * 4 + c4: ti * 4 + c4 + 1],
                self.ps[2][:, c4 * MCH + 63:c4 * MCH + 64], AF.Exp, scale=acol))
        tE1 = p.op("s", lambda e: e.activation(ar(B.MM_SCR + 3 * MTB, MTB),
                                               self.ps[2][:, 0:MTB], AF.Exp, scale=acol))
        p.waits("v", tmax(tDl, self.tk.get("mscrf_w"), self.tk.get("mscrf_rs")))
        tm2 = p.op("v", lambda e: e.tensor_scalar(mscrf[:, 0:MTB], self.ps[2][:, 0:MTB],
                                                  self.sb["clamp_cols"][:, ti:ti + 1], None, op0=OP.min))
        self.bank_guard(2, tm2, tE1)
        self.tk["mscrf_w"] = tm2
        p.waits("s", tmax(tm2))
        tE2 = p.op("s", lambda e: e.activation(ar(B.MM_SCR + 4 * MTB, MTB), mscrf[:, 0:MTB],
                                               AF.Exp, scale=nacol))
        self.tk["mscrf_rs"] = tE2
        self.bank_wait(3)
        p.raw("t", lambda e: e.matmul(self.ps[3][:, 0:MTB], sel, hr[:, 0:MTB], start=True, stop=False))
        tDr = p.op("t", lambda e: e.matmul(self.ps[3][:, 0:MTB], sel, hr[:, MTB:2 * MTB], start=False, stop=True))
        p.waits("s", tmax(tDr))
        tEe = p.op("s", lambda e: e.activation(ar(B.MM_SCR + 5 * MTB, MTB), self.ps[3][:, 0:MTB],
                                               AF.Exp, scale=nacol))
        self.bank_guard(3, tEe)
        if upto == "decay":
            self.tk[("yg", tb, ti)] = tEe
            self.tk["mscr_free_v"] = tm2
            self.tk["mscr_free_t"] = ("pe", p.cnt["pe"])
            self.tk[("mxin", ti)] = prevc
            self.tk[("xhtm", ti, 0)] = ("pe", p.cnt["pe"])
            self.tk[("xhtm", ti, 1)] = ("pe", p.cnt["pe"])
            return
        self.bank_wait(2)
        p.waits("t", tmax(self.tk.get("mdt_done")))
        tdtB = p.op("t", lambda e: e.matmul(self.ps[2][:, 0:MTB], sel, self.sb["mdtb"][:, :],
                                            start=True, stop=True))
        p.waits("s", tmax(tdtB))
        tE1sm = p.op("s", lambda e: e.activation(ar(B.MM_E1, MTB), self.ps[2][:, 0:MTB],
                                                 AF.Exp, scale=acol))
        p.waits("v", tmax(tdtB, vB))
        vdB = p.op("v", lambda e: e.tensor_tensor(ar(B.MM_SCR + 0 * MTB, MTB), ar(B.MM_SCR + 0 * MTB, MTB),
                                                  self.ps[2][:, 0:MTB], op=OP.mult))
        self.bank_guard(2, vdB, tE1sm)
        p.waits("v", tmax(vdB))
        vP = []
        for k in range(NBAND):
            vP.append(p.op("v", lambda e, k=k: e.tensor_tensor(
                ar(B.MM_P0 + k * MTB, MTB), ar(B.MM_SCR + 1 * MTB, MTB),
                ar(B.MM_SCR + 0 * MTB - k, MTB), op=OP.mult)))
        p.waits("v", tmax(tE1sm))
        p.op("v", lambda e: e.tensor_tensor(ar(B.MM_E1, MTB), ar(B.MM_E1, MTB),
                                            self.sb["bmask"][:], op=OP.mult))
        p.op("v", lambda e: e.tensor_tensor(ar(B.MM_E2, MTB), ar(B.MM_E1, MTB),
                                            ar(B.MM_E1 - 1, MTB), op=OP.mult))
        p.op("v", lambda e: e.tensor_tensor(ar(B.MM_E3, MTB), ar(B.MM_E1, MTB),
                                            ar(B.MM_E2 - 1, MTB), op=OP.mult))
        vPE = None
        for k in range(1, NBAND):
            vPE = p.op("v", lambda e, k=k: e.tensor_tensor(
                ar(B.MM_P0 + k * MTB, MTB), ar(B.MM_P0 + k * MTB, MTB),
                ar(B.MM_E1 + (k - 1) * MTB, MTB), op=OP.mult))
        p.waits("v", tmax(tE1, vC))
        vCt = p.op("v", lambda e: e.tensor_tensor(ar(B.MM_SCR + 1 * MTB, MTB), ar(B.MM_SCR + 1 * MTB, MTB),
                                                  ar(B.MM_SCR + 3 * MTB, MTB), op=OP.mult))
        p.waits("v", tmax(tE2, self.tk.get("mscr2_free")))
        vBt = p.op("v", lambda e: e.tensor_tensor(ar(B.MM_SCR2, MTB), ar(B.MM_SCR + 0 * MTB, MTB),
                                                  ar(B.MM_SCR + 4 * MTB, MTB), op=OP.mult))
        p.waits("v", tmax(tEe))
        vBe = p.op("v", lambda e: e.tensor_tensor(ar(B.MM_SCR + 0 * MTB, MTB), ar(B.MM_SCR + 0 * MTB, MTB),
                                                  ar(B.MM_SCR + 5 * MTB, MTB), op=OP.mult))
        dBe = ar(B.MM_SCR + 0 * MTB, MTB)
        for half in range(2):
            for c4 in range(4):
                self.bank_wait(7)
                p.waits("t", tmax(vBe))
                ttr = p.op("t", lambda e, half=half, c4=c4: e.transpose(
                    psb[half * 64:half * 64 + 64, 512 + c4 * 128:512 + (c4 + 1) * 128],
                    dBe[:, c4 * 64:(c4 + 1) * 64], self.sb["ident"][:],
                    tile_position=(0, half * 64)))
            dstt = ar(B.MM_SCR2 + MTB, 512)[half * 64:half * 64 + 64, :]
            p.waits("v", tmax(ttr))
            tev = p.op("v", lambda e, dstt=dstt, half=half: e.tensor_copy(
                dstt, psb[half * 64:half * 64 + 64, 512:1024]))
            self.bank_guard(7, tev)
            ev_toks.append(tev)
        dBT_tok = tev
        self.bank_wait(4)
        self.bank_wait(5)
        p.waits("t", tmax(vBt, vCt))
        for c4 in range(4):
            p.raw("t", lambda e, c4=c4: e.matmul(
                self.ps[4][0:64, c4 * 64:(c4 + 1) * 64],
                ar(B.MM_SCR2, MTB)[0:64, c4 * 64:(c4 + 1) * 64],
                ar(B.MM_SCR + 1 * MTB, MTB)[0:64, c4 * 64:(c4 + 1) * 64],
                start=True, stop=True, tile_position=(0, 0)))
            fn = lambda e, c4=c4: e.matmul(
                self.ps[5][64:128, c4 * 64:(c4 + 1) * 64],
                ar(B.MM_SCR2, MTB)[64:128, c4 * 64:(c4 + 1) * 64],
                ar(B.MM_SCR + 1 * MTB, MTB)[64:128, c4 * 64:(c4 + 1) * 64],
                start=True, stop=True, tile_position=(64, 64))
            if c4 == 3:
                tG = p.op("t", fn)
            else:
                p.raw("t", fn)
        mG = ar(B.MM_G, 512)
        p.waits("v", tmax(tG, self.tk.get("mG_free")))
        vGA = p.op("v", lambda e: e.tensor_tensor(mG[0:64, 0:256], self.ps[4][0:64, 0:256],
                                                  self.sb["LT4"][0:64, :], op=OP.mult))
        vGB = p.op("v", lambda e: e.tensor_tensor(mG[64:128, 256:512], self.ps[5][64:128, 0:256],
                                                  self.sb["LT4"][64:128, :], op=OP.mult))
        self.bank_guard(4, vGA)
        self.bank_guard(5, vGB)
        if upto == "noscan":
            self.tk[("yg", tb, ti)] = vGB
            self.tk["mscr_free_v"] = vGB
            self.tk["mscr_free_t"] = ("pe", p.cnt["pe"])
            self.tk["mscr2_free"] = ("pe", p.cnt["pe"])
            self.tk["mG_free"] = vGB
            self.tk[("mxin", ti)] = prevc
            self.tk[("xhtm", ti, 0)] = ("pe", p.cnt["pe"])
            self.tk[("xhtm", ti, 1)] = ("pe", p.cnt["pe"])
            return
        NGBK = NBAND
        gb_ps = [(3, 0), (3, 256), (1, 256), (6, 256)]
        p.waits("t", tmax(tEe, vPE, vP[0],
                          self.tk.get("mgb1"), self.tk.get("mgb2"), self.tk.get("mgb3")))
        tgb = None
        for k in range(NGBK):
            gbank, gco = gb_ps[k]
            fngb = lambda e, k=k, gbank=gbank, gco=gco: e.matmul(
                self.ps[gbank][:, gco:gco + MTB], self.sb["dup64blk"][:],
                ar(B.MM_P0 + k * MTB, MTB), start=True, stop=True)
            tgb = p.op("t", fngb)
        h_ti = self.sb["mh"][:, ti * 64:(ti + 1) * 64]
        hb_ti = self.sb["mhb"][:, ti * 64:(ti + 1) * 64]
        xt = ar(B.MM_XHTM + ti * 512, 512)
        dBT = ar(B.MM_SCR2 + MTB, 512)
        for c4 in range(4):
            p.waits("t", tmax(vCt, vGA, vGB, self.tk.get(("mh_w", ti)),
                              *[self.tk.get(("xhtm_t", ti, hh2)) for hh2 in range(2)]))
            p.raw("t", lambda e, c4=c4: e.matmul(
                self.ps[4][0:64, 256 + c4 * 64:256 + (c4 + 1) * 64],
                hb_ti[0:64, :], ar(B.MM_SCR + 1 * MTB, MTB)[0:64, c4 * 64:(c4 + 1) * 64],
                start=True, stop=False, tile_position=(0, 0)))
            p.raw("t", lambda e, c4=c4: e.matmul(
                self.ps[4][0:64, 256 + c4 * 64:256 + (c4 + 1) * 64],
                xt[0:64, c4 * 128:c4 * 128 + 64], mG[0:64, c4 * 64:(c4 + 1) * 64],
                start=False, stop=True, tile_position=(0, 0)))
            p.raw("t", lambda e, c4=c4: e.matmul(
                self.ps[5][64:128, 256 + c4 * 64:256 + (c4 + 1) * 64],
                hb_ti[64:128, :], ar(B.MM_SCR + 1 * MTB, MTB)[64:128, c4 * 64:(c4 + 1) * 64],
                start=True, stop=False, tile_position=(64, 64)))
            p.raw("t", lambda e, c4=c4: e.matmul(
                self.ps[5][64:128, 256 + c4 * 64:256 + (c4 + 1) * 64],
                xt[64:128, c4 * 128 + 64:c4 * 128 + 128], mG[64:128, 256 + c4 * 64:256 + (c4 + 1) * 64],
                start=False, stop=True, tile_position=(64, 64)))
            self.bank_wait(6)
            p.waits("t", tmax(dBT_tok))
            tSA = p.op("t", lambda e, c4=c4: e.matmul(
                self.ps[6][0:64, 0:64],
                dBT[0:64, c4 * 128:c4 * 128 + 64], xt[0:64, c4 * 128:c4 * 128 + 64],
                start=True, stop=True, tile_position=(0, 0)))
            self.bank_wait(2)
            tSB = p.op("t", lambda e, c4=c4: e.matmul(
                self.ps[2][64:128, 256:320],
                dBT[64:128, c4 * 128 + 64:c4 * 128 + 128], xt[64:128, c4 * 128 + 64:c4 * 128 + 128],
                start=True, stop=True, tile_position=(64, 64)))
            pf = self.sb["mPfull"][:, ti * 4 + c4: ti * 4 + c4 + 1]
            p.waits("v", tmax(tSA, tSB))
            thA = p.op("v", lambda e, pf=pf: e.scalar_tensor_tensor(
                h_ti[0:64, :], h_ti[0:64, :], pf[0:64, :], self.ps[6][0:64, 0:64],
                op0=OP.mult, op1=OP.add))
            thB = p.op("v", lambda e, pf=pf: e.scalar_tensor_tensor(
                h_ti[64:128, :], h_ti[64:128, :], pf[64:128, :], self.ps[2][64:128, 256:320],
                op0=OP.mult, op1=OP.add))
            self.bank_guard(6, thA)
            self.bank_guard(2, thB)
            p.waits("v", tmax(thA, thB))
            thb = p.op("v", lambda e: e.tensor_copy(hb_ti, h_ti))
            self.tk[("mh_w", ti)] = thb
        tyT = ("pe", p.cnt["pe"])
        if self.debug and tb == 0 and ti == 0:
            p.waits("v", tmax(tyT))
            p.op("v", lambda e: e.tensor_copy(ar(50000, 256)[0:64, :], self.ps[4][0:64, 256:512]))
            p.op("v", lambda e: e.tensor_copy(ar(50000, 256)[64:128, :], self.ps[5][64:128, 256:512]))
            p.op("v", lambda e: e.tensor_copy(ar(50256, 64), self.sb["mhb"][:, 0:64]))
            p.op("v", lambda e: e.tensor_copy(ar(50320, 256), ar(B.MM_SCR + 1 * MTB, MTB)))
        t_z = None
        self.bank_wait(0)
        p.waits("t", tmax(twin))
        for kt in range(NT):
            lhsT = ar(B.MM_WIN + pp * 4096 + kt * 512 + 384, 128)
            rhs = ar(B.MM_XN + kt * SEQ + tb * MTB, MTB)
            fn = lambda e, lhsT=lhsT, rhs=rhs, kt=kt: e.matmul(
                self.ps[0][:, 0:MTB], lhsT, rhs, start=(kt == 0), stop=(kt == NT - 1))
            if kt == NT - 1:
                t_z = p.op("t", fn)
            else:
                p.raw("t", fn)
        p.waits("s", tmax(t_z, vCt))
        tsz = p.op("s", lambda e: e.activation(ar(B.MM_SCR + 3 * MTB, MTB), self.ps[0][:, 0:MTB], AF.Silu))
        self.bank_guard(0, tsz)
        self.tk[("mwin", pp)] = t_z
        dcol = self.sb["d_cols"][:, ti:ti + 1]
        s4 = ar(B.MM_SCR + 4 * MTB, MTB)
        s5 = ar(B.MM_SCR + 5 * MTB, MTB)
        xh0 = B.MM_SCR + 2 * MTB
        p.waits("v", tmax(tsil, vBt, vBe, tgb, t_z))
        if NGBK > 0:
            vdx = p.op("v", lambda e: e.scalar_tensor_tensor(
                s4, self.ps[3][:, 0:MTB], dcol, ar(xh0, MTB), op0=OP.add, op1=OP.mult))
        else:
            vdx = p.op("v", lambda e: e.tensor_scalar(
                s4, ar(xh0, MTB), dcol, None, op0=OP.mult))
        mtoks = []
        for k in range(1, NGBK):
            gbank, gco = gb_ps[k]
            mk = p.op("v", lambda e, gbank=gbank, gco=gco, k=k: e.tensor_tensor(
                s5, self.ps[gbank][:, gco:gco + MTB], ar(xh0 - k, MTB), op=OP.mult))
            mtoks.append(mk)
            vdx = p.op("v", lambda e: e.tensor_tensor(s4, s4, s5, op=OP.add))
        if NGBK > 0:
            self.bank_guard(3, vdx if not mtoks else mtoks[0])
        if len(mtoks) > 0:
            self.tk["mgb1"] = mtoks[0]
        if len(mtoks) > 1:
            self.tk["mgb2"] = mtoks[1]
            self.bank_guard(1, mtoks[1])
        if len(mtoks) > 2:
            self.tk["mgb3"] = mtoks[2]
            self.bank_guard(6, mtoks[2])
        p.waits("v", tmax(tyT, vdx, self.tk.get("mscrf_w"), self.tk.get("mscrf_rs")))
        vy2a = p.op("v", lambda e: e.tensor_tensor(
            self.sb["mscrf"][0:64, 0:MTB], self.ps[4][0:64, 256:512],
            ar(B.MM_SCR + 4 * MTB, MTB)[0:64, :], op=OP.add))
        vy2 = p.op("v", lambda e: e.tensor_tensor(
            self.sb["mscrf"][64:128, 0:MTB], self.ps[5][64:128, 256:512],
            ar(B.MM_SCR + 4 * MTB, MTB)[64:128, :], op=OP.add))
        self.bank_guard(4, vy2a)
        self.bank_guard(5, vy2)
        self.tk["mscrf_w"] = vy2
        yg = ar(B.MM_YG + ti * MTB, MTB)
        p.waits("v", tmax(tsz, self.tk.get(("myg", ti))))
        vyg = p.op("v", lambda e, yg=yg: e.tensor_tensor(yg, self.sb["mscrf"][:, 0:MTB],
                                                         ar(B.MM_SCR + 3 * MTB, MTB), op=OP.mult))
        self.tk[("yg", tb, ti)] = vyg
        self.tk["mscr_free_v"] = vyg
        self.tk["mscr_free_t"] = tyT
        self.tk["mscr2_free"] = tyT
        self.tk["mG_free"] = tyT
        self.tk[("mxin", ti)] = prevc
        self.tk[("xhtm", ti, 0)] = tyT
        self.tk[("xhtm", ti, 1)] = tyT

    def _mamba_wout(self, tb):
        p = self.p
        ar = self.ar
        B = Build
        stage = self.sb["stage"]
        for oc in range(NT):
            pp = oc % 2
            prev = self.tk.get(("mwout", pp))
            if prev is not None:
                p.waits("g", tmax(prev))
            two = p.dma("g", "dq1",
                        ar(B.MM_WOUT + pp * 2048, 2048).rearrange("p (kt o) -> p kt o", o=128),
                        self.dram["woutT"][:, oc * 128:(oc + 1) * 128].rearrange("(kt p) o -> p kt o", p=128))
            bk = self.ps[1]
            self.bank_wait(1)
            p.waits("t", tmax(two, *[self.tk.get(("yg", tb, ti2)) for ti2 in range(NTI)]))
            for kt in range(16):
                lhsT = ar(B.MM_WOUT + pp * 2048 + kt * 128, 128)
                rhs = ar(B.MM_YG + kt * MTB, MTB)
                fn = lambda e, lhsT=lhsT, rhs=rhs, kt=kt: e.matmul(
                    bk[:, 0:MTB], lhsT, rhs, start=(kt == 0), stop=(kt == 15))
                if kt == 15:
                    tmm = p.op("t", fn)
                else:
                    p.raw("t", fn)
            self.tk[("mwout", pp)] = tmm
            if oc == NT - 1:
                for ti2 in range(NTI):
                    self.tk[("myg", ti2)] = tmm
            st = stage[:, (oc % 2) * CHW:(oc % 2) * CHW + MTB]
            p.waits("y", tmax(self.tk.get(("stageslot", oc % 2)), self.tk.get(("stageslot_s", oc % 2))))
            tld = p.dma("y", "dq2", st,
                        self.dram["xres"][oc * 128:(oc + 1) * 128, tb * MTB:(tb + 1) * MTB])
            p.waits("v", tmax(tmm, tld))
            tv = p.op("v", lambda e, st=st: e.tensor_tensor(st, bk[:, 0:MTB], st, op=OP.add))
            self.bank_guard(1, tv)
            p.wait("y", tv)
            td = p.dma("y", "dq2",
                       self.dram["xres"][oc * 128:(oc + 1) * 128, tb * MTB:(tb + 1) * MTB], st)
            self.tk[("stageslot", oc % 2)] = td
            self.final_toks.append(td)

    def finish(self):
        p = self.p
        for q in ["dq0", "dq1", "dq2", "dq3"]:
            for i in range(3):
                s = f"{q}_{i}"
                p.ops["y"].append(lambda e, sem=p.semh[s], val=p.cnt[s]: e.wait_ge(sem, val))


def build_and_run(maps, core_ids, stages=("full",), debug=False, trace=False):
    nc = bass.Bass("TRN2", target_bir_lowering=False)
    ctx = ExitStack()
    with ctx:
        b = Build(nc, ctx, stages, debug)
        b.emit_consts()
        st = set(stages)
        need_ln0 = st & {"full", "ln0", "attn0", "ffn0", "ffnln", "stack1", "mamba"}
        if need_ln0:
            b.emit_outer_ln("x0", "xstack")
        if "attn0" in st:
            b.emit_attn_sub(0)
        if "ffn0" in st:
            b.emit_attn_sub(0)
            b.emit_ffn_sub(0)
        if "ffnln" in st:
            b.emit_attn_sub(0)
            b.emit_ffn_sub(0, ln_only=True)
        if "stack1" in st:
            for l in range(DEPTH):
                b.emit_attn_sub(l)
                b.emit_ffn_sub(l)
        if "mamba" in st:
            for l in range(DEPTH):
                b.emit_attn_sub(l)
                b.emit_ffn_sub(l)
            b.emit_residual_combine("x0", "xstack", 0.5, "xres", mamba_xn=True)
            b.tk["mxn"] = ("ve", b.p.cnt["ve"])
            b.emit_mamba()
        for v in ["monly", "monly_dt", "monly_proj", "monly_xht", "monly_decay", "monly_noscan", "monly_nowout",
                  "monly_decay1", "monly_all1"]:
            if v in st:
                b.emit_residual_combine("x0", "xstack", 0.5, "xres", mamba_xn=True)
                b.tk["mxn"] = ("ve", b.p.cnt["ve"])
                b.emit_mamba(upto=v[6:] if len(v) > 5 else "all")
                if b.debug:
                    pp = b.p
                    frees = tmax(("ve", pp.cnt["ve"]), ("se", pp.cnt["se"]), ("pe", pp.cnt["pe"]))
                    pp.waits("g", frees)
                    dd = b.dram["dbga"]
                    b.final_toks.append(pp.dma("g", "dq1", dd[:, 0:1536], b.ar(Build.MM_SCR, 1536)))
                    b.final_toks.append(pp.dma("g", "dq1", dd[:, 1536:2048], b.ar(Build.MM_G, 512)))
                    b.final_toks.append(pp.dma("g", "dq1", dd[0:32, 2048:2304], b.sb["mdt"][:, :]))
                    b.final_toks.append(pp.dma("g", "dq1", dd[0:32, 2304:2560], b.sb["mDl"][:, :]))
                    b.final_toks.append(pp.dma("g", "dq1", dd[:, 2560:2624], b.sb["mPfull"][:, :]))
                    b.final_toks.append(pp.dma("g", "dq1", dd[:, 2624:2880], b.ar(Build.MM_YG, 256)))
                    b.final_toks.append(pp.dma("g", "dq1", dd[:, 2880:2944], b.sb["mh"][:, 0:64]))
                    b.final_toks.append(pp.dma("g", "dq1", dd[:, 2944:3456], b.ar(Build.MM_SCR2 + 256, 512)))
                    b.final_toks.append(pp.dma("g", "dq1", dd[:, 3456:3968], b.ar(Build.MM_XHTM, 512)))
                    b.final_toks.append(pp.dma("g", "dq1", dd[:, 3968:4224], b.ar(50000, 256)))
                    b.final_toks.append(pp.dma("g", "dq1", dd[:, 4224:4288], b.ar(50256, 64)))
                    b.final_toks.append(pp.dma("g", "dq1", dd[:, 4288:4544], b.ar(50320, 256)))
                    b.final_toks.append(pp.dma("g", "dq1", dd[:, 4544:4800], b.ar(Build.MM_SCR2, 256)))
        if "full" in st:
            for l in range(DEPTH):
                b.emit_attn_sub(l)
                b.emit_ffn_sub(l)
            b.emit_residual_combine("x0", "xstack", 0.5, "xres", mamba_xn=True)
            b.tk["mxn"] = ("ve", b.p.cnt["ve"])
            b.emit_mamba()
            b.phase_barrier()
            b.emit_outer_ln("xres", "xstack")
            for l in range(DEPTH):
                b.emit_attn_sub(l)
                b.emit_ffn_sub(l)
        if "full" in st or "fin" in st:
            b.phase_barrier()
            b.emit_final_combine()
        b.finish()
        with nc.Block() as block:
            b.p.run(block)
        if maps is None:
            ctx.pop_all()
            return nc
        res = run_bass_kernel_spmd(nc, maps, core_ids=core_ids, trace=trace)
    return res


def _in_maps(inputs, n_cores=8):
    consts = _host_consts()
    w = _host_weights(inputs)
    x = inputs["L0_attn_x"]
    maps = []
    for i in range(n_cores):
        m = {}
        m["x0"] = np.ascontiguousarray(x[i].T).astype(np.float32)
        for k, v in w.items():
            m[k] = v
        for k, v in consts.items():
            kk = "LTm" if k == "LT" else k
            m[kk] = v
        maps.append(m)
    return maps


def kernel(**inputs):
    maps = _in_maps(inputs)
    res = build_and_run(maps, list(range(8)), stages=("full",), debug=False)
    out = np.stack([np.ascontiguousarray(np.asarray(r["y"], np.float32).T)
                    for r in res.results], axis=0)
    return out
